# version 02
# baseline (speedup 1.0000x reference)
from contextlib import ExitStack

import numpy as np

import concourse.bass as bass
import concourse.mybir as mybir
from concourse.bass_utils import run_bass_kernel_spmd

F32 = mybir.dt.float32
BF16 = mybir.dt.bfloat16
I32 = mybir.dt.int32
ALU = mybir.AluOpType
AF = mybir.ActivationFunctionType
AX = mybir.AxisListType

B, T, D = 4, 4096, 2048
F = 5632
F2 = 2 * F
TPC = 2048
NKD = D // 128
ESL = 512
NES = D // ESL
TC = 512
NTCA = T // TC
NPRE = 4
TB = 512
NBC = TPC // TB
NTS = TB // 128
NOSL = D // 512
NJP = F // 512
NKF = F // 128
EPS = 1e-5
NG = F2 * D
ND = F * D

ENGINES = ("sync", "scalar", "vector", "tensor", "gpsimd")
DEBUG = False


class Sem:
    def __init__(self, handle):
        self.h = handle
        self.n = 0

    def inc(self, k):
        self.n += k
        return self.n


class Prog:
    def __init__(self):
        self.q = {e: [] for e in ENGINES}

    def on(self, eng, fn):
        self.q[eng].append(fn)

    def dma(self, eng, out, in_, sem, transpose=False):
        def fn(e, out=out, in_=in_, sem=sem, tr=transpose):
            if tr:
                e.dma_start_transpose(out, in_).then_inc(sem.h, 16)
            else:
                e.dma_start(out=out, in_=in_).then_inc(sem.h, 16)
        self.on(eng, fn)
        return sem.inc(16)

    def wait(self, eng, sem, val=None):
        v = sem.n if val is None else val
        if v <= 0:
            return
        self.on(eng, lambda e, h=sem.h, v=v: e.wait_ge(h, v))

    def vchain(self, sem, ops):
        self.wait("vector", sem, sem.n)
        vn = None
        for f in ops:
            vn = self.op("vector", f, sem)
            self.wait("vector", sem, vn)
        return vn

    def op(self, eng, fn, sem=None):
        if sem is None:
            self.on(eng, fn)
            return None
        def wrapped(e, fn=fn, sem=sem):
            fn(e).then_inc(sem.h, 1)
        self.on(eng, wrapped)
        return sem.inc(1)


def build_nc():
    nc = bass.Bass("TRN2", target_bir_lowering=False, debug=False,
                   detect_race_conditions=False)

    xT = nc.dram_tensor("xT", [D, T], F32, kind="ExternalInput").ap()
    xnat = nc.dram_tensor("xnat", [TPC, D], F32, kind="ExternalInput").ap()
    w_i = nc.dram_tensor("w_i", [D, D], F32, kind="ExternalInput").ap()
    w_f = nc.dram_tensor("w_f", [D, D], F32, kind="ExternalInput").ap()
    w_g = nc.dram_tensor("w_g", [D, D], F32, kind="ExternalInput").ap()
    w_oT = nc.dram_tensor("w_oT", [D, D], F32, kind="ExternalInput").ap()
    w_guT = nc.dram_tensor("w_guT", [D, F2], F32, kind="ExternalInput").ap()
    w_dnT = nc.dram_tensor("w_dnT", [F, D], F32, kind="ExternalInput").ap()
    sv_in = {}
    for nm in ("sv_i", "sv_f", "sv_g", "sv_gn", "sv_o", "sv_mk"):
        sv_in[nm] = nc.dram_tensor(nm, [16, 128], F32, kind="ExternalInput").ap()
    nwb_in = nc.dram_tensor("nwb", [128, D], F32, kind="ExternalInput").ap()
    eye_in = nc.dram_tensor("eye", [128, 128], F32, kind="ExternalInput").ap()
    out = nc.dram_tensor("out", [TPC, D], F32, kind="ExternalOutput").ap()

    KD = "ExternalOutput" if DEBUG else "Internal"
    x256s = nc.dram_tensor("x256s", [D, T], BF16, kind=KD).ap()
    oacts = nc.dram_tensor("oacts", [D, TPC], BF16, kind=KD).ap()
    woTb = nc.dram_tensor("woTb", [D, D], BF16, kind="Internal").ap()
    wqgu = nc.dram_tensor("wqgu", [D, F2], BF16, kind=KD).ap()
    wqdn = nc.dram_tensor("wqdn", [F, D], BF16, kind=KD).ap()
    xqs = nc.dram_tensor("xqs", [TB, D], BF16, kind=KD).ap()
    hids = nc.dram_tensor("hids", [TB, F], F32, kind=KD).ap()
    hqs = nc.dram_tensor("hqs", [TB, F], BF16, kind=KD).ap()
    if DEBUG:
        dbg_h2 = nc.dram_tensor("dbg_h2", [1, TPC], F32, kind="ExternalOutput").ap()
        dbg_cols = nc.dram_tensor("dbg_cols", [128, 96], F32, kind="ExternalOutput").ap()
        dbg_vals = nc.dram_tensor("dbg_vals", [1, 16], F32, kind="ExternalOutput").ap()
        dbg_carry = nc.dram_tensor("dbg_carry", [128, 16], F32, kind="ExternalOutput").ap()
        dbg_svc = nc.dram_tensor("dbg_svc", [128, 128], F32, kind="ExternalOutput").ap()
        dbg_y0 = nc.dram_tensor("dbg_y0", [128, 8192], BF16, kind="ExternalOutput").ap()
        dbg_y1 = nc.dram_tensor("dbg_y1", [128, 8192], BF16, kind="ExternalOutput").ap()
        dbg_y2p = nc.dram_tensor("dbg_y2p", [128, 8192], F32, kind="ExternalOutput").ap()
        dbg_y1l = nc.dram_tensor("dbg_y1l", [128, 8192], BF16, kind="ExternalOutput").ap()
        dbg_hqt = nc.dram_tensor("dbg_hqt", [128, 22528], BF16, kind="ExternalOutput").ap()

    prog = Prog()

    with ExitStack() as es_:
        ctx = es_.enter_context
        AB = ctx(nc.sbuf_tensor("AB", [128, 47104], BF16))
        AFr = ctx(nc.sbuf_tensor("AFr", [128, 18432], F32))
        AI = ctx(nc.sbuf_tensor("AI", [128, 4096], I32))
        svc = ctx(nc.sbuf_tensor("svc", [128, 128], F32))
        carry = ctx(nc.sbuf_tensor("carry", [128, 16], F32))
        nwbb = ctx(nc.sbuf_tensor("nwbb", [128, D], F32))
        eyef = ctx(nc.sbuf_tensor("eyef", [128, 128], F32))
        ones_col = ctx(nc.sbuf_tensor("ones_col", [128, 1], F32))
        ones_row = ctx(nc.sbuf_tensor("ones_row", [1, 128], F32))
        h2row = ctx(nc.sbuf_tensor("h2row", [1, TPC], F32))
        cols = ctx(nc.sbuf_tensor("cols", [128, 96], F32))
        vals = ctx(nc.sbuf_tensor("vals", [1, 16], F32))
        svraw = ctx(nc.sbuf_tensor("svraw", [16, 128], F32))

        P = [ctx(nc.psum_tensor(f"p{i}", [128, 512], F32)) for i in range(7)]
        PM = ctx(nc.psum_tensor("pm", [128, 512], F32))

        dLdS = Sem(ctx(nc.semaphore("dLdS")))
        dLdC = Sem(ctx(nc.semaphore("dLdC")))
        dLdG = Sem(ctx(nc.semaphore("dLdG")))
        dSp = Sem(ctx(nc.semaphore("dSp")))
        dOut = Sem(ctx(nc.semaphore("dOut")))
        cV = Sem(ctx(nc.semaphore("cV")))
        cS = Sem(ctx(nc.semaphore("cS")))
        cT = Sem(ctx(nc.semaphore("cT")))

        A_WSL = {"i": 0, "f": 8192, "g": 16384}
        A_XC = 24576
        A_SGB = 32768
        A_OAST = 34816
        B_Y1 = 0
        B_Y0 = 8192
        B_WOT = 16384
        B_XQT = 16384
        B_OACT = 24576
        B_G = 24576
        B_HQT = 24576
        B_U = 34816
        B_XQ = 8192
        B_HQST = 10240
        B_DNRING = 12288
        FA_IB = 0
        FA_HB = 2048
        FA_H2 = 4096
        FA_TMP = 4608
        FA_FB = 5120
        FB_XN = 0
        FB_GR = 2048
        FB_T1 = 2560
        FB_SCR = 3072
        FB_OUT = 5120
        FB_HID = 7168
        FB_Y0F = 10240

        def ab(off, n):
            return AB[:, off:off + n]

        def af(off, n):
            return AFr[:, off:off + n]

        def ai(off, n):
            return AI[:, off:off + n]

        prog.op("vector", lambda e: e.memset(ones_col[:], 1.0))
        prog.op("vector", lambda e: e.memset(ones_row[:], 1.0))
        prog.op("vector", lambda e: e.memset(h2row[:], 0.0))
        prog.op("vector", lambda e: e.memset(carry[:], 0.0))
        vset = prog.op("vector", lambda e: e.memset(cols[:], 0.0), cV)

        n = prog.dma("sync", eyef[:], eye_in, dLdS)
        prog.wait("tensor", dLdS, n)
        prog.dma("sync", nwbb[:], nwb_in, dLdS)

        sv_order = [("sv_i", 0), ("sv_f", 1), ("sv_g", 2), ("sv_gn", 3),
                    ("sv_o", 4), ("sv_mk", 5)]
        for nm, idx in sv_order:
            n = prog.dma("sync", svraw[:], sv_in[nm], dLdS)
            prog.wait("tensor", dLdS, n)
            tn = prog.op("tensor", lambda e: e.transpose(
                PM[:, 0:16], svraw[0:16, :], eyef[0:16, 0:16]), cT)
            prog.wait("vector", cT, tn)
            vn = prog.op("vector", lambda e, idx=idx: e.tensor_copy(
                svc[:, idx * 16:(idx + 1) * 16], PM[:, 0:16]), cV)
            prog.wait("tensor", cV, vn)
            prog.wait("sync", cT, tn)
        SV_SI256, SV_SF256, SV_SG256, SV_GNSO, SV_MASK = 96 - 16, 96, 32, 48, 80
        vn = prog.vchain(cV, [
            lambda e: e.tensor_scalar(
                svc[:, 0:16], svc[:, 0:16], 1.0 / 256.0, None, ALU.mult),
            lambda e: e.tensor_scalar(
                svc[:, 16:32], svc[:, 16:32], 1.0 / 256.0, None, ALU.mult),
            lambda e: e.tensor_scalar(
                svc[:, 32:48], svc[:, 32:48], 1.0 / 256.0, None, ALU.mult),
            lambda e: e.tensor_tensor(
                svc[:, 48:64], svc[:, 48:64], svc[:, 64:80], ALU.mult),
        ])
        SV_SI256, SV_SF256, SV_SG256, SV_GNSO, SV_MASK = 0, 16, 32, 48, 80

        for k in range(NKD):
            buf = A_XC + (k % 2) * 2048
            if k >= 2:
                prog.wait("gpsimd", dSp, 16 * (k - 1))
            prog.dma("gpsimd", ab(buf, 2048), w_oT[128 * k:128 * (k + 1), :], dLdG)
            prog.wait("gpsimd", dLdG)
            prog.dma("gpsimd", woTb[128 * k:128 * (k + 1), :], ab(buf, 2048), dSp)
        woT_done = dSp.n

        m = 0
        for k in range(NKD):
            for h in range(2):
                rs = slice(128 * k, 128 * (k + 1))
                cs = slice(2048 * h, 2048 * (h + 1))
                fo = FA_IB + (m % 2) * 2048
                io_ = (m % 2) * 2048
                bo = A_WSL["i"] + (m % 2) * 2048
                if m >= 2:
                    prog.wait("sync", dSp, woT_done + 16 * (m - 1))
                n = prog.dma("sync", af(fo, 2048), xT[rs, cs], dLdS)
                prog.wait("vector", dLdS, n)
                prog.op("vector", lambda e, fo=fo, io_=io_: e.tensor_scalar(
                    ai(io_, 2048), af(fo, 2048), 256.0, None, ALU.mult))
                vn = prog.op("vector", lambda e, io_=io_, bo=bo: e.tensor_copy(
                    ab(bo, 2048), ai(io_, 2048)), cV)
                prog.wait("sync", cV, vn)
                prog.dma("sync", x256s[rs, cs], ab(bo, 2048), dSp)
                m += 1
        x256_done = dSp.n

        def wtiles():
            for k in range(NKD):
                for j in range(5):
                    yield (w_guT, wqgu, 128 * k, 2048 * j, 2048, 0)
                yield (w_guT, wqgu, 128 * k, 10240, 1024, 0)
            for k in range(F // 128):
                yield (w_dnT, wqdn, 128 * k, 0, 2048, 1)

        prog.wait("scalar", cV, vset)
        tcount = 0
        for (wt, _, r0, c0, wd, which) in wtiles():
            fo = (tcount % 2) * 2048
            n = prog.dma("sync", af(fo, wd), wt[r0:r0 + 128, c0:c0 + wd], dLdS)
            prog.wait("scalar", dLdS, n)
            sn = prog.op("scalar", lambda e, fo=fo, wd=wd, which=which: e.activation(
                af(4096, wd), af(fo, wd), AF.Abs,
                accum_out=cols[:, 34 + which:35 + which]), cS)
            prog.wait("vector", cS, sn)
            vn = prog.op("vector", lambda e, which=which: e.tensor_tensor(
                cols[:, 32 + which:33 + which], cols[:, 32 + which:33 + which],
                cols[:, 34 + which:35 + which], ALU.add), cV)
            prog.wait("scalar", cV, vn)
            prog.wait("sync", cS, sn)
            tcount += 1
        prog.wait("tensor", cV, cV.n)
        tn = prog.op("tensor", lambda e: e.matmul(
            PM[0:1, 0:1], cols[:, 32:33], ones_col[:], start=True, stop=True), cT)
        tn = prog.op("tensor", lambda e: e.matmul(
            PM[0:1, 1:2], cols[:, 33:34], ones_col[:], start=False, stop=True,
            skip_group_check=True), cT)
        prog.wait("vector", cT, tn)
        vn = prog.vchain(cV, [
            lambda e: e.tensor_copy(vals[0:1, 0:2], PM[0:1, 0:2]),
        ])
        prog.wait("tensor", cV, vn)
        tn = prog.op("tensor", lambda e: e.matmul(
            PM[:, 16:18], ones_row[:], vals[0:1, 0:2], start=True, stop=True,
            skip_group_check=True), cT)
        prog.wait("vector", cT, tn)
        vn = prog.vchain(cV, [
            lambda e: e.tensor_copy(cols[:, 70:72], PM[:, 16:18]),
            lambda e: e.tensor_scalar(
                cols[:, 36:37], cols[:, 70:71], 1.0 / NG, 1e-5, ALU.mult, ALU.max),
            lambda e: e.tensor_scalar(
                cols[:, 37:38], cols[:, 71:72], 1.0 / ND, 1e-5, ALU.mult, ALU.max),
            lambda e: e.reciprocal(cols[:, 38:40], cols[:, 36:38]),
        ])
        C_MW1, C_MW2, C_SW1, C_SW2 = 36, 37, 38, 39
        prog.wait("tensor", cV, vn)

        tcount = 0
        for (wt, dst, r0, c0, wd, which) in wtiles():
            fo = (tcount % 2) * 2048
            io_ = (tcount % 2) * 2048
            bo = A_WSL["g"] + (tcount % 2) * 2048
            n = prog.dma("sync", af(fo, wd), wt[r0:r0 + 128, c0:c0 + wd], dLdS)
            if tcount >= 2:
                prog.wait("vector", dSp, x256_done + 16 * (tcount - 1))
            prog.wait("vector", dLdS, n)
            swcol = cols[:, C_SW1 + which:C_SW1 + which + 1]
            prog.op("vector", lambda e, fo=fo, io_=io_, wd=wd, sw=swcol: e.tensor_scalar(
                ai(io_, wd), af(fo, wd), sw, None, ALU.mult))
            vn = prog.op("vector", lambda e, io_=io_, bo=bo, wd=wd: e.tensor_scalar(
                ab(bo, wd), ai(io_, wd), 1.0, -1.0, ALU.min, ALU.max), cV)
            prog.wait("sync", cV, vn)
            prog.dma("sync", dst[r0:r0 + 128, c0:c0 + wd], ab(bo, wd), dSp)
            tcount += 1

        prog.wait("gpsimd", dSp, dSp.n)
        prog.wait("gpsimd", cV, cV.n)

        for es in range(NES):
            prog.wait("gpsimd", cT, cT.n)
            for k in range(NKD):
                for wname, wap in (("i", w_i), ("f", w_f), ("g", w_g)):
                    prog.dma("gpsimd", ab(A_WSL[wname] + k * 512, 512),
                             wap[128 * k:128 * (k + 1), es * ESL:(es + 1) * ESL],
                             dLdG)
            wsl_ready = dLdG.n

            for tc in range(NTCA):
                stored = tc >= NPRE
                tcol = tc * TC
                prog.wait("sync", cT, cT.n)
                xn = 0
                for k in range(NKD):
                    xn = prog.dma("sync", ab(A_XC + k * 512, 512),
                                  x256s[128 * k:128 * (k + 1), tcol:tcol + TC],
                                  dLdS)
                prog.wait("tensor", dLdS, xn)
                prog.wait("tensor", dLdG, wsl_ready)
                prog.wait("tensor", cV, cV.n)
                prog.wait("tensor", cS, cS.n)

                def mm_four(woff):
                    last = None
                    for et in range(4):
                        for kd in range(NKD):
                            def fn(e, et=et, kd=kd, woff=woff):
                                return e.matmul(
                                    P[et][:, 0:TC],
                                    ab(woff + kd * 512 + et * 128, 128),
                                    ab(A_XC + kd * 512, 512),
                                    start=(kd == 0), stop=(kd == NKD - 1))
                            if kd == NKD - 1:
                                last = prog.op("tensor", fn, cT)
                            else:
                                prog.op("tensor", fn)
                    return last

                tn = mm_four(A_WSL["f"])
                prog.wait("scalar", cT, tn)
                for et in range(4):
                    k = es * 4 + et
                    prog.op("scalar", lambda e, et=et, k=k: e.activation(
                        af(FA_FB + et * 512, 512), P[et][:, 0:TC], AF.Sigmoid,
                        scale=svc[:, SV_SF256 + k:SV_SF256 + k + 1]), cS)
                prog.wait("tensor", cS, cS.n)
                tn = mm_four(A_WSL["i"])
                prog.wait("vector", cT, tn)
                prog.wait("vector", cV, cV.n)
                for et in range(4):
                    k = es * 4 + et
                    prog.op("vector", lambda e, et=et, k=k: e.tensor_scalar(
                        af(FA_IB + et * 512, 512), P[et][:, 0:TC],
                        svc[:, SV_SI256 + k:SV_SI256 + k + 1], None, ALU.mult))
                    prog.op("vector", lambda e, et=et: e.tensor_tensor(
                        af(FA_TMP, 512), af(FA_FB + et * 512, 512),
                        af(FA_IB + et * 512, 512), ALU.mult))
                    prog.op("vector", lambda e, et=et: e.tensor_tensor(
                        af(FA_IB + et * 512, 512), af(FA_IB + et * 512, 512),
                        af(FA_TMP, 512), ALU.subtract))
                    vn = prog.op("vector", lambda e, et=et, k=k: e.tensor_tensor_scan(
                        af(FA_HB + et * 512, 512), af(FA_FB + et * 512, 512),
                        af(FA_IB + et * 512, 512), carry[:, k:k + 1],
                        ALU.mult, ALU.add), cV)
                    prog.wait("vector", cV, vn)
                    vn = prog.op("vector", lambda e, et=et, k=k: e.tensor_copy(
                        carry[:, k:k + 1], af(FA_HB + et * 512 + 511, 1)), cV)
                    prog.wait("vector", cV, vn)
                if tc == NPRE - 1:
                    vn = prog.op("vector", lambda e, es=es: e.tensor_tensor(
                        carry[:, es * 4:es * 4 + 4], carry[:, es * 4:es * 4 + 4],
                        svc[:, SV_MASK + es * 4:SV_MASK + es * 4 + 4],
                        ALU.mult), cV)
                    prog.wait("vector", cV, vn)
                if stored:
                    prog.wait("tensor", cV, cV.n)
                    tn = mm_four(A_WSL["g"])
                    prog.wait("scalar", cT, tn)
                    for et in range(4):
                        k = es * 4 + et
                        prog.op("scalar", lambda e, et=et, k=k: e.activation(
                            ab(A_SGB + et * 512, 512), P[et][:, 0:TC], AF.Silu,
                            scale=svc[:, SV_SG256 + k:SV_SG256 + k + 1]), cS)
                    prog.wait("vector", cS, cS.n)
                    prog.wait("vector", dSp, dSp.n)
                    stcol = tcol - NPRE * TC
                    for et in range(4):
                        k = es * 4 + et
                        prog.op("vector", lambda e, et=et, k=k: e.scalar_tensor_tensor(
                            ab(A_OAST + et * 512, 512), af(FA_HB + et * 512, 512),
                            svc[:, SV_GNSO + k:SV_GNSO + k + 1],
                            ab(A_SGB + et * 512, 512), ALU.mult, ALU.mult), cV)
                        sn = prog.op("scalar", lambda e, et=et: e.activation(
                            af(FA_H2, 512), af(FA_HB + et * 512, 512),
                            AF.Square), cS)
                        prog.wait("tensor", cS, sn)
                        tn = prog.op("tensor", lambda e, et=et: e.matmul(
                            PM[0:1, 0:TC], ones_col[:], af(FA_H2, 512),
                            start=(et == 0), stop=(et == 3),
                            skip_group_check=True), cT)
                        if et < 3:
                            prog.wait("scalar", cT, tn)
                    prog.wait("vector", cT, tn)
                    vn = prog.op("vector", lambda e, stcol=stcol: e.tensor_tensor(
                        h2row[0:1, stcol:stcol + TC], h2row[0:1, stcol:stcol + TC],
                        PM[0:1, 0:TC], ALU.add), cV)
                    prog.wait("gpsimd", cV, cV.n)
                    for et in range(4):
                        prog.dma("gpsimd",
                                 oacts[es * ESL + et * 128:es * ESL + et * 128 + 128,
                                       stcol:stcol + TC],
                                 ab(A_OAST + et * 512, 512), dSp)

        prog.wait("tensor", cV, cV.n)
        for j in range(TPC // 128):
            tn = prog.op("tensor", lambda e, j=j: e.transpose(
                PM[:, 0:1], h2row[0:1, 128 * j:128 * (j + 1)],
                eyef[0:1, 0:1]), cT)
            prog.wait("vector", cT, tn)
            vn = prog.op("vector", lambda e, j=j: e.tensor_copy(
                cols[:, j:j + 1], PM[:, 0:1]), cV)
            prog.wait("tensor", cV, vn)
        vn = prog.vchain(cV, [
            lambda e: e.tensor_scalar(
                cols[:, 0:16], cols[:, 0:16], 1.0 / D, EPS, ALU.mult, ALU.add),
        ])
        prog.wait("scalar", cV, vn)
        sn = prog.op("scalar", lambda e: e.activation(
            cols[:, 0:16], cols[:, 0:16], AF.Sqrt), cS)
        prog.wait("vector", cS, sn)
        vn = prog.vchain(cV, [
            lambda e: e.reciprocal(cols[:, 0:16], cols[:, 0:16]),
            lambda e: e.tensor_scalar(
                cols[:, 16:32], cols[:, 0:16], 256.0, None, ALU.mult),
        ])
        C_R256 = 16

        prog.wait("sync", cV, cV.n)
        prog.wait("sync", dSp, dSp.n)
        prog.wait("scalar", dSp, dSp.n)

        for bc in range(NBC):
            t0 = bc * TB
            prog.wait("sync", cT, cT.n)
            oact_n = 0
            for k in range(NKD):
                oact_n = prog.dma("sync", ab(B_OACT + k * 512, 512),
                                  oacts[128 * k:128 * (k + 1), t0:t0 + TB], dLdS)
            for osl in range(NOSL):
                prog.wait("scalar", cT, cT.n)
                wot_n = 0
                for k in range(NKD):
                    wot_n = prog.dma("scalar", ab(B_WOT + k * 512, 512),
                                     woTb[128 * k:128 * (k + 1),
                                          osl * 512:(osl + 1) * 512], dLdC)
                prog.wait("sync", cV, cV.n)
                xn_n = 0
                for ts in range(NTS):
                    xn_n = prog.dma("sync", af(FB_XN + ts * 512, 512),
                                    xnat[t0 + ts * 128:t0 + ts * 128 + 128,
                                         osl * 512:(osl + 1) * 512], dLdS)
                prog.wait("tensor", dLdC, wot_n)
                prog.wait("tensor", dLdS, oact_n)
                prog.wait("tensor", cV, cV.n)
                tn = None
                for kd in range(NKD):
                    for ts in range(NTS):
                        def fn(e, kd=kd, ts=ts):
                            return e.matmul(
                                P[ts][:, 0:512],
                                ab(B_OACT + kd * 512 + ts * 128, 128),
                                ab(B_WOT + kd * 512, 512),
                                start=(kd == 0), stop=(kd == NKD - 1))
                        if kd == NKD - 1 and ts == NTS - 1:
                            tn = prog.op("tensor", fn, cT)
                        else:
                            prog.op("tensor", fn)
                prog.wait("vector", cT, tn)
                prog.wait("vector", dLdS, xn_n)
                for ts in range(NTS):
                    rc = cols[:, C_R256 + bc * 4 + ts:C_R256 + bc * 4 + ts + 1]
                    prog.op("vector", lambda e, ts=ts, rc=rc: e.tensor_scalar(
                        ai(ts * 512, 512), P[ts][:, 0:512], rc, None, ALU.mult))
                    vn = prog.op("vector", lambda e, ts=ts, osl=osl: e.scalar_tensor_tensor(
                        af(FB_Y0F + ts * 2048 + osl * 512, 512), ai(ts * 512, 512),
                        1.0 / 256.0, af(FB_XN + ts * 512, 512),
                        ALU.mult, ALU.add), cV)

            xq_sp_base = dSp.n
            for ts in range(NTS):
                y0t = af(FB_Y0F + ts * 2048, 2048)
                vn = prog.vchain(cV, [
                    lambda e, y0t=y0t: e.reduce_sum(cols[:, 40:41], y0t, axis=AX.X),
                    lambda e: e.tensor_scalar(
                        cols[:, 40:41], cols[:, 40:41], 1.0 / D, None, ALU.mult),
                ])
                prog.wait("scalar", cV, vn)
                sn = prog.op("scalar", lambda e, y0t=y0t: e.activation(
                    af(FB_SCR, 2048), y0t, AF.Square,
                    accum_out=cols[:, 41:42]), cS)
                prog.wait("vector", cS, sn)
                vn = prog.vchain(cV, [
                    lambda e: e.tensor_tensor(
                        cols[:, 42:43], cols[:, 40:41], cols[:, 40:41], ALU.mult),
                    lambda e: e.tensor_scalar(
                        cols[:, 41:42], cols[:, 41:42], 1.0 / D, None, ALU.mult),
                    lambda e: e.tensor_tensor(
                        cols[:, 41:42], cols[:, 41:42], cols[:, 42:43], ALU.subtract),
                    lambda e: e.tensor_scalar(
                        cols[:, 41:42], cols[:, 41:42], 1.0, EPS, ALU.mult, ALU.add),
                ])
                prog.wait("scalar", cV, vn)
                sn = prog.op("scalar", lambda e: e.activation(
                    cols[:, 41:42], cols[:, 41:42], AF.Sqrt), cS)
                prog.wait("vector", cS, sn)
                vn = prog.vchain(cV, [
                    lambda e: e.reciprocal(cols[:, 41:42], cols[:, 41:42]),
                    lambda e, ts=ts, y0t=y0t: e.tensor_scalar(
                        ab(B_Y1 + ts * 2048, 2048), y0t, cols[:, 40:41],
                        cols[:, 41:42], ALU.subtract, ALU.mult),
                ])

                prog.wait("vector", dSp, dSp.n)
                vn = prog.vchain(cV, [
                    lambda e, y0t=y0t: e.scalar_tensor_tensor(
                        af(FB_SCR, 2048), y0t, cols[:, 40:41], nwbb[:, 0:2048],
                        ALU.subtract, ALU.mult),
                    lambda e: e.reduce_max(
                        cols[:, 44:45], af(FB_SCR, 2048), axis=AX.X,
                        apply_absolute_value=True),
                    lambda e: e.tensor_tensor(
                        cols[:, 43:44], cols[:, 41:42], cols[:, 41:42], ALU.mult),
                    lambda e: e.tensor_scalar(
                        cols[:, 43:44], cols[:, 43:44], -EPS, 1.0 + EPS,
                        ALU.mult, ALU.add),
                ])
                prog.wait("scalar", cV, vn)
                sn = prog.op("scalar", lambda e: e.activation(
                    cols[:, 43:44], cols[:, 43:44], AF.Sqrt), cS)
                prog.wait("vector", cS, sn)
                vn = prog.vchain(cV, [
                    lambda e: e.reciprocal(cols[:, 43:44], cols[:, 43:44]),
                    lambda e: e.tensor_tensor(
                        cols[:, 42:43], cols[:, 41:42], cols[:, 43:44], ALU.mult),
                    lambda e: e.tensor_tensor(
                        cols[:, 45:46], cols[:, 42:43], cols[:, 44:45], ALU.mult),
                    lambda e: e.tensor_scalar(
                        cols[:, 45:46], cols[:, 45:46], 1e-5, None, ALU.max),
                    lambda e: e.reciprocal(cols[:, 46:47], cols[:, 45:46]),
                    lambda e: e.tensor_scalar(
                        cols[:, 46:47], cols[:, 46:47], 127.0, None, ALU.mult),
                    lambda e: e.tensor_tensor(
                        cols[:, 46:47], cols[:, 46:47], cols[:, 42:43], ALU.mult),
                    lambda e, ts=ts: e.tensor_scalar(
                        cols[:, 48 + ts:49 + ts], cols[:, 45:46], 1.0 / 127.0,
                        None, ALU.mult),
                    lambda e, ts=ts: e.tensor_tensor(
                        cols[:, 48 + ts:49 + ts], cols[:, 48 + ts:49 + ts],
                        cols[:, C_MW1:C_MW1 + 1], ALU.mult),
                    lambda e: e.tensor_scalar(
                        ai(0, 2048), af(FB_SCR, 2048), cols[:, 46:47], None,
                        ALU.mult),
                    lambda e: e.tensor_copy(ab(B_XQ, 2048), ai(0, 2048)),
                ])
                prog.wait("sync", cV, vn)
                prog.dma("sync", xqs[ts * 128:(ts + 1) * 128, :], ab(B_XQ, 2048),
                         dSp)
            xq_sp = dSp.n
            if DEBUG and bc == 0:
                prog.wait("gpsimd", cV, cV.n)
                prog.dma("gpsimd", dbg_y0, ab(B_Y1, 8192), dOut)
                prog.dma("gpsimd", dbg_y1, ab(B_Y1, 8192), dOut)

            prog.wait("sync", cT, cT.n)
            prog.wait("sync", dSp, xq_sp)
            xqt_n = 0
            for kd in range(NKD):
                xqt_n = prog.dma("sync", ab(B_XQT + kd * 512, 512),
                                 xqs[0:TB, kd * 128:(kd + 1) * 128], dLdS,
                                 transpose=True)

            PU = [P[4], P[5], P[6], PM]
            for jp in range(NJP):
                prog.wait("scalar", cV, cV.n)
                gu_n = 0
                for kd in range(NKD):
                    prog.dma("scalar", ab(B_G + kd * 512, 512),
                             wqgu[128 * kd:128 * (kd + 1),
                                  jp * 512:(jp + 1) * 512], dLdC)
                    gu_n = prog.dma("scalar", ab(B_U + kd * 512, 512),
                                    wqgu[128 * kd:128 * (kd + 1),
                                         F + jp * 512:F + (jp + 1) * 512], dLdC)
                prog.wait("tensor", dLdC, gu_n)
                prog.wait("tensor", dLdS, xqt_n)
                prog.wait("tensor", cV, cV.n)
                prog.wait("tensor", cS, cS.n)
                tn = None
                for kd in range(NKD):
                    for ts in range(NTS):
                        prog.op("tensor", lambda e, kd=kd, ts=ts: e.matmul(
                            P[ts][:, 0:512],
                            ab(B_XQT + kd * 512 + ts * 128, 128),
                            ab(B_G + kd * 512, 512),
                            start=(kd == 0), stop=(kd == NKD - 1)))
                        def fnu(e, kd=kd, ts=ts):
                            return e.matmul(
                                PU[ts][:, 0:512],
                                ab(B_XQT + kd * 512 + ts * 128, 128),
                                ab(B_U + kd * 512, 512),
                                start=(kd == 0), stop=(kd == NKD - 1))
                        if kd == NKD - 1 and ts == NTS - 1:
                            tn = prog.op("tensor", fnu, cT)
                        else:
                            prog.op("tensor", fnu)
                prog.wait("scalar", cT, tn)
                prog.wait("vector", cT, tn)
                prog.wait("vector", dSp, dSp.n)
                for ts in range(NTS):
                    hb = FB_HID + ts * 512
                    sn = prog.op("scalar", lambda e, ts=ts: e.activation(
                        af(FB_GR, 512), P[ts][:, 0:512], AF.Relu), cS)
                    prog.wait("vector", cS, sn)
                    prog.op("vector", lambda e, ts=ts: e.tensor_tensor(
                        af(FB_T1, 512), af(FB_GR, 512), PU[ts][:, 0:512],
                        ALU.mult))
                    vn = prog.op("vector", lambda e, hb=hb: e.tensor_tensor(
                        af(hb, 512), af(FB_T1, 512), af(FB_GR, 512), ALU.mult), cV)
                    prog.wait("vector", cV, cV.n)
                    vn2 = prog.op("vector", lambda e, hb=hb: e.reduce_max(
                        cols[:, 56:57], af(hb, 512), axis=AX.X,
                        apply_absolute_value=True), cV)
                    prog.wait("vector", cV, vn2)
                    vn2 = prog.op("vector", lambda e, ts=ts: e.tensor_tensor(
                        cols[:, 52 + ts:53 + ts], cols[:, 52 + ts:53 + ts],
                        cols[:, 56:57], ALU.max), cV)
                    prog.wait("vector", cV, vn2)
                    prog.wait("gpsimd", cV, vn)
                    prog.dma("gpsimd", hids[ts * 128:(ts + 1) * 128,
                                            jp * 512:(jp + 1) * 512],
                             af(hb, 512), dSp)
                    prog.wait("scalar", cV, vn)
            hid_sp = dSp.n

            vn = prog.vchain(cV, [
                lambda e: e.tensor_tensor(
                    cols[:, 66:70], cols[:, 48:52], cols[:, 48:52], ALU.mult),
                lambda e: e.tensor_tensor(
                    cols[:, 66:70], cols[:, 66:70], cols[:, 48:52], ALU.mult),
                lambda e: e.tensor_tensor(
                    cols[:, 52:56], cols[:, 52:56], cols[:, 66:70], ALU.mult),
                lambda e: e.tensor_scalar(
                    cols[:, 52:56], cols[:, 52:56], 1e-5, None, ALU.max),
                lambda e: e.reciprocal(cols[:, 57:61], cols[:, 52:56]),
                lambda e: e.tensor_scalar(
                    cols[:, 57:61], cols[:, 57:61], 127.0, None, ALU.mult),
                lambda e: e.tensor_tensor(
                    cols[:, 57:61], cols[:, 57:61], cols[:, 66:70], ALU.mult),
                lambda e: e.tensor_scalar(
                    cols[:, 62:66], cols[:, 52:56], 1.0 / 127.0, None, ALU.mult),
                lambda e: e.tensor_scalar(
                    cols[:, 62:66], cols[:, 62:66], cols[:, C_MW2:C_MW2 + 1],
                    None, ALU.mult),
            ])

            prog.wait("sync", dSp, hid_sp)
            prog.wait("sync", cV, vn)
            hq_pieces = [(0, 2048), (2048, 2048), (4096, 1536)]
            for ts in range(NTS):
                for (c0, wd) in hq_pieces:
                    prog.wait("sync", dSp, dSp.n)
                    n = prog.dma("sync", af(FB_SCR, wd),
                                 hids[ts * 128:(ts + 1) * 128, c0:c0 + wd], dLdS)
                    prog.wait("vector", dLdS, n)
                    prog.op("vector", lambda e, ts=ts, wd=wd: e.tensor_scalar(
                        ai(0, wd), af(FB_SCR, wd),
                        cols[:, 57 + ts:58 + ts], None, ALU.mult))
                    vn = prog.op("vector", lambda e, wd=wd: e.tensor_copy(
                        ab(B_HQST, wd), ai(0, wd)), cV)
                    prog.wait("sync", cV, vn)
                    prog.dma("sync", hqs[ts * 128:(ts + 1) * 128, c0:c0 + wd],
                             ab(B_HQST, wd), dSp)
            hq_sp = dSp.n

            prog.wait("sync", cT, cT.n)
            prog.wait("sync", dSp, hq_sp)
            hqt_n = 0
            for kf in range(NKF):
                hqt_n = prog.dma("sync", ab(B_HQT + kf * 512, 512),
                                 hqs[0:TB, kf * 128:(kf + 1) * 128], dLdS,
                                 transpose=True)

            for osl in range(NOSL):
                prog.wait("tensor", dLdS, hqt_n)
                prog.wait("tensor", cV, cV.n)
                prog.wait("tensor", cS, cS.n)
                ct_base = cT.n
                dn_n = []
                for kf in range(NKF):
                    ring = B_DNRING + (kf % 4) * 512
                    if kf >= 4:
                        prog.wait("scalar", cT, ct_base + (kf - 3))
                    else:
                        prog.wait("scalar", cT, ct_base)
                    dn_n.append(prog.dma("scalar", ab(ring, 512),
                                         wqdn[128 * kf:128 * (kf + 1),
                                              osl * 512:(osl + 1) * 512], dLdC))
                    prog.wait("tensor", dLdC, dn_n[-1])
                    for ts in range(NTS):
                        def fnd(e, kf=kf, ts=ts, ring=ring):
                            return e.matmul(
                                P[ts][:, 0:512],
                                ab(B_HQT + kf * 512 + ts * 128, 128),
                                ab(ring, 512),
                                start=(kf == 0), stop=(kf == NKF - 1))
                        if ts == NTS - 1:
                            tn = prog.op("tensor", fnd, cT)
                        else:
                            prog.op("tensor", fnd)
                prog.wait("vector", cT, tn)
                for ts in range(NTS):
                    vn = prog.op("vector", lambda e, ts=ts, osl=osl: e.scalar_tensor_tensor(
                        af(FB_Y0F + ts * 2048 + osl * 512, 512), P[ts][:, 0:512],
                        cols[:, 62 + ts:63 + ts],
                        ab(B_Y1 + ts * 2048 + osl * 512, 512),
                        ALU.mult, ALU.add), cV)

            for ts in range(NTS):
                y2t = af(FB_Y0F + ts * 2048, 2048)
                vn = prog.vchain(cV, [
                    lambda e, y2t=y2t: e.reduce_sum(cols[:, 40:41], y2t, axis=AX.X),
                    lambda e: e.tensor_scalar(
                        cols[:, 40:41], cols[:, 40:41], 1.0 / D, None, ALU.mult),
                ])
                prog.wait("scalar", cV, vn)
                sn = prog.op("scalar", lambda e, y2t=y2t: e.activation(
                    af(FB_SCR, 2048), y2t, AF.Square,
                    accum_out=cols[:, 41:42]), cS)
                prog.wait("vector", cS, sn)
                vn = prog.vchain(cV, [
                    lambda e: e.tensor_tensor(
                        cols[:, 42:43], cols[:, 40:41], cols[:, 40:41], ALU.mult),
                    lambda e: e.tensor_scalar(
                        cols[:, 41:42], cols[:, 41:42], 1.0 / D, None, ALU.mult),
                    lambda e: e.tensor_tensor(
                        cols[:, 41:42], cols[:, 41:42], cols[:, 42:43], ALU.subtract),
                    lambda e: e.tensor_scalar(
                        cols[:, 41:42], cols[:, 41:42], 1.0, EPS, ALU.mult, ALU.add),
                ])
                prog.wait("scalar", cV, vn)
                sn = prog.op("scalar", lambda e: e.activation(
                    cols[:, 41:42], cols[:, 41:42], AF.Sqrt), cS)
                prog.wait("vector", cS, sn)
                vn = prog.vchain(cV, [
                    lambda e: e.reciprocal(cols[:, 41:42], cols[:, 41:42]),
                ])
                prog.wait("vector", dOut, dOut.n)
                vn = prog.op("vector", lambda e, y2t=y2t: e.tensor_scalar(
                    af(FB_OUT, 2048), y2t, cols[:, 40:41], cols[:, 41:42],
                    ALU.subtract, ALU.mult), cV)
                prog.wait("sync", cV, vn)
                prog.dma("sync", out[t0 + ts * 128:t0 + ts * 128 + 128, :],
                         af(FB_OUT, 2048), dOut)
            vn = prog.op("vector", lambda e: e.memset(cols[:, 52:56], 0.0), cV)

        if DEBUG:
            prog.wait("gpsimd", cV, cV.n)
            prog.dma("gpsimd", dbg_y2p, af(FB_Y0F, 8192), dOut)
            prog.dma("gpsimd", dbg_y1l, ab(B_Y1, 8192), dOut)
            prog.dma("gpsimd", dbg_hqt, ab(B_HQT, 22528), dOut)
            prog.dma("gpsimd", dbg_h2, h2row[:], dOut)
            prog.dma("gpsimd", dbg_cols, cols[:], dOut)
            prog.dma("gpsimd", dbg_vals, vals[:], dOut)
            prog.dma("gpsimd", dbg_carry, carry[:], dOut)
            prog.dma("gpsimd", dbg_svc, svc[:], dOut)
        prog.wait("gpsimd", dOut, dOut.n)

        with nc.Block() as block:
            @block.sync
            def _(e):
                for fn in prog.q["sync"]:
                    fn(e)

            @block.scalar
            def _(e):
                for fn in prog.q["scalar"]:
                    fn(e)

            @block.vector
            def _(e):
                for fn in prog.q["vector"]:
                    fn(e)

            @block.tensor
            def _(e):
                for fn in prog.q["tensor"]:
                    fn(e)

            @block.gpsimd
            def _(e):
                for fn in prog.q["gpsimd"]:
                    fn(e)

    return nc


_NC_CACHE = None


def kernel(**inputs):
    global _NC_CACHE
    x = np.asarray(inputs["L0_attn_hidden_states"], dtype=np.float32)
    w_i = np.ascontiguousarray(np.asarray(inputs["L0_attn_w_i"], dtype=np.float32))
    w_f = np.ascontiguousarray(np.asarray(inputs["L0_attn_w_f"], dtype=np.float32))
    w_g = np.ascontiguousarray(np.asarray(inputs["L0_attn_w_g"], dtype=np.float32))
    w_oT = np.ascontiguousarray(np.asarray(inputs["L0_attn_w_o"], dtype=np.float32).T)
    s_i = np.asarray(inputs["L0_attn_s_i"], dtype=np.float32)
    s_f = np.asarray(inputs["L0_attn_s_f"], dtype=np.float32)
    s_g = np.asarray(inputs["L0_attn_s_g"], dtype=np.float32)
    s_o = np.asarray(inputs["L0_attn_s_o"], dtype=np.float32)
    g_norm = np.asarray(inputs["L0_attn_g_norm_w"], dtype=np.float32)
    w_guT = np.ascontiguousarray(np.asarray(inputs["L0_ffn_w_gate_up"], dtype=np.float32).T)
    w_dnT = np.ascontiguousarray(np.asarray(inputs["L0_ffn_w_down"], dtype=np.float32).T)
    norm_w = np.asarray(inputs["L0_ffn_norm_weight"], dtype=np.float32)

    if _NC_CACHE is None:
        _NC_CACHE = build_nc()
    nc = _NC_CACHE

    eye = np.eye(128, dtype=np.float32)
    nwb = np.ascontiguousarray(np.broadcast_to(norm_w[None, :], (128, D)))
    svs = {
        "sv_i": np.ascontiguousarray(s_i.reshape(16, 128)),
        "sv_f": np.ascontiguousarray(s_f.reshape(16, 128)),
        "sv_g": np.ascontiguousarray(s_g.reshape(16, 128)),
        "sv_gn": np.ascontiguousarray(g_norm.reshape(16, 128)),
        "sv_o": np.ascontiguousarray(s_o.reshape(16, 128)),
    }

    in_maps = []
    for c in range(8):
        b, s = c // 2, c % 2
        if s == 0:
            xTc = np.ascontiguousarray(
                np.concatenate([x[b, 0:2048].T, x[b, 0:2048].T], axis=1))
            mk = np.zeros((16, 128), np.float32)
        else:
            xTc = np.ascontiguousarray(x[b].T)
            mk = np.ones((16, 128), np.float32)
        in_maps.append(dict(
            xT=xTc,
            xnat=np.ascontiguousarray(x[b, s * 2048:(s + 1) * 2048]),
            w_i=w_i, w_f=w_f, w_g=w_g, w_oT=w_oT, w_guT=w_guT, w_dnT=w_dnT,
            nwb=nwb, eye=eye, sv_mk=np.ascontiguousarray(mk), **svs,
        ))

    res = run_bass_kernel_spmd(nc, in_maps, core_ids=list(range(8)))
    outp = np.empty((B, T, D), dtype=np.float32)
    for c in range(8):
        b, s = c // 2, c % 2
        outp[b, s * 2048:(s + 1) * 2048] = res.results[c]["out"]
    if DEBUG:
        return outp, res.results
    return outp


# version 03
# speedup vs baseline: 1.1130x; 1.1130x over previous
from contextlib import ExitStack

import numpy as np

import concourse.bass as bass
import concourse.mybir as mybir
from concourse.bass_utils import run_bass_kernel_spmd

F32 = mybir.dt.float32
BF16 = mybir.dt.bfloat16
I32 = mybir.dt.int32
ALU = mybir.AluOpType
AF = mybir.ActivationFunctionType
AX = mybir.AxisListType

B, T, D = 4, 4096, 2048
F = 5632
F2 = 2 * F
TPC = 2048
NKD = D // 128
ESL = 512
NES = D // ESL
TC = 512
NTCA = T // TC
NPRE = 4
TB = 512
NBC = TPC // TB
NTS = TB // 128
NOSL = D // 512
NJP = F // 512
NKF = F // 128
EPS = 1e-5
NG = F2 * D
ND = F * D

ENGINES = ("sync", "scalar", "vector", "tensor", "gpsimd")
DEBUG = False


class Sem:
    def __init__(self, handle):
        self.h = handle
        self.n = 0

    def inc(self, k):
        self.n += k
        return self.n


class Prog:
    def __init__(self):
        self.q = {e: [] for e in ENGINES}

    def on(self, eng, fn):
        self.q[eng].append(fn)

    def dma(self, eng, out, in_, sem, transpose=False):
        def fn(e, out=out, in_=in_, sem=sem, tr=transpose):
            if tr:
                e.dma_start_transpose(out, in_).then_inc(sem.h, 16)
            else:
                e.dma_start(out=out, in_=in_).then_inc(sem.h, 16)
        self.on(eng, fn)
        return sem.inc(16)

    def wait(self, eng, sem, val=None):
        v = sem.n if val is None else val
        if v <= 0:
            return
        self.on(eng, lambda e, h=sem.h, v=v: e.wait_ge(h, v))

    def vchain(self, sem, ops):
        self.wait("vector", sem, sem.n)
        vn = None
        for f in ops:
            vn = self.op("vector", f, sem)
            self.wait("vector", sem, vn)
        return vn

    def op(self, eng, fn, sem=None):
        if sem is None:
            self.on(eng, fn)
            return None
        def wrapped(e, fn=fn, sem=sem):
            fn(e).then_inc(sem.h, 1)
        self.on(eng, wrapped)
        return sem.inc(1)


def build_nc():
    nc = bass.Bass("TRN2", target_bir_lowering=False, debug=False,
                   detect_race_conditions=False)

    xT = nc.dram_tensor("xT", [D, T], F32, kind="ExternalInput").ap()
    xnat = nc.dram_tensor("xnat", [TPC, D], F32, kind="ExternalInput").ap()
    w_i = nc.dram_tensor("w_i", [D, D], F32, kind="ExternalInput").ap()
    w_f = nc.dram_tensor("w_f", [D, D], F32, kind="ExternalInput").ap()
    w_g = nc.dram_tensor("w_g", [D, D], F32, kind="ExternalInput").ap()
    w_oT = nc.dram_tensor("w_oT", [D, D], F32, kind="ExternalInput").ap()
    w_guT = nc.dram_tensor("w_guT", [D, F2], F32, kind="ExternalInput").ap()
    w_dnT = nc.dram_tensor("w_dnT", [F, D], F32, kind="ExternalInput").ap()
    sv_in = {}
    for nm in ("sv_i", "sv_f", "sv_g", "sv_gn", "sv_o", "sv_mk"):
        sv_in[nm] = nc.dram_tensor(nm, [16, 128], F32, kind="ExternalInput").ap()
    nwb_in = nc.dram_tensor("nwb", [128, D], F32, kind="ExternalInput").ap()
    eye_in = nc.dram_tensor("eye", [128, 128], F32, kind="ExternalInput").ap()
    out = nc.dram_tensor("out", [TPC, D], F32, kind="ExternalOutput").ap()

    KD = "ExternalOutput" if DEBUG else "Internal"
    x256s = nc.dram_tensor("x256s", [D, T], BF16, kind=KD).ap()
    oacts = nc.dram_tensor("oacts", [D, TPC], BF16, kind=KD).ap()
    woTb = nc.dram_tensor("woTb", [D, D], BF16, kind="Internal").ap()
    wqgu = nc.dram_tensor("wqgu", [D, F2], BF16, kind=KD).ap()
    wqdn = nc.dram_tensor("wqdn", [F, D], BF16, kind=KD).ap()
    xqs = nc.dram_tensor("xqs", [TB, D], BF16, kind=KD).ap()
    hids = nc.dram_tensor("hids", [TB, F], F32, kind=KD).ap()
    hqs = nc.dram_tensor("hqs", [TB, F], BF16, kind=KD).ap()
    if DEBUG:
        dbg_h2 = nc.dram_tensor("dbg_h2", [1, TPC], F32, kind="ExternalOutput").ap()
        dbg_cols = nc.dram_tensor("dbg_cols", [128, 96], F32, kind="ExternalOutput").ap()
        dbg_vals = nc.dram_tensor("dbg_vals", [1, 16], F32, kind="ExternalOutput").ap()
        dbg_carry = nc.dram_tensor("dbg_carry", [128, 16], F32, kind="ExternalOutput").ap()
        dbg_svc = nc.dram_tensor("dbg_svc", [128, 128], F32, kind="ExternalOutput").ap()
        dbg_y0 = nc.dram_tensor("dbg_y0", [128, 8192], BF16, kind="ExternalOutput").ap()
        dbg_y1 = nc.dram_tensor("dbg_y1", [128, 8192], BF16, kind="ExternalOutput").ap()
        dbg_y2p = nc.dram_tensor("dbg_y2p", [128, 8192], F32, kind="ExternalOutput").ap()
        dbg_y1l = nc.dram_tensor("dbg_y1l", [128, 8192], BF16, kind="ExternalOutput").ap()
        dbg_hqt = nc.dram_tensor("dbg_hqt", [128, 22528], BF16, kind="ExternalOutput").ap()

    prog = Prog()

    with ExitStack() as es_:
        ctx = es_.enter_context
        AB = ctx(nc.sbuf_tensor("AB", [128, 47104], BF16))
        AFr = ctx(nc.sbuf_tensor("AFr", [128, 18432], F32))
        AI = ctx(nc.sbuf_tensor("AI", [128, 4096], I32))
        svc = ctx(nc.sbuf_tensor("svc", [128, 128], F32))
        carry = ctx(nc.sbuf_tensor("carry", [128, 16], F32))
        nwbb = ctx(nc.sbuf_tensor("nwbb", [128, D], F32))
        eyef = ctx(nc.sbuf_tensor("eyef", [128, 128], F32))
        ones_col = ctx(nc.sbuf_tensor("ones_col", [128, 1], F32))
        ones_row = ctx(nc.sbuf_tensor("ones_row", [1, 128], F32))
        h2row = ctx(nc.sbuf_tensor("h2row", [1, TPC], F32))
        cols = ctx(nc.sbuf_tensor("cols", [128, 96], F32))
        vals = ctx(nc.sbuf_tensor("vals", [1, 16], F32))
        cols2 = ctx(nc.sbuf_tensor("cols2", [128, 160], F32))
        svraw = ctx(nc.sbuf_tensor("svraw", [16, 128], F32))

        P = [ctx(nc.psum_tensor(f"p{i}", [128, 512], F32)) for i in range(7)]
        PM = ctx(nc.psum_tensor("pm", [128, 512], F32))

        dLdS = Sem(ctx(nc.semaphore("dLdS")))
        dLdC = Sem(ctx(nc.semaphore("dLdC")))
        dLdG = Sem(ctx(nc.semaphore("dLdG")))
        dSp = Sem(ctx(nc.semaphore("dSp")))
        dOut = Sem(ctx(nc.semaphore("dOut")))
        cV = Sem(ctx(nc.semaphore("cV")))
        cS = Sem(ctx(nc.semaphore("cS")))
        cT = Sem(ctx(nc.semaphore("cT")))

        A_WSL = {"i": 0, "f": 8192, "g": 16384}
        A_XC = 24576
        A_SGB = 32768
        A_OAST = 34816
        B_Y1 = 0
        B_Y0 = 8192
        B_WOT = 16384
        B_XQT = 16384
        B_OACT = 24576
        B_G = 24576
        B_HQT = 24576
        B_U = 34816
        B_XQ = 8192
        B_HQST = 10240
        B_DNRING = 12288
        FA_IB = 0
        FA_HB = 2048
        FA_H2 = 4096
        FA_TMP = 4608
        FA_FB = 5120
        FB_XN = 0
        FB_GR = 2048
        FB_T1 = 2560
        FB_SCR = 3072
        FB_OUT = 5120
        FB_HID = 7168
        FB_Y0F = 10240

        def ab(off, n):
            return AB[:, off:off + n]

        def af(off, n):
            return AFr[:, off:off + n]

        def ai(off, n):
            return AI[:, off:off + n]

        prog.op("vector", lambda e: e.memset(ones_col[:], 1.0))
        prog.op("vector", lambda e: e.memset(ones_row[:], 1.0))
        prog.op("vector", lambda e: e.memset(h2row[:], 0.0))
        prog.op("vector", lambda e: e.memset(carry[:], 0.0))
        vset = prog.op("vector", lambda e: e.memset(cols[:], 0.0), cV)

        n = prog.dma("sync", eyef[:], eye_in, dLdS)
        prog.wait("tensor", dLdS, n)
        prog.dma("sync", nwbb[:], nwb_in, dLdS)

        sv_order = [("sv_i", 0), ("sv_f", 1), ("sv_g", 2), ("sv_gn", 3),
                    ("sv_o", 4), ("sv_mk", 5)]
        for nm, idx in sv_order:
            n = prog.dma("sync", svraw[:], sv_in[nm], dLdS)
            prog.wait("tensor", dLdS, n)
            tn = prog.op("tensor", lambda e: e.transpose(
                PM[:, 0:16], svraw[0:16, :], eyef[0:16, 0:16]), cT)
            prog.wait("vector", cT, tn)
            vn = prog.op("vector", lambda e, idx=idx: e.tensor_copy(
                svc[:, idx * 16:(idx + 1) * 16], PM[:, 0:16]), cV)
            prog.wait("tensor", cV, vn)
            prog.wait("sync", cT, tn)
        SV_SI256, SV_SF256, SV_SG256, SV_GNSO, SV_MASK = 96 - 16, 96, 32, 48, 80
        vn = prog.vchain(cV, [
            lambda e: e.tensor_scalar(
                svc[:, 0:16], svc[:, 0:16], 1.0 / 256.0, None, ALU.mult),
            lambda e: e.tensor_scalar(
                svc[:, 16:32], svc[:, 16:32], 1.0 / 256.0, None, ALU.mult),
            lambda e: e.tensor_scalar(
                svc[:, 32:48], svc[:, 32:48], 1.0 / 256.0, None, ALU.mult),
            lambda e: e.tensor_tensor(
                svc[:, 48:64], svc[:, 48:64], svc[:, 64:80], ALU.mult),
        ])
        SV_SI256, SV_SF256, SV_SG256, SV_GNSO, SV_MASK = 0, 16, 32, 48, 80

        for k in range(NKD):
            buf = A_XC + (k % 2) * 2048
            if k >= 2:
                prog.wait("gpsimd", dSp, 16 * (k - 1))
            prog.dma("gpsimd", ab(buf, 2048), w_oT[128 * k:128 * (k + 1), :], dLdG)
            prog.wait("gpsimd", dLdG)
            prog.dma("gpsimd", woTb[128 * k:128 * (k + 1), :], ab(buf, 2048), dSp)
        woT_done = dSp.n

        m = 0
        for k in range(NKD):
            for h in range(2):
                rs = slice(128 * k, 128 * (k + 1))
                cs = slice(2048 * h, 2048 * (h + 1))
                fo = FA_IB + (m % 2) * 2048
                io_ = (m % 2) * 2048
                bo = A_WSL["i"] + (m % 2) * 2048
                if m >= 2:
                    prog.wait("sync", dSp, woT_done + 16 * (m - 1))
                n = prog.dma("sync", af(fo, 2048), xT[rs, cs], dLdS)
                prog.wait("vector", dLdS, n)
                prog.op("vector", lambda e, fo=fo, io_=io_: e.tensor_scalar(
                    ai(io_, 2048), af(fo, 2048), 256.0, None, ALU.mult))
                vn = prog.op("vector", lambda e, io_=io_, bo=bo: e.tensor_copy(
                    ab(bo, 2048), ai(io_, 2048)), cV)
                prog.wait("sync", cV, vn)
                prog.dma("sync", x256s[rs, cs], ab(bo, 2048), dSp)
                m += 1
        x256_done = dSp.n

        def wtiles():
            for k in range(NKD):
                for j in range(5):
                    yield (w_guT, wqgu, 128 * k, 2048 * j, 2048, 0)
                yield (w_guT, wqgu, 128 * k, 10240, 1024, 0)
            for k in range(F // 128):
                yield (w_dnT, wqdn, 128 * k, 0, 2048, 1)

        prog.wait("scalar", cV, vset)
        tcount = 0
        sn_hist = []
        NT_GU = 96
        for (wt, _, r0, c0, wd, which) in wtiles():
            fo = (tcount % 2) * 2048
            if tcount >= 2:
                prog.wait("sync", cS, sn_hist[tcount - 2])
            n = prog.dma("sync", af(fo, wd), wt[r0:r0 + 128, c0:c0 + wd], dLdS)
            prog.wait("scalar", dLdS, n)
            sn = prog.op("scalar", lambda e, fo=fo, wd=wd, t=tcount: e.activation(
                af(4096 + (t % 2) * 2048, wd), af(fo, wd), AF.Abs,
                accum_out=cols2[:, t:t + 1]), cS)
            sn_hist.append(sn)
            tcount += 1
        prog.wait("vector", cS, cS.n)
        vn = prog.vchain(cV, [
            lambda e: e.reduce_sum(cols[:, 32:33], cols2[:, 0:NT_GU], axis=AX.X),
            lambda e: e.reduce_sum(cols[:, 33:34], cols2[:, NT_GU:tcount], axis=AX.X),
        ])
        prog.wait("tensor", cV, cV.n)
        tn = prog.op("tensor", lambda e: e.matmul(
            PM[0:1, 0:1], cols[:, 32:33], ones_col[:], start=True, stop=True), cT)
        tn = prog.op("tensor", lambda e: e.matmul(
            PM[0:1, 1:2], cols[:, 33:34], ones_col[:], start=False, stop=True,
            skip_group_check=True), cT)
        prog.wait("vector", cT, tn)
        vn = prog.vchain(cV, [
            lambda e: e.tensor_copy(vals[0:1, 0:2], PM[0:1, 0:2]),
        ])
        prog.wait("tensor", cV, vn)
        tn = prog.op("tensor", lambda e: e.matmul(
            PM[:, 16:18], ones_row[:], vals[0:1, 0:2], start=True, stop=True,
            skip_group_check=True), cT)
        prog.wait("vector", cT, tn)
        vn = prog.vchain(cV, [
            lambda e: e.tensor_copy(cols[:, 70:72], PM[:, 16:18]),
            lambda e: e.tensor_scalar(
                cols[:, 36:37], cols[:, 70:71], 1.0 / NG, 1e-5, ALU.mult, ALU.max),
            lambda e: e.tensor_scalar(
                cols[:, 37:38], cols[:, 71:72], 1.0 / ND, 1e-5, ALU.mult, ALU.max),
            lambda e: e.reciprocal(cols[:, 38:40], cols[:, 36:38]),
        ])
        C_MW1, C_MW2, C_SW1, C_SW2 = 36, 37, 38, 39
        prog.wait("tensor", cV, vn)

        tcount = 0
        vn_hist = []
        for (wt, dst, r0, c0, wd, which) in wtiles():
            fo = (tcount % 2) * 2048
            io_ = (tcount % 2) * 2048
            bo = A_WSL["g"] + (tcount % 2) * 2048
            if tcount >= 2:
                prog.wait("sync", cV, vn_hist[tcount - 2])
            n = prog.dma("sync", af(fo, wd), wt[r0:r0 + 128, c0:c0 + wd], dLdS)
            if tcount >= 2:
                prog.wait("vector", dSp, x256_done + 16 * (tcount - 1))
            prog.wait("vector", dLdS, n)
            swcol = cols[:, C_SW1 + which:C_SW1 + which + 1]
            prog.op("vector", lambda e, fo=fo, io_=io_, wd=wd, sw=swcol: e.tensor_scalar(
                ai(io_, wd), af(fo, wd), sw, None, ALU.mult))
            vn = prog.op("vector", lambda e, io_=io_, bo=bo, wd=wd: e.tensor_scalar(
                ab(bo, wd), ai(io_, wd), 1.0, -1.0, ALU.min, ALU.max), cV)
            vn_hist.append(vn)
            prog.wait("scalar", cV, vn)
            prog.dma("scalar", dst[r0:r0 + 128, c0:c0 + wd], ab(bo, wd), dSp)
            tcount += 1

        prog.wait("gpsimd", dSp, dSp.n)
        prog.wait("gpsimd", cV, cV.n)

        for es in range(NES):
            prog.wait("gpsimd", cT, cT.n)
            for k in range(NKD):
                for wname, wap in (("i", w_i), ("f", w_f), ("g", w_g)):
                    prog.dma("gpsimd", ab(A_WSL[wname] + k * 512, 512),
                             wap[128 * k:128 * (k + 1), es * ESL:(es + 1) * ESL],
                             dLdG)
            wsl_ready = dLdG.n

            for tc in range(NTCA):
                stored = tc >= NPRE
                tcol = tc * TC
                prog.wait("sync", cT, cT.n)
                xn = 0
                for k in range(NKD):
                    xn = prog.dma("sync", ab(A_XC + k * 512, 512),
                                  x256s[128 * k:128 * (k + 1), tcol:tcol + TC],
                                  dLdS)
                prog.wait("tensor", dLdS, xn)
                prog.wait("tensor", dLdG, wsl_ready)
                prog.wait("tensor", cV, cV.n)
                prog.wait("tensor", cS, cS.n)

                def mm_four(woff):
                    last = None
                    for et in range(4):
                        for kd in range(NKD):
                            def fn(e, et=et, kd=kd, woff=woff):
                                return e.matmul(
                                    P[et][:, 0:TC],
                                    ab(woff + kd * 512 + et * 128, 128),
                                    ab(A_XC + kd * 512, 512),
                                    start=(kd == 0), stop=(kd == NKD - 1))
                            if kd == NKD - 1:
                                last = prog.op("tensor", fn, cT)
                            else:
                                prog.op("tensor", fn)
                    return last

                tn = mm_four(A_WSL["f"])
                prog.wait("scalar", cT, tn)
                for et in range(4):
                    k = es * 4 + et
                    prog.op("scalar", lambda e, et=et, k=k: e.activation(
                        af(FA_FB + et * 512, 512), P[et][:, 0:TC], AF.Sigmoid,
                        scale=svc[:, SV_SF256 + k:SV_SF256 + k + 1]), cS)
                prog.wait("tensor", cS, cS.n)
                tn = mm_four(A_WSL["i"])
                prog.wait("vector", cT, tn)
                prog.wait("vector", cV, cV.n)
                for et in range(4):
                    k = es * 4 + et
                    prog.op("vector", lambda e, et=et, k=k: e.tensor_scalar(
                        af(FA_IB + et * 512, 512), P[et][:, 0:TC],
                        svc[:, SV_SI256 + k:SV_SI256 + k + 1], None, ALU.mult))
                    prog.op("vector", lambda e, et=et: e.tensor_tensor(
                        af(FA_TMP, 512), af(FA_FB + et * 512, 512),
                        af(FA_IB + et * 512, 512), ALU.mult))
                    prog.op("vector", lambda e, et=et: e.tensor_tensor(
                        af(FA_IB + et * 512, 512), af(FA_IB + et * 512, 512),
                        af(FA_TMP, 512), ALU.subtract))
                    vn = prog.op("vector", lambda e, et=et, k=k: e.tensor_tensor_scan(
                        af(FA_HB + et * 512, 512), af(FA_FB + et * 512, 512),
                        af(FA_IB + et * 512, 512), carry[:, k:k + 1],
                        ALU.mult, ALU.add), cV)
                    prog.wait("vector", cV, vn)
                    vn = prog.op("vector", lambda e, et=et, k=k: e.tensor_copy(
                        carry[:, k:k + 1], af(FA_HB + et * 512 + 511, 1)), cV)
                    prog.wait("vector", cV, vn)
                if tc == NPRE - 1:
                    vn = prog.op("vector", lambda e, es=es: e.tensor_tensor(
                        carry[:, es * 4:es * 4 + 4], carry[:, es * 4:es * 4 + 4],
                        svc[:, SV_MASK + es * 4:SV_MASK + es * 4 + 4],
                        ALU.mult), cV)
                    prog.wait("vector", cV, vn)
                if stored:
                    prog.wait("tensor", cV, cV.n)
                    tn = mm_four(A_WSL["g"])
                    prog.wait("scalar", cT, tn)
                    for et in range(4):
                        k = es * 4 + et
                        prog.op("scalar", lambda e, et=et, k=k: e.activation(
                            ab(A_SGB + et * 512, 512), P[et][:, 0:TC], AF.Silu,
                            scale=svc[:, SV_SG256 + k:SV_SG256 + k + 1]), cS)
                    prog.wait("vector", cS, cS.n)
                    prog.wait("vector", dSp, dSp.n)
                    stcol = tcol - NPRE * TC
                    for et in range(4):
                        k = es * 4 + et
                        prog.op("vector", lambda e, et=et, k=k: e.scalar_tensor_tensor(
                            ab(A_OAST + et * 512, 512), af(FA_HB + et * 512, 512),
                            svc[:, SV_GNSO + k:SV_GNSO + k + 1],
                            ab(A_SGB + et * 512, 512), ALU.mult, ALU.mult), cV)
                        sn = prog.op("scalar", lambda e, et=et: e.activation(
                            af(FA_H2, 512), af(FA_HB + et * 512, 512),
                            AF.Square), cS)
                        prog.wait("tensor", cS, sn)
                        tn = prog.op("tensor", lambda e, et=et: e.matmul(
                            PM[0:1, 0:TC], ones_col[:], af(FA_H2, 512),
                            start=(et == 0), stop=(et == 3),
                            skip_group_check=True), cT)
                        if et < 3:
                            prog.wait("scalar", cT, tn)
                    prog.wait("vector", cT, tn)
                    vn = prog.op("vector", lambda e, stcol=stcol: e.tensor_tensor(
                        h2row[0:1, stcol:stcol + TC], h2row[0:1, stcol:stcol + TC],
                        PM[0:1, 0:TC], ALU.add), cV)
                    prog.wait("gpsimd", cV, cV.n)
                    for et in range(4):
                        prog.dma("gpsimd",
                                 oacts[es * ESL + et * 128:es * ESL + et * 128 + 128,
                                       stcol:stcol + TC],
                                 ab(A_OAST + et * 512, 512), dSp)

        prog.wait("tensor", cV, cV.n)
        for j in range(TPC // 128):
            tn = prog.op("tensor", lambda e, j=j: e.transpose(
                PM[:, 0:1], h2row[0:1, 128 * j:128 * (j + 1)],
                eyef[0:1, 0:1]), cT)
            prog.wait("vector", cT, tn)
            vn = prog.op("vector", lambda e, j=j: e.tensor_copy(
                cols[:, j:j + 1], PM[:, 0:1]), cV)
            prog.wait("tensor", cV, vn)
        vn = prog.vchain(cV, [
            lambda e: e.tensor_scalar(
                cols[:, 0:16], cols[:, 0:16], 1.0 / D, EPS, ALU.mult, ALU.add),
        ])
        prog.wait("scalar", cV, vn)
        sn = prog.op("scalar", lambda e: e.activation(
            cols[:, 0:16], cols[:, 0:16], AF.Sqrt), cS)
        prog.wait("vector", cS, sn)
        vn = prog.vchain(cV, [
            lambda e: e.reciprocal(cols[:, 0:16], cols[:, 0:16]),
            lambda e: e.tensor_scalar(
                cols[:, 16:32], cols[:, 0:16], 256.0, None, ALU.mult),
        ])
        C_R256 = 16

        prog.wait("sync", cV, cV.n)
        prog.wait("sync", dSp, dSp.n)
        prog.wait("scalar", dSp, dSp.n)

        for bc in range(NBC):
            t0 = bc * TB
            prog.wait("sync", cT, cT.n)
            oact_n = 0
            for k in range(NKD):
                oact_n = prog.dma("sync", ab(B_OACT + k * 512, 512),
                                  oacts[128 * k:128 * (k + 1), t0:t0 + TB], dLdS)
            for osl in range(NOSL):
                prog.wait("scalar", cT, cT.n)
                wot_n = 0
                for k in range(NKD):
                    wot_n = prog.dma("scalar", ab(B_WOT + k * 512, 512),
                                     woTb[128 * k:128 * (k + 1),
                                          osl * 512:(osl + 1) * 512], dLdC)
                prog.wait("sync", cV, cV.n)
                xn_n = 0
                for ts in range(NTS):
                    xn_n = prog.dma("sync", af(FB_XN + ts * 512, 512),
                                    xnat[t0 + ts * 128:t0 + ts * 128 + 128,
                                         osl * 512:(osl + 1) * 512], dLdS)
                prog.wait("tensor", dLdC, wot_n)
                prog.wait("tensor", dLdS, oact_n)
                prog.wait("tensor", cV, cV.n)
                tn = None
                for kd in range(NKD):
                    for ts in range(NTS):
                        def fn(e, kd=kd, ts=ts):
                            return e.matmul(
                                P[ts][:, 0:512],
                                ab(B_OACT + kd * 512 + ts * 128, 128),
                                ab(B_WOT + kd * 512, 512),
                                start=(kd == 0), stop=(kd == NKD - 1))
                        if kd == NKD - 1 and ts == NTS - 1:
                            tn = prog.op("tensor", fn, cT)
                        else:
                            prog.op("tensor", fn)
                prog.wait("vector", cT, tn)
                prog.wait("vector", dLdS, xn_n)
                for ts in range(NTS):
                    rc = cols[:, C_R256 + bc * 4 + ts:C_R256 + bc * 4 + ts + 1]
                    prog.op("vector", lambda e, ts=ts, rc=rc: e.tensor_scalar(
                        ai(ts * 512, 512), P[ts][:, 0:512], rc, None, ALU.mult))
                    vn = prog.op("vector", lambda e, ts=ts, osl=osl: e.scalar_tensor_tensor(
                        af(FB_Y0F + ts * 2048 + osl * 512, 512), ai(ts * 512, 512),
                        1.0 / 256.0, af(FB_XN + ts * 512, 512),
                        ALU.mult, ALU.add), cV)

            xq_sp_base = dSp.n
            for ts in range(NTS):
                y0t = af(FB_Y0F + ts * 2048, 2048)
                vn = prog.vchain(cV, [
                    lambda e, y0t=y0t: e.reduce_sum(cols[:, 40:41], y0t, axis=AX.X),
                    lambda e: e.tensor_scalar(
                        cols[:, 40:41], cols[:, 40:41], 1.0 / D, None, ALU.mult),
                ])
                prog.wait("scalar", cV, vn)
                sn = prog.op("scalar", lambda e, y0t=y0t: e.activation(
                    af(FB_SCR, 2048), y0t, AF.Square,
                    accum_out=cols[:, 41:42]), cS)
                prog.wait("vector", cS, sn)
                vn = prog.vchain(cV, [
                    lambda e: e.tensor_tensor(
                        cols[:, 42:43], cols[:, 40:41], cols[:, 40:41], ALU.mult),
                    lambda e: e.tensor_scalar(
                        cols[:, 41:42], cols[:, 41:42], 1.0 / D, None, ALU.mult),
                    lambda e: e.tensor_tensor(
                        cols[:, 41:42], cols[:, 41:42], cols[:, 42:43], ALU.subtract),
                    lambda e: e.tensor_scalar(
                        cols[:, 41:42], cols[:, 41:42], 1.0, EPS, ALU.mult, ALU.add),
                ])
                prog.wait("scalar", cV, vn)
                sn = prog.op("scalar", lambda e: e.activation(
                    cols[:, 41:42], cols[:, 41:42], AF.Sqrt), cS)
                prog.wait("vector", cS, sn)
                vn = prog.vchain(cV, [
                    lambda e: e.reciprocal(cols[:, 41:42], cols[:, 41:42]),
                    lambda e, ts=ts, y0t=y0t: e.tensor_scalar(
                        ab(B_Y1 + ts * 2048, 2048), y0t, cols[:, 40:41],
                        cols[:, 41:42], ALU.subtract, ALU.mult),
                ])

                prog.wait("vector", dSp, dSp.n)
                vn = prog.vchain(cV, [
                    lambda e, y0t=y0t: e.scalar_tensor_tensor(
                        af(FB_SCR, 2048), y0t, cols[:, 40:41], nwbb[:, 0:2048],
                        ALU.subtract, ALU.mult),
                    lambda e: e.reduce_max(
                        cols[:, 44:45], af(FB_SCR, 2048), axis=AX.X,
                        apply_absolute_value=True),
                    lambda e: e.tensor_tensor(
                        cols[:, 43:44], cols[:, 41:42], cols[:, 41:42], ALU.mult),
                    lambda e: e.tensor_scalar(
                        cols[:, 43:44], cols[:, 43:44], -EPS, 1.0 + EPS,
                        ALU.mult, ALU.add),
                ])
                prog.wait("scalar", cV, vn)
                sn = prog.op("scalar", lambda e: e.activation(
                    cols[:, 43:44], cols[:, 43:44], AF.Sqrt), cS)
                prog.wait("vector", cS, sn)
                vn = prog.vchain(cV, [
                    lambda e: e.reciprocal(cols[:, 43:44], cols[:, 43:44]),
                    lambda e: e.tensor_tensor(
                        cols[:, 42:43], cols[:, 41:42], cols[:, 43:44], ALU.mult),
                    lambda e: e.tensor_tensor(
                        cols[:, 45:46], cols[:, 42:43], cols[:, 44:45], ALU.mult),
                    lambda e: e.tensor_scalar(
                        cols[:, 45:46], cols[:, 45:46], 1e-5, None, ALU.max),
                    lambda e: e.reciprocal(cols[:, 46:47], cols[:, 45:46]),
                    lambda e: e.tensor_scalar(
                        cols[:, 46:47], cols[:, 46:47], 127.0, None, ALU.mult),
                    lambda e: e.tensor_tensor(
                        cols[:, 46:47], cols[:, 46:47], cols[:, 42:43], ALU.mult),
                    lambda e, ts=ts: e.tensor_scalar(
                        cols[:, 48 + ts:49 + ts], cols[:, 45:46], 1.0 / 127.0,
                        None, ALU.mult),
                    lambda e, ts=ts: e.tensor_tensor(
                        cols[:, 48 + ts:49 + ts], cols[:, 48 + ts:49 + ts],
                        cols[:, C_MW1:C_MW1 + 1], ALU.mult),
                    lambda e: e.tensor_scalar(
                        ai(0, 2048), af(FB_SCR, 2048), cols[:, 46:47], None,
                        ALU.mult),
                    lambda e: e.tensor_copy(ab(B_XQ, 2048), ai(0, 2048)),
                ])
                prog.wait("sync", cV, vn)
                prog.dma("sync", xqs[ts * 128:(ts + 1) * 128, :], ab(B_XQ, 2048),
                         dSp)
            xq_sp = dSp.n
            if DEBUG and bc == 0:
                prog.wait("gpsimd", cV, cV.n)
                prog.dma("gpsimd", dbg_y0, ab(B_Y1, 8192), dOut)
                prog.dma("gpsimd", dbg_y1, ab(B_Y1, 8192), dOut)

            prog.wait("sync", cT, cT.n)
            prog.wait("sync", dSp, xq_sp)
            xqt_n = 0
            for kd in range(NKD):
                xqt_n = prog.dma("sync", ab(B_XQT + kd * 512, 512),
                                 xqs[0:TB, kd * 128:(kd + 1) * 128], dLdS,
                                 transpose=True)

            PU = [P[4], P[5], P[6], PM]
            for jp in range(NJP):
                prog.wait("scalar", cV, cV.n)
                gu_n = 0
                for kd in range(NKD):
                    prog.dma("scalar", ab(B_G + kd * 512, 512),
                             wqgu[128 * kd:128 * (kd + 1),
                                  jp * 512:(jp + 1) * 512], dLdC)
                    gu_n = prog.dma("scalar", ab(B_U + kd * 512, 512),
                                    wqgu[128 * kd:128 * (kd + 1),
                                         F + jp * 512:F + (jp + 1) * 512], dLdC)
                prog.wait("tensor", dLdC, gu_n)
                prog.wait("tensor", dLdS, xqt_n)
                prog.wait("tensor", cV, cV.n)
                prog.wait("tensor", cS, cS.n)
                tn = None
                for kd in range(NKD):
                    for ts in range(NTS):
                        prog.op("tensor", lambda e, kd=kd, ts=ts: e.matmul(
                            P[ts][:, 0:512],
                            ab(B_XQT + kd * 512 + ts * 128, 128),
                            ab(B_G + kd * 512, 512),
                            start=(kd == 0), stop=(kd == NKD - 1)))
                        def fnu(e, kd=kd, ts=ts):
                            return e.matmul(
                                PU[ts][:, 0:512],
                                ab(B_XQT + kd * 512 + ts * 128, 128),
                                ab(B_U + kd * 512, 512),
                                start=(kd == 0), stop=(kd == NKD - 1))
                        if kd == NKD - 1 and ts == NTS - 1:
                            tn = prog.op("tensor", fnu, cT)
                        else:
                            prog.op("tensor", fnu)
                prog.wait("scalar", cT, tn)
                prog.wait("vector", cT, tn)
                prog.wait("vector", dSp, dSp.n)
                for ts in range(NTS):
                    hb = FB_HID + ts * 512
                    sn = prog.op("scalar", lambda e, ts=ts: e.activation(
                        af(FB_GR, 512), P[ts][:, 0:512], AF.Relu), cS)
                    prog.wait("vector", cS, sn)
                    prog.op("vector", lambda e, ts=ts: e.tensor_tensor(
                        af(FB_T1, 512), af(FB_GR, 512), PU[ts][:, 0:512],
                        ALU.mult))
                    vn = prog.op("vector", lambda e, hb=hb: e.tensor_tensor(
                        af(hb, 512), af(FB_T1, 512), af(FB_GR, 512), ALU.mult), cV)
                    prog.wait("vector", cV, cV.n)
                    vn2 = prog.op("vector", lambda e, hb=hb: e.reduce_max(
                        cols[:, 56:57], af(hb, 512), axis=AX.X,
                        apply_absolute_value=True), cV)
                    prog.wait("vector", cV, vn2)
                    vn2 = prog.op("vector", lambda e, ts=ts: e.tensor_tensor(
                        cols[:, 52 + ts:53 + ts], cols[:, 52 + ts:53 + ts],
                        cols[:, 56:57], ALU.max), cV)
                    prog.wait("vector", cV, vn2)
                    prog.wait("gpsimd", cV, vn)
                    prog.dma("gpsimd", hids[ts * 128:(ts + 1) * 128,
                                            jp * 512:(jp + 1) * 512],
                             af(hb, 512), dSp)
                    prog.wait("scalar", cV, vn)
            hid_sp = dSp.n

            vn = prog.vchain(cV, [
                lambda e: e.tensor_tensor(
                    cols[:, 66:70], cols[:, 48:52], cols[:, 48:52], ALU.mult),
                lambda e: e.tensor_tensor(
                    cols[:, 66:70], cols[:, 66:70], cols[:, 48:52], ALU.mult),
                lambda e: e.tensor_tensor(
                    cols[:, 52:56], cols[:, 52:56], cols[:, 66:70], ALU.mult),
                lambda e: e.tensor_scalar(
                    cols[:, 52:56], cols[:, 52:56], 1e-5, None, ALU.max),
                lambda e: e.reciprocal(cols[:, 57:61], cols[:, 52:56]),
                lambda e: e.tensor_scalar(
                    cols[:, 57:61], cols[:, 57:61], 127.0, None, ALU.mult),
                lambda e: e.tensor_tensor(
                    cols[:, 57:61], cols[:, 57:61], cols[:, 66:70], ALU.mult),
                lambda e: e.tensor_scalar(
                    cols[:, 62:66], cols[:, 52:56], 1.0 / 127.0, None, ALU.mult),
                lambda e: e.tensor_scalar(
                    cols[:, 62:66], cols[:, 62:66], cols[:, C_MW2:C_MW2 + 1],
                    None, ALU.mult),
            ])

            prog.wait("sync", dSp, hid_sp)
            prog.wait("sync", cV, vn)
            hq_pieces = [(0, 2048), (2048, 2048), (4096, 1536)]
            for ts in range(NTS):
                for (c0, wd) in hq_pieces:
                    prog.wait("sync", dSp, dSp.n)
                    n = prog.dma("sync", af(FB_SCR, wd),
                                 hids[ts * 128:(ts + 1) * 128, c0:c0 + wd], dLdS)
                    prog.wait("vector", dLdS, n)
                    prog.op("vector", lambda e, ts=ts, wd=wd: e.tensor_scalar(
                        ai(0, wd), af(FB_SCR, wd),
                        cols[:, 57 + ts:58 + ts], None, ALU.mult))
                    vn = prog.op("vector", lambda e, wd=wd: e.tensor_copy(
                        ab(B_HQST, wd), ai(0, wd)), cV)
                    prog.wait("sync", cV, vn)
                    prog.dma("sync", hqs[ts * 128:(ts + 1) * 128, c0:c0 + wd],
                             ab(B_HQST, wd), dSp)
            hq_sp = dSp.n

            prog.wait("sync", cT, cT.n)
            prog.wait("sync", dSp, hq_sp)
            hqt_n = 0
            for kf in range(NKF):
                hqt_n = prog.dma("sync", ab(B_HQT + kf * 512, 512),
                                 hqs[0:TB, kf * 128:(kf + 1) * 128], dLdS,
                                 transpose=True)

            for osl in range(NOSL):
                prog.wait("tensor", dLdS, hqt_n)
                prog.wait("tensor", cV, cV.n)
                prog.wait("tensor", cS, cS.n)
                ct_base = cT.n
                dn_n = []
                for kf in range(NKF):
                    ring = B_DNRING + (kf % 4) * 512
                    if kf >= 4:
                        prog.wait("scalar", cT, ct_base + (kf - 3))
                    else:
                        prog.wait("scalar", cT, ct_base)
                    dn_n.append(prog.dma("scalar", ab(ring, 512),
                                         wqdn[128 * kf:128 * (kf + 1),
                                              osl * 512:(osl + 1) * 512], dLdC))
                    prog.wait("tensor", dLdC, dn_n[-1])
                    for ts in range(NTS):
                        def fnd(e, kf=kf, ts=ts, ring=ring):
                            return e.matmul(
                                P[ts][:, 0:512],
                                ab(B_HQT + kf * 512 + ts * 128, 128),
                                ab(ring, 512),
                                start=(kf == 0), stop=(kf == NKF - 1))
                        if ts == NTS - 1:
                            tn = prog.op("tensor", fnd, cT)
                        else:
                            prog.op("tensor", fnd)
                prog.wait("vector", cT, tn)
                for ts in range(NTS):
                    vn = prog.op("vector", lambda e, ts=ts, osl=osl: e.scalar_tensor_tensor(
                        af(FB_Y0F + ts * 2048 + osl * 512, 512), P[ts][:, 0:512],
                        cols[:, 62 + ts:63 + ts],
                        ab(B_Y1 + ts * 2048 + osl * 512, 512),
                        ALU.mult, ALU.add), cV)

            for ts in range(NTS):
                y2t = af(FB_Y0F + ts * 2048, 2048)
                vn = prog.vchain(cV, [
                    lambda e, y2t=y2t: e.reduce_sum(cols[:, 40:41], y2t, axis=AX.X),
                    lambda e: e.tensor_scalar(
                        cols[:, 40:41], cols[:, 40:41], 1.0 / D, None, ALU.mult),
                ])
                prog.wait("scalar", cV, vn)
                sn = prog.op("scalar", lambda e, y2t=y2t: e.activation(
                    af(FB_SCR, 2048), y2t, AF.Square,
                    accum_out=cols[:, 41:42]), cS)
                prog.wait("vector", cS, sn)
                vn = prog.vchain(cV, [
                    lambda e: e.tensor_tensor(
                        cols[:, 42:43], cols[:, 40:41], cols[:, 40:41], ALU.mult),
                    lambda e: e.tensor_scalar(
                        cols[:, 41:42], cols[:, 41:42], 1.0 / D, None, ALU.mult),
                    lambda e: e.tensor_tensor(
                        cols[:, 41:42], cols[:, 41:42], cols[:, 42:43], ALU.subtract),
                    lambda e: e.tensor_scalar(
                        cols[:, 41:42], cols[:, 41:42], 1.0, EPS, ALU.mult, ALU.add),
                ])
                prog.wait("scalar", cV, vn)
                sn = prog.op("scalar", lambda e: e.activation(
                    cols[:, 41:42], cols[:, 41:42], AF.Sqrt), cS)
                prog.wait("vector", cS, sn)
                vn = prog.vchain(cV, [
                    lambda e: e.reciprocal(cols[:, 41:42], cols[:, 41:42]),
                ])
                prog.wait("vector", dOut, dOut.n)
                vn = prog.op("vector", lambda e, y2t=y2t: e.tensor_scalar(
                    af(FB_OUT, 2048), y2t, cols[:, 40:41], cols[:, 41:42],
                    ALU.subtract, ALU.mult), cV)
                prog.wait("sync", cV, vn)
                prog.dma("sync", out[t0 + ts * 128:t0 + ts * 128 + 128, :],
                         af(FB_OUT, 2048), dOut)
            vn = prog.op("vector", lambda e: e.memset(cols[:, 52:56], 0.0), cV)

        if DEBUG:
            prog.wait("gpsimd", cV, cV.n)
            prog.dma("gpsimd", dbg_y2p, af(FB_Y0F, 8192), dOut)
            prog.dma("gpsimd", dbg_y1l, ab(B_Y1, 8192), dOut)
            prog.dma("gpsimd", dbg_hqt, ab(B_HQT, 22528), dOut)
            prog.dma("gpsimd", dbg_h2, h2row[:], dOut)
            prog.dma("gpsimd", dbg_cols, cols[:], dOut)
            prog.dma("gpsimd", dbg_vals, vals[:], dOut)
            prog.dma("gpsimd", dbg_carry, carry[:], dOut)
            prog.dma("gpsimd", dbg_svc, svc[:], dOut)
        prog.wait("gpsimd", dOut, dOut.n)

        with nc.Block() as block:
            @block.sync
            def _(e):
                for fn in prog.q["sync"]:
                    fn(e)

            @block.scalar
            def _(e):
                for fn in prog.q["scalar"]:
                    fn(e)

            @block.vector
            def _(e):
                for fn in prog.q["vector"]:
                    fn(e)

            @block.tensor
            def _(e):
                for fn in prog.q["tensor"]:
                    fn(e)

            @block.gpsimd
            def _(e):
                for fn in prog.q["gpsimd"]:
                    fn(e)

    return nc


_NC_CACHE = None


def kernel(**inputs):
    global _NC_CACHE
    x = np.asarray(inputs["L0_attn_hidden_states"], dtype=np.float32)
    w_i = np.ascontiguousarray(np.asarray(inputs["L0_attn_w_i"], dtype=np.float32))
    w_f = np.ascontiguousarray(np.asarray(inputs["L0_attn_w_f"], dtype=np.float32))
    w_g = np.ascontiguousarray(np.asarray(inputs["L0_attn_w_g"], dtype=np.float32))
    w_oT = np.ascontiguousarray(np.asarray(inputs["L0_attn_w_o"], dtype=np.float32).T)
    s_i = np.asarray(inputs["L0_attn_s_i"], dtype=np.float32)
    s_f = np.asarray(inputs["L0_attn_s_f"], dtype=np.float32)
    s_g = np.asarray(inputs["L0_attn_s_g"], dtype=np.float32)
    s_o = np.asarray(inputs["L0_attn_s_o"], dtype=np.float32)
    g_norm = np.asarray(inputs["L0_attn_g_norm_w"], dtype=np.float32)
    w_guT = np.ascontiguousarray(np.asarray(inputs["L0_ffn_w_gate_up"], dtype=np.float32).T)
    w_dnT = np.ascontiguousarray(np.asarray(inputs["L0_ffn_w_down"], dtype=np.float32).T)
    norm_w = np.asarray(inputs["L0_ffn_norm_weight"], dtype=np.float32)

    if _NC_CACHE is None:
        _NC_CACHE = build_nc()
    nc = _NC_CACHE

    eye = np.eye(128, dtype=np.float32)
    nwb = np.ascontiguousarray(np.broadcast_to(norm_w[None, :], (128, D)))
    svs = {
        "sv_i": np.ascontiguousarray(s_i.reshape(16, 128)),
        "sv_f": np.ascontiguousarray(s_f.reshape(16, 128)),
        "sv_g": np.ascontiguousarray(s_g.reshape(16, 128)),
        "sv_gn": np.ascontiguousarray(g_norm.reshape(16, 128)),
        "sv_o": np.ascontiguousarray(s_o.reshape(16, 128)),
    }

    in_maps = []
    for c in range(8):
        b, s = c // 2, c % 2
        if s == 0:
            xTc = np.ascontiguousarray(
                np.concatenate([x[b, 0:2048].T, x[b, 0:2048].T], axis=1))
            mk = np.zeros((16, 128), np.float32)
        else:
            xTc = np.ascontiguousarray(x[b].T)
            mk = np.ones((16, 128), np.float32)
        in_maps.append(dict(
            xT=xTc,
            xnat=np.ascontiguousarray(x[b, s * 2048:(s + 1) * 2048]),
            w_i=w_i, w_f=w_f, w_g=w_g, w_oT=w_oT, w_guT=w_guT, w_dnT=w_dnT,
            nwb=nwb, eye=eye, sv_mk=np.ascontiguousarray(mk), **svs,
        ))

    res = run_bass_kernel_spmd(nc, in_maps, core_ids=list(range(8)))
    outp = np.empty((B, T, D), dtype=np.float32)
    for c in range(8):
        b, s = c // 2, c % 2
        outp[b, s * 2048:(s + 1) * 2048] = res.results[c]["out"]
    if DEBUG:
        return outp, res.results
    return outp


# version 05
# speedup vs baseline: 1.2703x; 1.1414x over previous
from contextlib import ExitStack

import numpy as np

import concourse.bass as bass
import concourse.mybir as mybir
from concourse.bass_utils import run_bass_kernel_spmd

F32 = mybir.dt.float32
BF16 = mybir.dt.bfloat16
I32 = mybir.dt.int32
ALU = mybir.AluOpType
AF = mybir.ActivationFunctionType
AX = mybir.AxisListType

B, T, D = 4, 4096, 2048
F = 5632
F2 = 2 * F
TPC = 2048
NKD = D // 128
ESL = 512
NES = D // ESL
TC = 512
NTCA = T // TC
NPRE = 4
TB = 512
NBC = TPC // TB
NTS = TB // 128
NOSL = D // 512
NJP = F // 512
NKF = F // 128
EPS = 1e-5
NG = F2 * D
ND = F * D

ENGINES = ("sync", "scalar", "vector", "tensor", "gpsimd")
DEBUG = False


class Sem:
    def __init__(self, handle):
        self.h = handle
        self.n = 0

    def inc(self, k):
        self.n += k
        return self.n


class Prog:
    def __init__(self):
        self.q = {e: [] for e in ENGINES}

    def on(self, eng, fn):
        self.q[eng].append(fn)

    def dma(self, eng, out, in_, sem, transpose=False):
        def fn(e, out=out, in_=in_, sem=sem, tr=transpose):
            if tr:
                e.dma_start_transpose(out, in_).then_inc(sem.h, 16)
            else:
                e.dma_start(out=out, in_=in_).then_inc(sem.h, 16)
        self.on(eng, fn)
        return sem.inc(16)

    def wait(self, eng, sem, val=None):
        v = sem.n if val is None else val
        if v <= 0:
            return
        self.on(eng, lambda e, h=sem.h, v=v: e.wait_ge(h, v))

    def vchain(self, sem, ops):
        self.wait("vector", sem, sem.n)
        vn = None
        for f in ops:
            vn = self.op("vector", f, sem)
            self.wait("vector", sem, vn)
        return vn

    def op(self, eng, fn, sem=None):
        if sem is None:
            self.on(eng, fn)
            return None
        def wrapped(e, fn=fn, sem=sem):
            fn(e).then_inc(sem.h, 1)
        self.on(eng, wrapped)
        return sem.inc(1)


def build_nc():
    nc = bass.Bass("TRN2", target_bir_lowering=False, debug=False,
                   detect_race_conditions=False)

    xT = nc.dram_tensor("xT", [D, T], F32, kind="ExternalInput").ap()
    xnat = nc.dram_tensor("xnat", [TPC, D], F32, kind="ExternalInput").ap()
    w_i = nc.dram_tensor("w_i", [D, D], F32, kind="ExternalInput").ap()
    w_f = nc.dram_tensor("w_f", [D, D], F32, kind="ExternalInput").ap()
    w_g = nc.dram_tensor("w_g", [D, D], F32, kind="ExternalInput").ap()
    w_oT = nc.dram_tensor("w_oT", [D, D], F32, kind="ExternalInput").ap()
    w_guT = nc.dram_tensor("w_guT", [D, F2], F32, kind="ExternalInput").ap()
    w_dnT = nc.dram_tensor("w_dnT", [F, D], F32, kind="ExternalInput").ap()
    sv_in = {}
    for nm in ("sv_i", "sv_f", "sv_g", "sv_gn", "sv_o", "sv_mk"):
        sv_in[nm] = nc.dram_tensor(nm, [16, 128], F32, kind="ExternalInput").ap()
    nwb_in = nc.dram_tensor("nwb", [128, D], F32, kind="ExternalInput").ap()
    eye_in = nc.dram_tensor("eye", [128, 128], F32, kind="ExternalInput").ap()
    out = nc.dram_tensor("out", [TPC, D], F32, kind="ExternalOutput").ap()

    KD = "ExternalOutput" if DEBUG else "Internal"
    x256s = nc.dram_tensor("x256s", [D, T], BF16, kind=KD).ap()
    oacts = nc.dram_tensor("oacts", [D, TPC], BF16, kind=KD).ap()
    woTb = nc.dram_tensor("woTb", [D, D], BF16, kind="Internal").ap()
    wqgu = nc.dram_tensor("wqgu", [D, F2], BF16, kind=KD).ap()
    wqdn = nc.dram_tensor("wqdn", [F, D], BF16, kind=KD).ap()
    xqs = nc.dram_tensor("xqs", [TB, D], BF16, kind=KD).ap()
    hids = nc.dram_tensor("hids", [TB, F], F32, kind=KD).ap()
    hqs = nc.dram_tensor("hqs", [TB, F], BF16, kind=KD).ap()
    if DEBUG:
        dbg_h2 = nc.dram_tensor("dbg_h2", [1, TPC], F32, kind="ExternalOutput").ap()
        dbg_cols = nc.dram_tensor("dbg_cols", [128, 96], F32, kind="ExternalOutput").ap()
        dbg_vals = nc.dram_tensor("dbg_vals", [1, 16], F32, kind="ExternalOutput").ap()
        dbg_carry = nc.dram_tensor("dbg_carry", [128, 16], F32, kind="ExternalOutput").ap()
        dbg_svc = nc.dram_tensor("dbg_svc", [128, 128], F32, kind="ExternalOutput").ap()
        dbg_y0 = nc.dram_tensor("dbg_y0", [128, 8192], BF16, kind="ExternalOutput").ap()
        dbg_y1 = nc.dram_tensor("dbg_y1", [128, 8192], BF16, kind="ExternalOutput").ap()
        dbg_y2p = nc.dram_tensor("dbg_y2p", [128, 8192], F32, kind="ExternalOutput").ap()
        dbg_y1l = nc.dram_tensor("dbg_y1l", [128, 8192], BF16, kind="ExternalOutput").ap()
        dbg_hqt = nc.dram_tensor("dbg_hqt", [128, 22528], BF16, kind="ExternalOutput").ap()

    prog = Prog()

    with ExitStack() as es_:
        ctx = es_.enter_context
        AB = ctx(nc.sbuf_tensor("AB", [128, 47104], BF16))
        AFr = ctx(nc.sbuf_tensor("AFr", [128, 18432], F32))
        AI = ctx(nc.sbuf_tensor("AI", [128, 4096], I32))
        svc = ctx(nc.sbuf_tensor("svc", [128, 128], F32))
        carry = ctx(nc.sbuf_tensor("carry", [128, 16], F32))
        nwbb = ctx(nc.sbuf_tensor("nwbb", [128, D], F32))
        eyef = ctx(nc.sbuf_tensor("eyef", [128, 128], F32))
        ones_col = ctx(nc.sbuf_tensor("ones_col", [128, 1], F32))
        ones_row = ctx(nc.sbuf_tensor("ones_row", [1, 128], F32))
        h2row = ctx(nc.sbuf_tensor("h2row", [1, TPC], F32))
        cols = ctx(nc.sbuf_tensor("cols", [128, 96], F32))
        vals = ctx(nc.sbuf_tensor("vals", [1, 16], F32))
        cols2 = ctx(nc.sbuf_tensor("cols2", [128, 160], F32))
        svraw = ctx(nc.sbuf_tensor("svraw", [16, 128], F32))

        P = [ctx(nc.psum_tensor(f"p{i}", [128, 512], F32)) for i in range(7)]
        PM = ctx(nc.psum_tensor("pm", [128, 512], F32))

        dLdS = Sem(ctx(nc.semaphore("dLdS")))
        dLdC = Sem(ctx(nc.semaphore("dLdC")))
        dLdG = Sem(ctx(nc.semaphore("dLdG")))
        dSp = Sem(ctx(nc.semaphore("dSp")))
        dOut = Sem(ctx(nc.semaphore("dOut")))
        cV = Sem(ctx(nc.semaphore("cV")))
        cS = Sem(ctx(nc.semaphore("cS")))
        cT = Sem(ctx(nc.semaphore("cT")))

        A_WSL = {"i": 0, "f": 8192, "g": 16384}
        A_XC = 24576
        A_SGB = 32768
        A_OAST = 34816
        B_Y1 = 0
        B_Y0 = 8192
        B_WOT = 16384
        B_XQT = 16384
        B_OACT = 24576
        B_G = 24576
        B_HQT = 24576
        B_U = 34816
        B_XQ = 8192
        B_HQST = 10240
        B_DNRING = 12288
        FA_IB = 0
        FA_HB = 2048
        FA_H2 = 4096
        FA_TMP = 4608
        FA_FB = 5120
        FB_XN = 0
        FB_GR = 2048
        FB_T1 = 2560
        FB_SCR = 3072
        FB_OUT = 5120
        FB_HID = 7168
        FB_Y0F = 10240

        def ab(off, n):
            return AB[:, off:off + n]

        def af(off, n):
            return AFr[:, off:off + n]

        def ai(off, n):
            return AI[:, off:off + n]

        prog.op("vector", lambda e: e.memset(ones_col[:], 1.0))
        prog.op("vector", lambda e: e.memset(ones_row[:], 1.0))
        prog.op("vector", lambda e: e.memset(h2row[:], 0.0))
        prog.op("vector", lambda e: e.memset(carry[:], 0.0))
        vset = prog.op("vector", lambda e: e.memset(cols[:], 0.0), cV)

        n = prog.dma("sync", eyef[:], eye_in, dLdS)
        prog.wait("tensor", dLdS, n)
        prog.dma("sync", nwbb[:], nwb_in, dLdS)

        sv_order = [("sv_i", 0), ("sv_f", 1), ("sv_g", 2), ("sv_gn", 3),
                    ("sv_o", 4), ("sv_mk", 5)]
        for nm, idx in sv_order:
            n = prog.dma("sync", svraw[:], sv_in[nm], dLdS)
            prog.wait("tensor", dLdS, n)
            tn = prog.op("tensor", lambda e: e.transpose(
                PM[:, 0:16], svraw[0:16, :], eyef[0:16, 0:16]), cT)
            prog.wait("vector", cT, tn)
            vn = prog.op("vector", lambda e, idx=idx: e.tensor_copy(
                svc[:, idx * 16:(idx + 1) * 16], PM[:, 0:16]), cV)
            prog.wait("tensor", cV, vn)
            prog.wait("sync", cT, tn)
        SV_SI256, SV_SF256, SV_SG256, SV_GNSO, SV_MASK = 96 - 16, 96, 32, 48, 80
        vn = prog.vchain(cV, [
            lambda e: e.tensor_scalar(
                svc[:, 0:16], svc[:, 0:16], 1.0 / 256.0, None, ALU.mult),
            lambda e: e.tensor_scalar(
                svc[:, 16:32], svc[:, 16:32], 1.0 / 256.0, None, ALU.mult),
            lambda e: e.tensor_scalar(
                svc[:, 32:48], svc[:, 32:48], 1.0 / 256.0, None, ALU.mult),
            lambda e: e.tensor_tensor(
                svc[:, 48:64], svc[:, 48:64], svc[:, 64:80], ALU.mult),
        ])
        SV_SI256, SV_SF256, SV_SG256, SV_GNSO, SV_MASK = 0, 16, 32, 48, 80

        for k in range(NKD):
            buf = A_XC + (k % 2) * 2048
            if k >= 2:
                prog.wait("gpsimd", dSp, 16 * (k - 1))
            prog.dma("gpsimd", ab(buf, 2048), w_oT[128 * k:128 * (k + 1), :], dLdG)
            prog.wait("gpsimd", dLdG)
            prog.dma("gpsimd", woTb[128 * k:128 * (k + 1), :], ab(buf, 2048), dSp)
        woT_done = dSp.n

        m = 0
        for k in range(NKD):
            for h in range(2):
                rs = slice(128 * k, 128 * (k + 1))
                cs = slice(2048 * h, 2048 * (h + 1))
                fo = FA_IB + (m % 2) * 2048
                io_ = (m % 2) * 2048
                bo = A_WSL["i"] + (m % 2) * 2048
                if m >= 2:
                    prog.wait("sync", dSp, woT_done + 16 * (m - 1))
                n = prog.dma("sync", af(fo, 2048), xT[rs, cs], dLdS)
                prog.wait("vector", dLdS, n)
                prog.op("vector", lambda e, fo=fo, io_=io_: e.tensor_scalar(
                    ai(io_, 2048), af(fo, 2048), 256.0, None, ALU.mult))
                vn = prog.op("vector", lambda e, io_=io_, bo=bo: e.tensor_copy(
                    ab(bo, 2048), ai(io_, 2048)), cV)
                prog.wait("sync", cV, vn)
                prog.dma("sync", x256s[rs, cs], ab(bo, 2048), dSp)
                m += 1
        x256_done = dSp.n

        def wtiles():
            for k in range(NKD):
                for j in range(5):
                    yield (w_guT, wqgu, 128 * k, 2048 * j, 2048, 0)
                yield (w_guT, wqgu, 128 * k, 10240, 1024, 0)
            for k in range(F // 128):
                yield (w_dnT, wqdn, 128 * k, 0, 2048, 1)

        prog.wait("scalar", cV, vset)
        tcount = 0
        sn_hist = []
        NT_GU = 96
        for (wt, _, r0, c0, wd, which) in wtiles():
            fo = (tcount % 2) * 2048
            if tcount >= 2:
                prog.wait("sync", cS, sn_hist[tcount - 2])
            n = prog.dma("sync", af(fo, wd), wt[r0:r0 + 128, c0:c0 + wd], dLdS)
            prog.wait("scalar", dLdS, n)
            sn = prog.op("scalar", lambda e, fo=fo, wd=wd, t=tcount: e.activation(
                af(4096 + (t % 2) * 2048, wd), af(fo, wd), AF.Abs,
                accum_out=cols2[:, t:t + 1]), cS)
            sn_hist.append(sn)
            tcount += 1
        prog.wait("vector", cS, cS.n)
        vn = prog.vchain(cV, [
            lambda e: e.reduce_sum(cols[:, 32:33], cols2[:, 0:NT_GU], axis=AX.X),
            lambda e: e.reduce_sum(cols[:, 33:34], cols2[:, NT_GU:tcount], axis=AX.X),
        ])
        prog.wait("tensor", cV, cV.n)
        tn = prog.op("tensor", lambda e: e.matmul(
            PM[0:1, 0:1], cols[:, 32:33], ones_col[:], start=True, stop=True), cT)
        tn = prog.op("tensor", lambda e: e.matmul(
            PM[0:1, 1:2], cols[:, 33:34], ones_col[:], start=False, stop=True,
            skip_group_check=True), cT)
        prog.wait("vector", cT, tn)
        vn = prog.vchain(cV, [
            lambda e: e.tensor_copy(vals[0:1, 0:2], PM[0:1, 0:2]),
        ])
        prog.wait("tensor", cV, vn)
        tn = prog.op("tensor", lambda e: e.matmul(
            PM[:, 16:18], ones_row[:], vals[0:1, 0:2], start=True, stop=True,
            skip_group_check=True), cT)
        prog.wait("vector", cT, tn)
        vn = prog.vchain(cV, [
            lambda e: e.tensor_copy(cols[:, 70:72], PM[:, 16:18]),
            lambda e: e.tensor_scalar(
                cols[:, 36:37], cols[:, 70:71], 1.0 / NG, 1e-5, ALU.mult, ALU.max),
            lambda e: e.tensor_scalar(
                cols[:, 37:38], cols[:, 71:72], 1.0 / ND, 1e-5, ALU.mult, ALU.max),
            lambda e: e.reciprocal(cols[:, 38:40], cols[:, 36:38]),
        ])
        C_MW1, C_MW2, C_SW1, C_SW2 = 36, 37, 38, 39
        prog.wait("tensor", cV, vn)

        tcount = 0
        vn_hist = []
        for (wt, dst, r0, c0, wd, which) in wtiles():
            fo = (tcount % 2) * 2048
            io_ = (tcount % 2) * 2048
            bo = A_WSL["g"] + (tcount % 2) * 2048
            if tcount >= 2:
                prog.wait("sync", cV, vn_hist[tcount - 2])
            n = prog.dma("sync", af(fo, wd), wt[r0:r0 + 128, c0:c0 + wd], dLdS)
            if tcount >= 2:
                prog.wait("vector", dSp, x256_done + 16 * (tcount - 1))
            prog.wait("vector", dLdS, n)
            swcol = cols[:, C_SW1 + which:C_SW1 + which + 1]
            prog.op("vector", lambda e, fo=fo, io_=io_, wd=wd, sw=swcol: e.tensor_scalar(
                ai(io_, wd), af(fo, wd), sw, None, ALU.mult))
            vn = prog.op("vector", lambda e, io_=io_, bo=bo, wd=wd: e.tensor_scalar(
                ab(bo, wd), ai(io_, wd), 1.0, -1.0, ALU.min, ALU.max), cV)
            vn_hist.append(vn)
            prog.wait("scalar", cV, vn)
            prog.dma("scalar", dst[r0:r0 + 128, c0:c0 + wd], ab(bo, wd), dSp)
            tcount += 1

        prog.wait("gpsimd", dSp, dSp.n)
        prog.wait("gpsimd", cV, cV.n)

        for es in range(NES):
            prog.wait("gpsimd", cT, cT.n)
            for k in range(NKD):
                for wname, wap in (("i", w_i), ("f", w_f), ("g", w_g)):
                    prog.dma("gpsimd", ab(A_WSL[wname] + k * 512, 512),
                             wap[128 * k:128 * (k + 1), es * ESL:(es + 1) * ESL],
                             dLdG)
            wsl_ready = dLdG.n

            for tc in range(NTCA):
                stored = tc >= NPRE
                tcol = tc * TC
                prog.wait("sync", cT, cT.n)
                xn = 0
                for k in range(NKD):
                    xn = prog.dma("sync", ab(A_XC + k * 512, 512),
                                  x256s[128 * k:128 * (k + 1), tcol:tcol + TC],
                                  dLdS)
                prog.wait("tensor", dLdS, xn)
                prog.wait("tensor", dLdG, wsl_ready)
                prog.wait("tensor", cV, cV.n)
                prog.wait("tensor", cS, cS.n)

                def mm_four(woff):
                    last = None
                    for et in range(4):
                        for kd in range(NKD):
                            def fn(e, et=et, kd=kd, woff=woff):
                                return e.matmul(
                                    P[et][:, 0:TC],
                                    ab(woff + kd * 512 + et * 128, 128),
                                    ab(A_XC + kd * 512, 512),
                                    start=(kd == 0), stop=(kd == NKD - 1))
                            if kd == NKD - 1:
                                last = prog.op("tensor", fn, cT)
                            else:
                                prog.op("tensor", fn)
                    return last

                tn = mm_four(A_WSL["f"])
                prog.wait("scalar", cT, tn)
                for et in range(4):
                    k = es * 4 + et
                    prog.op("scalar", lambda e, et=et, k=k: e.activation(
                        af(FA_FB + et * 512, 512), P[et][:, 0:TC], AF.Sigmoid,
                        scale=svc[:, SV_SF256 + k:SV_SF256 + k + 1]), cS)
                prog.wait("tensor", cS, cS.n)
                tn = mm_four(A_WSL["i"])
                prog.wait("vector", cT, tn)
                prog.wait("vector", cV, cV.n)
                for et in range(4):
                    k = es * 4 + et
                    prog.op("vector", lambda e, et=et, k=k: e.tensor_scalar(
                        af(FA_IB + et * 512, 512), P[et][:, 0:TC],
                        svc[:, SV_SI256 + k:SV_SI256 + k + 1], None, ALU.mult))
                    prog.op("vector", lambda e, et=et: e.tensor_tensor(
                        af(FA_TMP, 512), af(FA_FB + et * 512, 512),
                        af(FA_IB + et * 512, 512), ALU.mult))
                    prog.op("vector", lambda e, et=et: e.tensor_tensor(
                        af(FA_IB + et * 512, 512), af(FA_IB + et * 512, 512),
                        af(FA_TMP, 512), ALU.subtract))
                    vn = prog.op("vector", lambda e, et=et, k=k: e.tensor_tensor_scan(
                        af(FA_HB + et * 512, 512), af(FA_FB + et * 512, 512),
                        af(FA_IB + et * 512, 512), carry[:, k:k + 1],
                        ALU.mult, ALU.add), cV)
                    prog.wait("vector", cV, vn)
                    vn = prog.op("vector", lambda e, et=et, k=k: e.tensor_copy(
                        carry[:, k:k + 1], af(FA_HB + et * 512 + 511, 1)), cV)
                    prog.wait("vector", cV, vn)
                if tc == NPRE - 1:
                    vn = prog.op("vector", lambda e, es=es: e.tensor_tensor(
                        carry[:, es * 4:es * 4 + 4], carry[:, es * 4:es * 4 + 4],
                        svc[:, SV_MASK + es * 4:SV_MASK + es * 4 + 4],
                        ALU.mult), cV)
                    prog.wait("vector", cV, vn)
                if stored:
                    prog.wait("tensor", cV, cV.n)
                    tn = mm_four(A_WSL["g"])
                    prog.wait("scalar", cT, tn)
                    for et in range(4):
                        k = es * 4 + et
                        prog.op("scalar", lambda e, et=et, k=k: e.activation(
                            ab(A_SGB + et * 512, 512), P[et][:, 0:TC], AF.Silu,
                            scale=svc[:, SV_SG256 + k:SV_SG256 + k + 1]), cS)
                    prog.wait("vector", cS, cS.n)
                    prog.wait("vector", dSp, dSp.n)
                    stcol = tcol - NPRE * TC
                    for et in range(4):
                        k = es * 4 + et
                        prog.op("vector", lambda e, et=et, k=k: e.scalar_tensor_tensor(
                            ab(A_OAST + et * 512, 512), af(FA_HB + et * 512, 512),
                            svc[:, SV_GNSO + k:SV_GNSO + k + 1],
                            ab(A_SGB + et * 512, 512), ALU.mult, ALU.mult), cV)
                        sn = prog.op("scalar", lambda e, et=et: e.activation(
                            af(FA_H2, 512), af(FA_HB + et * 512, 512),
                            AF.Square), cS)
                        prog.wait("tensor", cS, sn)
                        tn = prog.op("tensor", lambda e, et=et: e.matmul(
                            PM[0:1, 0:TC], ones_col[:], af(FA_H2, 512),
                            start=(et == 0), stop=(et == 3),
                            skip_group_check=True), cT)
                        if et < 3:
                            prog.wait("scalar", cT, tn)
                    prog.wait("vector", cT, tn)
                    vn = prog.op("vector", lambda e, stcol=stcol: e.tensor_tensor(
                        h2row[0:1, stcol:stcol + TC], h2row[0:1, stcol:stcol + TC],
                        PM[0:1, 0:TC], ALU.add), cV)
                    prog.wait("gpsimd", cV, cV.n)
                    for et in range(4):
                        prog.dma("gpsimd",
                                 oacts[es * ESL + et * 128:es * ESL + et * 128 + 128,
                                       stcol:stcol + TC],
                                 ab(A_OAST + et * 512, 512), dSp)

        prog.wait("tensor", cV, cV.n)
        for j in range(TPC // 128):
            tn = prog.op("tensor", lambda e, j=j: e.transpose(
                PM[:, 0:1], h2row[0:1, 128 * j:128 * (j + 1)],
                eyef[0:1, 0:1]), cT)
            prog.wait("vector", cT, tn)
            vn = prog.op("vector", lambda e, j=j: e.tensor_copy(
                cols[:, j:j + 1], PM[:, 0:1]), cV)
            prog.wait("tensor", cV, vn)
        vn = prog.vchain(cV, [
            lambda e: e.tensor_scalar(
                cols[:, 0:16], cols[:, 0:16], 1.0 / D, EPS, ALU.mult, ALU.add),
        ])
        prog.wait("scalar", cV, vn)
        sn = prog.op("scalar", lambda e: e.activation(
            cols[:, 0:16], cols[:, 0:16], AF.Sqrt), cS)
        prog.wait("vector", cS, sn)
        vn = prog.vchain(cV, [
            lambda e: e.reciprocal(cols[:, 0:16], cols[:, 0:16]),
            lambda e: e.tensor_scalar(
                cols[:, 16:32], cols[:, 0:16], 256.0, None, ALU.mult),
        ])
        C_R256 = 16

        prog.wait("sync", cV, cV.n)
        prog.wait("sync", dSp, dSp.n)
        prog.wait("scalar", dSp, dSp.n)

        for bc in range(NBC):
            t0 = bc * TB
            prog.wait("sync", cT, cT.n)
            oact_n = 0
            for k in range(NKD):
                oact_n = prog.dma("sync", ab(B_OACT + k * 512, 512),
                                  oacts[128 * k:128 * (k + 1), t0:t0 + TB], dLdS)
            mm4_ticks = []
            mm4_evacs = []
            PSETS = [P[0:4], [P[4], P[5], P[6], PM]]
            MM4_DUAL = False
            for osl in range(NOSL):
                PS = PSETS[osl % 2] if MM4_DUAL else PSETS[0]
                prog.wait("sync", cV, cV.n)
                xn_n = 0
                for ts in range(NTS):
                    xn_n = prog.dma("sync", af(FB_XN + ts * 512, 512),
                                    xnat[t0 + ts * 128:t0 + ts * 128 + 128,
                                         osl * 512:(osl + 1) * 512], dLdS)
                prog.wait("tensor", dLdS, oact_n)
                if MM4_DUAL:
                    if osl < 2:
                        prog.wait("tensor", cV, cV.n)
                    else:
                        prog.wait("tensor", cV, mm4_evacs[osl - 2])
                else:
                    prog.wait("tensor", cV, cV.n)
                MM4_RING = False
                if MM4_RING:
                    for kd in range(NKD):
                        g = osl * NKD + kd
                        slot = B_WOT + (g % 4) * 512
                        if g >= 4:
                            prog.wait("scalar", cT, mm4_ticks[g - 4])
                        else:
                            prog.wait("scalar", cT, cT.n)
                        wn = prog.dma("scalar", ab(slot, 512),
                                      woTb[128 * kd:128 * (kd + 1),
                                           osl * 512:(osl + 1) * 512], dLdC)
                        prog.wait("tensor", dLdC, wn)
                        for ts in range(NTS):
                            def fn(e, kd=kd, ts=ts, slot=slot, PS=PS):
                                return e.matmul(
                                    PS[ts][:, 0:512],
                                    ab(B_OACT + kd * 512 + ts * 128, 128),
                                    ab(slot, 512),
                                    start=(kd == 0), stop=(kd == NKD - 1))
                            if ts == NTS - 1:
                                tn = prog.op("tensor", fn, cT)
                            else:
                                prog.op("tensor", fn)
                        mm4_ticks.append(tn)
                else:
                    prog.wait("scalar", cT, cT.n)
                    wn = 0
                    for kd in range(NKD):
                        wn = prog.dma("scalar", ab(B_WOT + kd * 512, 512),
                                      woTb[128 * kd:128 * (kd + 1),
                                           osl * 512:(osl + 1) * 512], dLdC)
                    prog.wait("tensor", dLdC, wn)
                    for kd in range(NKD):
                        for ts in range(NTS):
                            def fn(e, kd=kd, ts=ts, PS=PS):
                                return e.matmul(
                                    PS[ts][:, 0:512],
                                    ab(B_OACT + kd * 512 + ts * 128, 128),
                                    ab(B_WOT + kd * 512, 512),
                                    start=(kd == 0), stop=(kd == NKD - 1))
                            if kd == NKD - 1 and ts == NTS - 1:
                                tn = prog.op("tensor", fn, cT)
                            else:
                                prog.op("tensor", fn)
                    mm4_ticks.append(tn)
                prog.wait("vector", cT, tn)
                prog.wait("vector", dLdS, xn_n)
                for ts in range(NTS):
                    rc = cols[:, C_R256 + bc * 4 + ts:C_R256 + bc * 4 + ts + 1]
                    prog.op("vector", lambda e, ts=ts, rc=rc, PS=PS: e.tensor_scalar(
                        ai(ts * 512, 512), PS[ts][:, 0:512], rc, None, ALU.mult))
                    vn = prog.op("vector", lambda e, ts=ts, osl=osl: e.scalar_tensor_tensor(
                        af(FB_Y0F + ts * 2048 + osl * 512, 512), ai(ts * 512, 512),
                        1.0 / 256.0, af(FB_XN + ts * 512, 512),
                        ALU.mult, ALU.add), cV)
                mm4_evacs.append(vn)

            xq_sp_base = dSp.n
            for ts in range(NTS):
                y0t = af(FB_Y0F + ts * 2048, 2048)
                vn = prog.vchain(cV, [
                    lambda e, y0t=y0t: e.reduce_sum(cols[:, 40:41], y0t, axis=AX.X),
                    lambda e: e.tensor_scalar(
                        cols[:, 40:41], cols[:, 40:41], 1.0 / D, None, ALU.mult),
                ])
                prog.wait("scalar", cV, vn)
                sn = prog.op("scalar", lambda e, y0t=y0t: e.activation(
                    af(FB_SCR, 2048), y0t, AF.Square,
                    accum_out=cols[:, 41:42]), cS)
                prog.wait("vector", cS, sn)
                vn = prog.vchain(cV, [
                    lambda e: e.tensor_tensor(
                        cols[:, 42:43], cols[:, 40:41], cols[:, 40:41], ALU.mult),
                    lambda e: e.tensor_scalar(
                        cols[:, 41:42], cols[:, 41:42], 1.0 / D, None, ALU.mult),
                    lambda e: e.tensor_tensor(
                        cols[:, 41:42], cols[:, 41:42], cols[:, 42:43], ALU.subtract),
                    lambda e: e.tensor_scalar(
                        cols[:, 41:42], cols[:, 41:42], 1.0, EPS, ALU.mult, ALU.add),
                ])
                prog.wait("scalar", cV, vn)
                sn = prog.op("scalar", lambda e: e.activation(
                    cols[:, 41:42], cols[:, 41:42], AF.Sqrt), cS)
                prog.wait("vector", cS, sn)
                vn = prog.vchain(cV, [
                    lambda e: e.reciprocal(cols[:, 41:42], cols[:, 41:42]),
                    lambda e, ts=ts, y0t=y0t: e.tensor_scalar(
                        ab(B_Y1 + ts * 2048, 2048), y0t, cols[:, 40:41],
                        cols[:, 41:42], ALU.subtract, ALU.mult),
                ])

                prog.wait("vector", dSp, dSp.n)
                vn = prog.vchain(cV, [
                    lambda e, y0t=y0t: e.scalar_tensor_tensor(
                        af(FB_SCR, 2048), y0t, cols[:, 40:41], nwbb[:, 0:2048],
                        ALU.subtract, ALU.mult),
                    lambda e: e.reduce_max(
                        cols[:, 44:45], af(FB_SCR, 2048), axis=AX.X,
                        apply_absolute_value=True),
                    lambda e: e.tensor_tensor(
                        cols[:, 43:44], cols[:, 41:42], cols[:, 41:42], ALU.mult),
                    lambda e: e.tensor_scalar(
                        cols[:, 43:44], cols[:, 43:44], -EPS, 1.0 + EPS,
                        ALU.mult, ALU.add),
                ])
                prog.wait("scalar", cV, vn)
                sn = prog.op("scalar", lambda e: e.activation(
                    cols[:, 43:44], cols[:, 43:44], AF.Sqrt), cS)
                prog.wait("vector", cS, sn)
                vn = prog.vchain(cV, [
                    lambda e: e.reciprocal(cols[:, 43:44], cols[:, 43:44]),
                    lambda e: e.tensor_tensor(
                        cols[:, 42:43], cols[:, 41:42], cols[:, 43:44], ALU.mult),
                    lambda e: e.tensor_tensor(
                        cols[:, 45:46], cols[:, 42:43], cols[:, 44:45], ALU.mult),
                    lambda e: e.tensor_scalar(
                        cols[:, 45:46], cols[:, 45:46], 1e-5, None, ALU.max),
                    lambda e: e.reciprocal(cols[:, 46:47], cols[:, 45:46]),
                    lambda e: e.tensor_scalar(
                        cols[:, 46:47], cols[:, 46:47], 127.0, None, ALU.mult),
                    lambda e: e.tensor_tensor(
                        cols[:, 46:47], cols[:, 46:47], cols[:, 42:43], ALU.mult),
                    lambda e, ts=ts: e.tensor_scalar(
                        cols[:, 48 + ts:49 + ts], cols[:, 45:46], 1.0 / 127.0,
                        None, ALU.mult),
                    lambda e, ts=ts: e.tensor_tensor(
                        cols[:, 48 + ts:49 + ts], cols[:, 48 + ts:49 + ts],
                        cols[:, C_MW1:C_MW1 + 1], ALU.mult),
                    lambda e: e.tensor_scalar(
                        ai(0, 2048), af(FB_SCR, 2048), cols[:, 46:47], None,
                        ALU.mult),
                    lambda e: e.tensor_copy(ab(B_XQ, 2048), ai(0, 2048)),
                ])
                prog.wait("sync", cV, vn)
                prog.dma("sync", xqs[ts * 128:(ts + 1) * 128, :], ab(B_XQ, 2048),
                         dSp)
            xq_sp = dSp.n
            if DEBUG and bc == 0:
                prog.wait("gpsimd", cV, cV.n)
                prog.dma("gpsimd", dbg_y0, ab(B_Y1, 8192), dOut)
                prog.dma("gpsimd", dbg_y1, ab(B_Y1, 8192), dOut)

            prog.wait("sync", cT, cT.n)
            prog.wait("sync", dSp, xq_sp)
            xqt_n = 0
            for kd in range(NKD):
                xqt_n = prog.dma("sync", ab(B_XQT + kd * 512, 512),
                                 xqs[0:TB, kd * 128:(kd + 1) * 128], dLdS,
                                 transpose=True)

            PU = [P[4], P[5], P[6], PM]
            GR_SLOT = [FB_GR, FB_T1, 9216, 9728]
            relu_tick = [0, 0, 0, 0]
            t1_tick = [0, 0, 0, 0]
            hm_tick = [0, 0, 0, 0]
            spill_cnt = [0, 0, 0, 0]
            gu_ticks = []
            prog.wait("tensor", dLdS, xqt_n)
            prog.wait("tensor", cV, cV.n)
            prog.wait("tensor", cS, cS.n)
            prog.wait("scalar", cT, cT.n)
            prog.wait("sync", cT, cT.n)
            for jp in range(NJP):
                if jp > 0:
                    prog.wait("tensor", cS, relu_tick[NTS - 1])
                    prog.wait("tensor", cV, t1_tick[NTS - 1])
                for kd in range(NKD):
                    g = jp * NKD + kd
                    gs = B_G + (g % 4) * 512
                    us = B_U + (g % 4) * 512
                    if g >= 4:
                        prog.wait("scalar", cT, gu_ticks[g - 4])
                        prog.wait("sync", cT, gu_ticks[g - 4])
                    gn_ = prog.dma("scalar", ab(gs, 512),
                                   wqgu[128 * kd:128 * (kd + 1),
                                        jp * 512:(jp + 1) * 512], dLdC)
                    un_ = prog.dma("sync", ab(us, 512),
                                   wqgu[128 * kd:128 * (kd + 1),
                                        F + jp * 512:F + (jp + 1) * 512], dLdS)
                    prog.wait("tensor", dLdC, gn_)
                    prog.wait("tensor", dLdS, un_)
                    for ts in range(NTS):
                        prog.op("tensor", lambda e, kd=kd, ts=ts, gs=gs: e.matmul(
                            P[ts][:, 0:512],
                            ab(B_XQT + kd * 512 + ts * 128, 128),
                            ab(gs, 512),
                            start=(kd == 0), stop=(kd == NKD - 1)))
                        def fnu(e, kd=kd, ts=ts, us=us):
                            return e.matmul(
                                PU[ts][:, 0:512],
                                ab(B_XQT + kd * 512 + ts * 128, 128),
                                ab(us, 512),
                                start=(kd == 0), stop=(kd == NKD - 1))
                        if ts == NTS - 1:
                            tn = prog.op("tensor", fnu, cT)
                        else:
                            prog.op("tensor", fnu)
                    gu_ticks.append(tn)
                prog.wait("scalar", cT, tn)
                prog.wait("vector", cT, tn)
                for ts in range(NTS):
                    hb = FB_HID + ts * 512
                    gr = GR_SLOT[ts]
                    if jp > 0:
                        prog.wait("scalar", cV, hm_tick[ts])
                    sn = prog.op("scalar", lambda e, ts=ts, gr=gr: e.activation(
                        af(gr, 512), P[ts][:, 0:512], AF.Relu), cS)
                    relu_tick[ts] = sn
                    prog.wait("vector", cS, sn)
                    if jp > 0:
                        prog.wait("vector", dSp, spill_cnt[ts])
                    t1n = prog.op("vector", lambda e, hb=hb, gr=gr, ts=ts: e.tensor_tensor(
                        af(hb, 512), af(gr, 512), PU[ts][:, 0:512], ALU.mult), cV)
                    t1_tick[ts] = t1n
                    hm = prog.op("vector", lambda e, hb=hb, gr=gr: e.tensor_tensor(
                        af(hb, 512), af(hb, 512), af(gr, 512), ALU.mult), cV)
                    hm_tick[ts] = hm
                    prog.wait("vector", cV, hm)
                    vn2 = prog.op("vector", lambda e, hb=hb: e.reduce_max(
                        cols[:, 56:57], af(hb, 512), axis=AX.X,
                        apply_absolute_value=True), cV)
                    prog.wait("vector", cV, vn2)
                    vn2 = prog.op("vector", lambda e, ts=ts: e.tensor_tensor(
                        cols[:, 52 + ts:53 + ts], cols[:, 52 + ts:53 + ts],
                        cols[:, 56:57], ALU.max), cV)
                    prog.wait("vector", cV, vn2)
                    prog.wait("gpsimd", cV, hm)
                    spill_cnt[ts] = prog.dma(
                        "gpsimd", hids[ts * 128:(ts + 1) * 128,
                                       jp * 512:(jp + 1) * 512],
                        af(hb, 512), dSp)
            hid_sp = dSp.n

            vn = prog.vchain(cV, [
                lambda e: e.tensor_tensor(
                    cols[:, 66:70], cols[:, 48:52], cols[:, 48:52], ALU.mult),
                lambda e: e.tensor_tensor(
                    cols[:, 66:70], cols[:, 66:70], cols[:, 48:52], ALU.mult),
                lambda e: e.tensor_tensor(
                    cols[:, 52:56], cols[:, 52:56], cols[:, 66:70], ALU.mult),
                lambda e: e.tensor_scalar(
                    cols[:, 52:56], cols[:, 52:56], 1e-5, None, ALU.max),
                lambda e: e.reciprocal(cols[:, 57:61], cols[:, 52:56]),
                lambda e: e.tensor_scalar(
                    cols[:, 57:61], cols[:, 57:61], 127.0, None, ALU.mult),
                lambda e: e.tensor_tensor(
                    cols[:, 57:61], cols[:, 57:61], cols[:, 66:70], ALU.mult),
                lambda e: e.tensor_scalar(
                    cols[:, 62:66], cols[:, 52:56], 1.0 / 127.0, None, ALU.mult),
                lambda e: e.tensor_scalar(
                    cols[:, 62:66], cols[:, 62:66], cols[:, C_MW2:C_MW2 + 1],
                    None, ALU.mult),
            ])

            prog.wait("sync", dSp, hid_sp)
            prog.wait("sync", cV, vn)
            hq_pieces = [(0, 2048), (2048, 2048), (4096, 1536)]
            for ts in range(NTS):
                for (c0, wd) in hq_pieces:
                    prog.wait("sync", dSp, dSp.n)
                    n = prog.dma("sync", af(FB_SCR, wd),
                                 hids[ts * 128:(ts + 1) * 128, c0:c0 + wd], dLdS)
                    prog.wait("vector", dLdS, n)
                    prog.op("vector", lambda e, ts=ts, wd=wd: e.tensor_scalar(
                        ai(0, wd), af(FB_SCR, wd),
                        cols[:, 57 + ts:58 + ts], None, ALU.mult))
                    vn = prog.op("vector", lambda e, wd=wd: e.tensor_copy(
                        ab(B_HQST, wd), ai(0, wd)), cV)
                    prog.wait("sync", cV, vn)
                    prog.dma("sync", hqs[ts * 128:(ts + 1) * 128, c0:c0 + wd],
                             ab(B_HQST, wd), dSp)
            hq_sp = dSp.n

            prog.wait("sync", cT, cT.n)
            prog.wait("sync", dSp, hq_sp)
            hqt_n = 0
            for kf in range(NKF):
                hqt_n = prog.dma("sync", ab(B_HQT + kf * 512, 512),
                                 hqs[0:TB, kf * 128:(kf + 1) * 128], dLdS,
                                 transpose=True)

            dn_ticks = []
            prog.wait("tensor", dLdS, hqt_n)
            prog.wait("tensor", cV, cV.n)
            prog.wait("tensor", cS, cS.n)
            prog.wait("scalar", cT, cT.n)
            prog.wait("sync", cT, cT.n)
            dn_evacs = []
            DPSETS = [P[0:4], [P[4], P[5], P[6], PM]]
            DN_DUAL = False
            for osl in range(NOSL):
                PSd = DPSETS[osl % 2] if DN_DUAL else DPSETS[0]
                if DN_DUAL and osl >= 2:
                    prog.wait("tensor", cV, dn_evacs[osl - 2])
                elif osl >= 1:
                    prog.wait("tensor", cV, dn_evacs[osl - 1])
                for kf in range(NKF):
                    g = osl * NKF + kf
                    ring = B_DNRING + (g % 8) * 512
                    q = "scalar" if (g % 2 == 0) else "sync"
                    qs = dLdC if q == "scalar" else dLdS
                    if g >= 8:
                        prog.wait(q, cT, dn_ticks[g - 8])
                    n_ = prog.dma(q, ab(ring, 512),
                                  wqdn[128 * kf:128 * (kf + 1),
                                       osl * 512:(osl + 1) * 512], qs)
                    prog.wait("tensor", qs, n_)
                    for ts in range(NTS):
                        def fnd(e, kf=kf, ts=ts, ring=ring, PSd=PSd):
                            return e.matmul(
                                PSd[ts][:, 0:512],
                                ab(B_HQT + kf * 512 + ts * 128, 128),
                                ab(ring, 512),
                                start=(kf == 0), stop=(kf == NKF - 1))
                        if ts == NTS - 1:
                            tn = prog.op("tensor", fnd, cT)
                        else:
                            prog.op("tensor", fnd)
                    dn_ticks.append(tn)
                prog.wait("vector", cT, tn)
                for ts in range(NTS):
                    vn = prog.op("vector", lambda e, ts=ts, osl=osl, PSd=PSd: e.scalar_tensor_tensor(
                        af(FB_Y0F + ts * 2048 + osl * 512, 512), PSd[ts][:, 0:512],
                        cols[:, 62 + ts:63 + ts],
                        ab(B_Y1 + ts * 2048 + osl * 512, 512),
                        ALU.mult, ALU.add), cV)
                dn_evacs.append(vn)

            for ts in range(NTS):
                y2t = af(FB_Y0F + ts * 2048, 2048)
                vn = prog.vchain(cV, [
                    lambda e, y2t=y2t: e.reduce_sum(cols[:, 40:41], y2t, axis=AX.X),
                    lambda e: e.tensor_scalar(
                        cols[:, 40:41], cols[:, 40:41], 1.0 / D, None, ALU.mult),
                ])
                prog.wait("scalar", cV, vn)
                sn = prog.op("scalar", lambda e, y2t=y2t: e.activation(
                    af(FB_SCR, 2048), y2t, AF.Square,
                    accum_out=cols[:, 41:42]), cS)
                prog.wait("vector", cS, sn)
                vn = prog.vchain(cV, [
                    lambda e: e.tensor_tensor(
                        cols[:, 42:43], cols[:, 40:41], cols[:, 40:41], ALU.mult),
                    lambda e: e.tensor_scalar(
                        cols[:, 41:42], cols[:, 41:42], 1.0 / D, None, ALU.mult),
                    lambda e: e.tensor_tensor(
                        cols[:, 41:42], cols[:, 41:42], cols[:, 42:43], ALU.subtract),
                    lambda e: e.tensor_scalar(
                        cols[:, 41:42], cols[:, 41:42], 1.0, EPS, ALU.mult, ALU.add),
                ])
                prog.wait("scalar", cV, vn)
                sn = prog.op("scalar", lambda e: e.activation(
                    cols[:, 41:42], cols[:, 41:42], AF.Sqrt), cS)
                prog.wait("vector", cS, sn)
                vn = prog.vchain(cV, [
                    lambda e: e.reciprocal(cols[:, 41:42], cols[:, 41:42]),
                ])
                prog.wait("vector", dOut, dOut.n)
                vn = prog.op("vector", lambda e, y2t=y2t: e.tensor_scalar(
                    af(FB_OUT, 2048), y2t, cols[:, 40:41], cols[:, 41:42],
                    ALU.subtract, ALU.mult), cV)
                prog.wait("sync", cV, vn)
                prog.dma("sync", out[t0 + ts * 128:t0 + ts * 128 + 128, :],
                         af(FB_OUT, 2048), dOut)
            vn = prog.op("vector", lambda e: e.memset(cols[:, 52:56], 0.0), cV)

        if DEBUG:
            prog.wait("gpsimd", cV, cV.n)
            prog.dma("gpsimd", dbg_y2p, af(FB_Y0F, 8192), dOut)
            prog.dma("gpsimd", dbg_y1l, ab(B_Y1, 8192), dOut)
            prog.dma("gpsimd", dbg_hqt, ab(B_HQT, 22528), dOut)
            prog.dma("gpsimd", dbg_h2, h2row[:], dOut)
            prog.dma("gpsimd", dbg_cols, cols[:], dOut)
            prog.dma("gpsimd", dbg_vals, vals[:], dOut)
            prog.dma("gpsimd", dbg_carry, carry[:], dOut)
            prog.dma("gpsimd", dbg_svc, svc[:], dOut)
        prog.wait("gpsimd", dOut, dOut.n)

        with nc.Block() as block:
            @block.sync
            def _(e):
                for fn in prog.q["sync"]:
                    fn(e)

            @block.scalar
            def _(e):
                for fn in prog.q["scalar"]:
                    fn(e)

            @block.vector
            def _(e):
                for fn in prog.q["vector"]:
                    fn(e)

            @block.tensor
            def _(e):
                for fn in prog.q["tensor"]:
                    fn(e)

            @block.gpsimd
            def _(e):
                for fn in prog.q["gpsimd"]:
                    fn(e)

    return nc


_NC_CACHE = None


def kernel(**inputs):
    global _NC_CACHE
    x = np.asarray(inputs["L0_attn_hidden_states"], dtype=np.float32)
    w_i = np.ascontiguousarray(np.asarray(inputs["L0_attn_w_i"], dtype=np.float32))
    w_f = np.ascontiguousarray(np.asarray(inputs["L0_attn_w_f"], dtype=np.float32))
    w_g = np.ascontiguousarray(np.asarray(inputs["L0_attn_w_g"], dtype=np.float32))
    w_oT = np.ascontiguousarray(np.asarray(inputs["L0_attn_w_o"], dtype=np.float32).T)
    s_i = np.asarray(inputs["L0_attn_s_i"], dtype=np.float32)
    s_f = np.asarray(inputs["L0_attn_s_f"], dtype=np.float32)
    s_g = np.asarray(inputs["L0_attn_s_g"], dtype=np.float32)
    s_o = np.asarray(inputs["L0_attn_s_o"], dtype=np.float32)
    g_norm = np.asarray(inputs["L0_attn_g_norm_w"], dtype=np.float32)
    w_guT = np.ascontiguousarray(np.asarray(inputs["L0_ffn_w_gate_up"], dtype=np.float32).T)
    w_dnT = np.ascontiguousarray(np.asarray(inputs["L0_ffn_w_down"], dtype=np.float32).T)
    norm_w = np.asarray(inputs["L0_ffn_norm_weight"], dtype=np.float32)

    if _NC_CACHE is None:
        _NC_CACHE = build_nc()
    nc = _NC_CACHE

    eye = np.eye(128, dtype=np.float32)
    nwb = np.ascontiguousarray(np.broadcast_to(norm_w[None, :], (128, D)))
    svs = {
        "sv_i": np.ascontiguousarray(s_i.reshape(16, 128)),
        "sv_f": np.ascontiguousarray(s_f.reshape(16, 128)),
        "sv_g": np.ascontiguousarray(s_g.reshape(16, 128)),
        "sv_gn": np.ascontiguousarray(g_norm.reshape(16, 128)),
        "sv_o": np.ascontiguousarray(s_o.reshape(16, 128)),
    }

    in_maps = []
    for c in range(8):
        b, s = c // 2, c % 2
        if s == 0:
            xTc = np.ascontiguousarray(
                np.concatenate([x[b, 0:2048].T, x[b, 0:2048].T], axis=1))
            mk = np.zeros((16, 128), np.float32)
        else:
            xTc = np.ascontiguousarray(x[b].T)
            mk = np.ones((16, 128), np.float32)
        in_maps.append(dict(
            xT=xTc,
            xnat=np.ascontiguousarray(x[b, s * 2048:(s + 1) * 2048]),
            w_i=w_i, w_f=w_f, w_g=w_g, w_oT=w_oT, w_guT=w_guT, w_dnT=w_dnT,
            nwb=nwb, eye=eye, sv_mk=np.ascontiguousarray(mk), **svs,
        ))

    res = run_bass_kernel_spmd(nc, in_maps, core_ids=list(range(8)))
    outp = np.empty((B, T, D), dtype=np.float32)
    for c in range(8):
        b, s = c // 2, c % 2
        outp[b, s * 2048:(s + 1) * 2048] = res.results[c]["out"]
    if DEBUG:
        return outp, res.results
    return outp


# version 13
# speedup vs baseline: 1.3812x; 1.0872x over previous
from contextlib import ExitStack

import numpy as np

import concourse.bass as bass
import concourse.mybir as mybir
from concourse.bass_utils import run_bass_kernel_spmd

F32 = mybir.dt.float32
BF16 = mybir.dt.bfloat16
I32 = mybir.dt.int32
ALU = mybir.AluOpType
AF = mybir.ActivationFunctionType
AX = mybir.AxisListType

B, T, D = 4, 4096, 2048
F = 5632
F2 = 2 * F
TPC = 2048
NKD = D // 128
ESL = 512
NES = D // ESL
TC = 512
NTCA = T // TC
NPRE = 4
TB = 512
NBC = TPC // TB
NTS = TB // 128
NOSL = D // 512
NJP = F // 512
NKF = F // 128
EPS = 1e-5
NG = F2 * D
ND = F * D

ENGINES = ("sync", "scalar", "vector", "tensor", "gpsimd")
DEBUG = False


class Sem:
    def __init__(self, handle):
        self.h = handle
        self.n = 0

    def inc(self, k):
        self.n += k
        return self.n


class Prog:
    def __init__(self):
        self.q = {e: [] for e in ENGINES}

    def on(self, eng, fn):
        self.q[eng].append(fn)

    def dma(self, eng, out, in_, sem, transpose=False):
        def fn(e, out=out, in_=in_, sem=sem, tr=transpose):
            if tr:
                e.dma_start_transpose(out, in_).then_inc(sem.h, 16)
            else:
                e.dma_start(out=out, in_=in_).then_inc(sem.h, 16)
        self.on(eng, fn)
        return sem.inc(16)

    def wait(self, eng, sem, val=None):
        v = sem.n if val is None else val
        if v <= 0:
            return
        self.on(eng, lambda e, h=sem.h, v=v: e.wait_ge(h, v))

    def vchain(self, sem, ops):
        self.wait("vector", sem, sem.n)
        vn = None
        for f in ops:
            vn = self.op("vector", f, sem)
            self.wait("vector", sem, vn)
        return vn

    def op(self, eng, fn, sem=None):
        if sem is None:
            self.on(eng, fn)
            return None
        def wrapped(e, fn=fn, sem=sem):
            fn(e).then_inc(sem.h, 1)
        self.on(eng, wrapped)
        return sem.inc(1)


def build_nc():
    nc = bass.Bass("TRN2", target_bir_lowering=False, debug=False,
                   detect_race_conditions=False)

    xT = nc.dram_tensor("xT", [D, T], F32, kind="ExternalInput").ap()
    xnat = nc.dram_tensor("xnat", [TPC, D], F32, kind="ExternalInput").ap()
    w_i = nc.dram_tensor("w_i", [D, D], F32, kind="ExternalInput").ap()
    w_f = nc.dram_tensor("w_f", [D, D], F32, kind="ExternalInput").ap()
    w_g = nc.dram_tensor("w_g", [D, D], F32, kind="ExternalInput").ap()
    w_oT = nc.dram_tensor("w_oT", [D, D], F32, kind="ExternalInput").ap()
    w_guT = nc.dram_tensor("w_guT", [D, F2], F32, kind="ExternalInput").ap()
    w_dnT = nc.dram_tensor("w_dnT", [F, D], F32, kind="ExternalInput").ap()
    sv_in = {}
    for nm in ("sv_i", "sv_f", "sv_g", "sv_gn", "sv_o", "sv_mk"):
        sv_in[nm] = nc.dram_tensor(nm, [16, 128], F32, kind="ExternalInput").ap()
    nwb_in = nc.dram_tensor("nwb", [128, D], F32, kind="ExternalInput").ap()
    eye_in = nc.dram_tensor("eye", [128, 128], F32, kind="ExternalInput").ap()
    out = nc.dram_tensor("out", [TPC, D], F32, kind="ExternalOutput").ap()

    KD = "ExternalOutput" if DEBUG else "Internal"
    x256s = nc.dram_tensor("x256s", [D, T], BF16, kind=KD).ap()
    oacts = nc.dram_tensor("oacts", [D, TPC], BF16, kind=KD).ap()
    woTb = nc.dram_tensor("woTb", [D, D], BF16, kind="Internal").ap()
    wqgu = nc.dram_tensor("wqgu", [D, F2], BF16, kind=KD).ap()
    wqdn = nc.dram_tensor("wqdn", [F, D], BF16, kind=KD).ap()
    xqs = nc.dram_tensor("xqs", [TB, D], BF16, kind=KD).ap()
    hids = nc.dram_tensor("hids", [TB, F], F32, kind=KD).ap()
    hqs = nc.dram_tensor("hqs", [TB, F], BF16, kind=KD).ap()
    if DEBUG:
        dbg_h2 = nc.dram_tensor("dbg_h2", [1, TPC], F32, kind="ExternalOutput").ap()
        dbg_cols = nc.dram_tensor("dbg_cols", [128, 96], F32, kind="ExternalOutput").ap()
        dbg_vals = nc.dram_tensor("dbg_vals", [1, 16], F32, kind="ExternalOutput").ap()
        dbg_carry = nc.dram_tensor("dbg_carry", [128, 16], F32, kind="ExternalOutput").ap()
        dbg_svc = nc.dram_tensor("dbg_svc", [128, 128], F32, kind="ExternalOutput").ap()
        dbg_cols2 = nc.dram_tensor("dbg_cols2", [128, 160], F32, kind="ExternalOutput").ap()
        dbg_y0 = nc.dram_tensor("dbg_y0", [128, 8192], BF16, kind="ExternalOutput").ap()
        dbg_y1 = nc.dram_tensor("dbg_y1", [128, 8192], BF16, kind="ExternalOutput").ap()
        dbg_y2p = nc.dram_tensor("dbg_y2p", [128, 8192], F32, kind="ExternalOutput").ap()
        dbg_y1l = nc.dram_tensor("dbg_y1l", [128, 8192], BF16, kind="ExternalOutput").ap()
        dbg_hqt = nc.dram_tensor("dbg_hqt", [128, 22528], BF16, kind="ExternalOutput").ap()

    prog = Prog()

    with ExitStack() as es_:
        ctx = es_.enter_context
        AB = ctx(nc.sbuf_tensor("AB", [128, 47104], BF16))
        AFr = ctx(nc.sbuf_tensor("AFr", [128, 18432], F32))
        AI = ctx(nc.sbuf_tensor("AI", [128, 4096], I32))
        svc = ctx(nc.sbuf_tensor("svc", [128, 128], F32))
        carry = ctx(nc.sbuf_tensor("carry", [128, 16], F32))
        nwbb = ctx(nc.sbuf_tensor("nwbb", [128, D], F32))
        eyef = ctx(nc.sbuf_tensor("eyef", [128, 128], F32))
        ones_col = ctx(nc.sbuf_tensor("ones_col", [128, 1], F32))
        ones_row = ctx(nc.sbuf_tensor("ones_row", [1, 128], F32))
        h2row = ctx(nc.sbuf_tensor("h2row", [1, TPC], F32))
        cols = ctx(nc.sbuf_tensor("cols", [128, 96], F32))
        vals = ctx(nc.sbuf_tensor("vals", [1, 16], F32))
        cols2 = ctx(nc.sbuf_tensor("cols2", [128, 160], F32))
        p0bo = ctx(nc.sbuf_tensor("p0bo", [128, 4096], BF16))
        svraw = ctx(nc.sbuf_tensor("svraw", [16, 128], F32))

        P = [ctx(nc.psum_tensor(f"p{i}", [128, 512], F32)) for i in range(7)]
        PM = ctx(nc.psum_tensor("pm", [128, 512], F32))

        dLdS = Sem(ctx(nc.semaphore("dLdS")))
        dLdC = Sem(ctx(nc.semaphore("dLdC")))
        dLdG = Sem(ctx(nc.semaphore("dLdG")))
        dSp = Sem(ctx(nc.semaphore("dSp")))
        dOut = Sem(ctx(nc.semaphore("dOut")))
        cV = Sem(ctx(nc.semaphore("cV")))
        cS = Sem(ctx(nc.semaphore("cS")))
        cT = Sem(ctx(nc.semaphore("cT")))

        A_WSL = {"i": 0, "f": 8192, "g": 16384}
        A_XC = 24576
        A_SGB = 40960
        A_OAST = 43008
        B_Y1 = 0
        B_Y0 = 8192
        B_WOT = 16384
        B_XQT = 16384
        B_OACT = 24576
        B_G = 24576
        B_HQT = 24576
        B_U = 34816
        B_XQ = 8192
        B_HQST = 10240
        B_DNRING = 12288
        FA_IB = 0
        FA_HB = 2048
        FA_H2 = 4096
        FA_TMP = 4608
        FA_FB = 5120
        FB_XN = 0
        FB_GR = 2048
        FB_T1 = 2560
        FB_SCR = 3072
        FB_OUT = 5120
        FB_HID = 7168
        FB_Y0F = 10240

        def ab(off, n):
            return AB[:, off:off + n]

        def af(off, n):
            return AFr[:, off:off + n]

        def ai(off, n):
            return AI[:, off:off + n]

        prog.op("vector", lambda e: e.memset(ones_col[:], 1.0))
        prog.op("vector", lambda e: e.memset(ones_row[:], 1.0))
        prog.op("vector", lambda e: e.memset(h2row[:], 0.0))
        prog.op("vector", lambda e: e.memset(carry[:], 0.0))
        vset = prog.op("vector", lambda e: e.memset(cols[:], 0.0), cV)

        n = prog.dma("sync", eyef[:], eye_in, dLdS)
        prog.wait("tensor", dLdS, n)
        prog.dma("sync", nwbb[:], nwb_in, dLdS)

        sv_order = [("sv_i", 0), ("sv_f", 1), ("sv_g", 2), ("sv_gn", 3),
                    ("sv_o", 4), ("sv_mk", 5)]
        for nm, idx in sv_order:
            n = prog.dma("sync", svraw[:], sv_in[nm], dLdS)
            prog.wait("tensor", dLdS, n)
            tn = prog.op("tensor", lambda e: e.transpose(
                PM[:, 0:16], svraw[0:16, :], eyef[0:16, 0:16]), cT)
            prog.wait("vector", cT, tn)
            vn = prog.op("vector", lambda e, idx=idx: e.tensor_copy(
                svc[:, idx * 16:(idx + 1) * 16], PM[:, 0:16]), cV)
            prog.wait("tensor", cV, vn)
            prog.wait("sync", cT, tn)
        SV_SI256, SV_SF256, SV_SG256, SV_GNSO, SV_MASK = 96 - 16, 96, 32, 48, 80
        vn = prog.vchain(cV, [
            lambda e: e.tensor_scalar(
                svc[:, 0:16], svc[:, 0:16], 1.0 / 256.0, None, ALU.mult),
            lambda e: e.tensor_scalar(
                svc[:, 16:32], svc[:, 16:32], 1.0 / 256.0, None, ALU.mult),
            lambda e: e.tensor_scalar(
                svc[:, 32:48], svc[:, 32:48], 1.0 / 256.0, None, ALU.mult),
            lambda e: e.tensor_tensor(
                svc[:, 48:64], svc[:, 48:64], svc[:, 64:80], ALU.mult),
        ])
        SV_SI256, SV_SF256, SV_SG256, SV_GNSO, SV_MASK = 0, 16, 32, 48, 80

        for k in range(NKD):
            buf = A_XC + (k % 2) * 2048
            if k >= 2:
                prog.wait("gpsimd", dSp, 16 * (k - 1))
            prog.dma("gpsimd", ab(buf, 2048), w_oT[128 * k:128 * (k + 1), :], dLdG)
            prog.wait("gpsimd", dLdG)
            prog.dma("gpsimd", woTb[128 * k:128 * (k + 1), :], ab(buf, 2048), dSp)
        woT_done = dSp.n

        m = 0
        for k in range(NKD):
            for h in range(2):
                rs = slice(128 * k, 128 * (k + 1))
                cs = slice(2048 * h, 2048 * (h + 1))
                fo = FA_IB + (m % 2) * 2048
                io_ = (m % 2) * 2048
                bo = A_WSL["i"] + (m % 2) * 2048
                if m >= 2:
                    prog.wait("sync", dSp, woT_done + 16 * (m - 1))
                n = prog.dma("sync", af(fo, 2048), xT[rs, cs], dLdS)
                prog.wait("vector", dLdS, n)
                prog.op("vector", lambda e, fo=fo, io_=io_: e.tensor_scalar(
                    ai(io_, 2048), af(fo, 2048), 256.0, None, ALU.mult))
                vn = prog.op("vector", lambda e, io_=io_, bo=bo: e.tensor_copy(
                    ab(bo, 2048), ai(io_, 2048)), cV)
                prog.wait("sync", cV, vn)
                prog.dma("sync", x256s[rs, cs], ab(bo, 2048), dSp)
                m += 1
        x256_done = dSp.n

        def wtiles():
            for k in range(NKD):
                for j in range(5):
                    yield (w_guT, wqgu, 128 * k, 2048 * j, 2048, 0)
                yield (w_guT, wqgu, 128 * k, 10240, 1024, 0)
            for k in range(F // 128):
                yield (w_dnT, wqdn, 128 * k, 0, 2048, 1)

        prog.wait("scalar", cV, vset)
        tiles_list = list(wtiles())
        NT_GU = 96
        AF_P0 = 8192

        def p0b_gen():
            sn_hist = []
            ld_hist = []

            def consume(t):
                fo = AF_P0 + (t % 4) * 2048
                wd = tiles_list[t][4]
                sn = prog.op("vector", lambda e, fo=fo, wd=wd, t=t: e.tensor_reduce(
                    cols2[:, t:t + 1], af(fo, wd), AX.X, ALU.add,
                    apply_absolute_value=True), cV)
                sn_hist.append(sn)

            for t, (wt, _, r0, c0, wd, which) in enumerate(tiles_list):
                fo = AF_P0 + (t % 4) * 2048
                if t >= 3:
                    prog.wait("sync", cV, sn_hist[t - 3])
                ld_hist.append(prog.dma("sync", af(fo, wd),
                                        wt[r0:r0 + 128, c0:c0 + wd], dLdS))
                if t >= 1:
                    prog.wait("vector", dLdS, ld_hist[t])
                    consume(t - 1)
                yield
            prog.wait("vector", dLdS, ld_hist[-1])
            consume(len(tiles_list) - 1)
        def p0_stats_finalize():
            vn = prog.vchain(cV, [
                lambda e: e.reduce_sum(cols[:, 32:33], cols2[:, 0:NT_GU], axis=AX.X),
                lambda e: e.reduce_sum(cols[:, 33:34], cols2[:, NT_GU:len(tiles_list)], axis=AX.X),
            ])
            prog.wait("tensor", cV, cV.n)
            tn = prog.op("tensor", lambda e: e.matmul(
                PM[0:1, 0:1], cols[:, 32:33], ones_col[:], start=True, stop=True), cT)
            tn = prog.op("tensor", lambda e: e.matmul(
                PM[0:1, 1:2], cols[:, 33:34], ones_col[:], start=False, stop=True,
                skip_group_check=True), cT)
            prog.wait("vector", cT, tn)
            vn = prog.vchain(cV, [
                lambda e: e.tensor_copy(vals[0:1, 0:2], PM[0:1, 0:2]),
            ])
            prog.wait("tensor", cV, vn)
            tn = prog.op("tensor", lambda e: e.matmul(
                PM[:, 16:18], ones_row[:], vals[0:1, 0:2], start=True, stop=True,
                skip_group_check=True), cT)
            prog.wait("vector", cT, tn)
            vn = prog.vchain(cV, [
                lambda e: e.tensor_copy(cols[:, 70:72], PM[:, 16:18]),
                lambda e: e.tensor_scalar(
                    cols[:, 36:37], cols[:, 70:71], 1.0 / NG, 1e-5, ALU.mult, ALU.max),
                lambda e: e.tensor_scalar(
                    cols[:, 37:38], cols[:, 71:72], 1.0 / ND, 1e-5, ALU.mult, ALU.max),
                lambda e: e.reciprocal(cols[:, 38:40], cols[:, 36:38]),
            ])
            prog.wait("tensor", cV, vn)
        C_MW1, C_MW2, C_SW1, C_SW2 = 36, 37, 38, 39

        def p0c_gen():
            vn_hist = []
            ld2_hist = []
            st_hist = []

            def consume(t):
                fo = AF_P0 + (t % 4) * 2048
                io_ = (t % 2) * 2048
                bo = (t % 2) * 2048
                (wt, dst, r0, c0, wd, which) = tiles_list[t]
                if t >= 2:
                    prog.wait("vector", dSp, st_hist[t - 2])
                swcol = cols[:, C_SW1 + which:C_SW1 + which + 1]
                prog.op("vector", lambda e, fo=fo, io_=io_, wd=wd, sw=swcol: e.tensor_scalar(
                    ai(io_, wd), af(fo, wd), sw, None, ALU.mult))
                vn = prog.op("vector", lambda e, io_=io_, bo=bo, wd=wd: e.tensor_scalar(
                    p0bo[:, bo:bo + wd], ai(io_, wd), 1.0, -1.0,
                    ALU.min, ALU.max), cV)
                vn_hist.append(vn)
                prog.wait("scalar", cV, vn)
                st_hist.append(prog.dma(
                    "scalar", dst[r0:r0 + 128, c0:c0 + wd],
                    p0bo[:, bo:bo + wd], dSp))

            for t, (wt, dst, r0, c0, wd, which) in enumerate(tiles_list):
                fo = AF_P0 + (t % 4) * 2048
                if t >= 3:
                    prog.wait("sync", cV, vn_hist[t - 3])
                ld2_hist.append(prog.dma("sync", af(fo, wd),
                                         wt[r0:r0 + 128, c0:c0 + wd], dLdS))
                if t >= 1:
                    prog.wait("vector", dLdS, ld2_hist[t])
                    consume(t - 1)
                yield
            prog.wait("vector", dLdS, ld2_hist[-1])
            consume(len(tiles_list) - 1)

        prog.wait("gpsimd", dSp, dSp.n)
        prog.wait("gpsimd", cV, cV.n)
        p0_state = {"gen": p0b_gen(), "phase": "b"}

        def p0_drive(nsteps):
            for _ in range(nsteps):
                if p0_state["gen"] is None:
                    return
                try:
                    next(p0_state["gen"])
                except StopIteration:
                    if p0_state["phase"] == "b":
                        if not P0_INTERLEAVE_C:
                            p0_state["gen"] = None
                            p0_state["phase"] = "b_done"
                            return
                        p0_stats_finalize()
                        p0_state["gen"] = p0c_gen()
                        p0_state["phase"] = "c"
                    else:
                        p0_state["gen"] = None
                        return

        def p0_drain():
            while p0_state["gen"] is not None:
                p0_drive(1000)

        P0_INTERLEAVE_C = True

        seq = [(es, tc) for es in range(NES) for tc in range(NTCA)]
        chunk_tick = {}
        xc_n = {}
        PI = [P[4], P[5], P[6], PM]

        def emit_xc_load(idx):
            es, tc = seq[idx]
            tcol = tc * TC
            base = A_XC + (idx % 2) * 8192
            n = 0
            for k in range(NKD):
                n = prog.dma("sync", ab(base + k * 512, 512),
                             x256s[128 * k:128 * (k + 1), tcol:tcol + TC], dLdS)
            xc_n[idx] = n

        emit_xc_load(0)
        for es in range(NES):
            prog.wait("gpsimd", cT, cT.n)
            wsl_ready = {}
            for wname, wap in (("f", w_f), ("i", w_i), ("g", w_g)):
                for k in range(NKD):
                    prog.dma("gpsimd", ab(A_WSL[wname] + k * 512, 512),
                             wap[128 * k:128 * (k + 1), es * ESL:(es + 1) * ESL],
                             dLdG)
                wsl_ready[wname] = dLdG.n

            for tc in range(NTCA):
                idx = es * NTCA + tc
                stored = tc >= NPRE
                tcol = tc * TC
                xcb = A_XC + (idx % 2) * 8192
                if idx + 1 < len(seq):
                    if idx >= 1:
                        prog.wait("sync", cT, chunk_tick[idx - 1])
                    emit_xc_load(idx + 1)
                prog.wait("tensor", dLdS, xc_n[idx])
                prog.wait("tensor", cV, cV.n)
                prog.wait("tensor", cS, cS.n)

                def mm_four(woff, banks):
                    last = None
                    for et in range(4):
                        for kd in range(NKD):
                            def fn(e, et=et, kd=kd, woff=woff, banks=banks, xcb=xcb):
                                return e.matmul(
                                    banks[et][:, 0:TC],
                                    ab(woff + kd * 512 + et * 128, 128),
                                    ab(xcb + kd * 512, 512),
                                    start=(kd == 0), stop=(kd == NKD - 1))
                            if kd == NKD - 1:
                                last = prog.op("tensor", fn, cT)
                            else:
                                prog.op("tensor", fn)
                    return last

                prog.wait("tensor", dLdG, wsl_ready["f"])
                tn = mm_four(A_WSL["f"], P)
                prog.wait("scalar", cT, tn)
                sig_last = 0
                for et in range(4):
                    k = es * 4 + et
                    sig_last = prog.op("scalar", lambda e, et=et, k=k: e.activation(
                        af(FA_FB + et * 512, 512), P[et][:, 0:TC], AF.Sigmoid,
                        scale=svc[:, SV_SF256 + k:SV_SF256 + k + 1]), cS)
                prog.wait("tensor", dLdG, wsl_ready["i"])
                tn = mm_four(A_WSL["i"], PI)
                i_tick = tn
                prog.wait("vector", cT, tn)
                prog.wait("vector", cS, sig_last)
                prog.wait("vector", cV, cV.n)
                for et in range(4):
                    k = es * 4 + et
                    prog.op("vector", lambda e, et=et, k=k: e.tensor_scalar(
                        af(FA_IB + et * 512, 512), PI[et][:, 0:TC],
                        svc[:, SV_SI256 + k:SV_SI256 + k + 1], None, ALU.mult))
                    prog.op("vector", lambda e, et=et: e.tensor_tensor(
                        af(FA_TMP, 512), af(FA_FB + et * 512, 512),
                        af(FA_IB + et * 512, 512), ALU.mult))
                    prog.op("vector", lambda e, et=et: e.tensor_tensor(
                        af(FA_IB + et * 512, 512), af(FA_IB + et * 512, 512),
                        af(FA_TMP, 512), ALU.subtract))
                    vn = prog.op("vector", lambda e, et=et, k=k: e.tensor_tensor_scan(
                        af(FA_HB + et * 512, 512), af(FA_FB + et * 512, 512),
                        af(FA_IB + et * 512, 512), carry[:, k:k + 1],
                        ALU.mult, ALU.add), cV)
                    prog.wait("vector", cV, vn)
                    vn = prog.op("vector", lambda e, et=et, k=k: e.tensor_copy(
                        carry[:, k:k + 1], af(FA_HB + et * 512 + 511, 1)), cV)
                    prog.wait("vector", cV, vn)
                if tc == NPRE - 1:
                    vn = prog.op("vector", lambda e, es=es: e.tensor_tensor(
                        carry[:, es * 4:es * 4 + 4], carry[:, es * 4:es * 4 + 4],
                        svc[:, SV_MASK + es * 4:SV_MASK + es * 4 + 4],
                        ALU.mult), cV)
                    prog.wait("vector", cV, vn)
                if stored:
                    prog.wait("tensor", cS, sig_last)
                    prog.wait("tensor", dLdG, wsl_ready["g"])
                    tn = mm_four(A_WSL["g"], P)
                    prog.wait("scalar", cT, tn)
                    for et in range(4):
                        k = es * 4 + et
                        prog.op("scalar", lambda e, et=et, k=k: e.activation(
                            ab(A_SGB + et * 512, 512), P[et][:, 0:TC], AF.Silu,
                            scale=svc[:, SV_SG256 + k:SV_SG256 + k + 1]), cS)
                    prog.wait("vector", cS, cS.n)
                    prog.wait("vector", dSp, dSp.n)
                    stcol = tcol - NPRE * TC
                    for et in range(4):
                        k = es * 4 + et
                        oa_vn = prog.op("vector", lambda e, et=et, k=k: e.scalar_tensor_tensor(
                            ab(A_OAST + et * 512, 512), af(FA_HB + et * 512, 512),
                            svc[:, SV_GNSO + k:SV_GNSO + k + 1],
                            ab(A_SGB + et * 512, 512), ALU.mult, ALU.mult), cV)
                        prog.wait("scalar", cV, oa_vn)
                        sn = prog.op("scalar", lambda e, et=et: e.activation(
                            af(FA_H2, 512), af(FA_HB + et * 512, 512),
                            AF.Square), cS)
                        prog.wait("tensor", cS, sn)
                        if et == 0:
                            prog.wait("tensor", cV, cV.n)
                        tn = prog.op("tensor", lambda e, et=et: e.matmul(
                            PM[0:1, 0:TC], ones_col[:], af(FA_H2, 512),
                            start=(et == 0), stop=(et == 3),
                            skip_group_check=True), cT)
                        if et < 3:
                            prog.wait("scalar", cT, tn)
                    prog.wait("vector", cT, tn)
                    vn = prog.op("vector", lambda e, stcol=stcol: e.tensor_tensor(
                        h2row[0:1, stcol:stcol + TC], h2row[0:1, stcol:stcol + TC],
                        PM[0:1, 0:TC], ALU.add), cV)
                    prog.wait("gpsimd", cV, cV.n)
                    for et in range(4):
                        prog.dma("gpsimd",
                                 oacts[es * ESL + et * 128:es * ESL + et * 128 + 128,
                                       stcol:stcol + TC],
                                 ab(A_OAST + et * 512, 512), dSp)
                chunk_tick[idx] = cT.n
                p0_drive(11)

        p0_drain()
        if p0_state["phase"] == "b_done":
            p0_stats_finalize()
            p0_state["gen"] = p0c_gen()
            p0_state["phase"] = "c"
            p0_drain()

        prog.wait("tensor", cV, cV.n)
        for j in range(TPC // 128):
            tn = prog.op("tensor", lambda e, j=j: e.transpose(
                PM[:, 0:1], h2row[0:1, 128 * j:128 * (j + 1)],
                eyef[0:1, 0:1]), cT)
            prog.wait("vector", cT, tn)
            vn = prog.op("vector", lambda e, j=j: e.tensor_copy(
                cols[:, j:j + 1], PM[:, 0:1]), cV)
            prog.wait("tensor", cV, vn)
        vn = prog.vchain(cV, [
            lambda e: e.tensor_scalar(
                cols[:, 0:16], cols[:, 0:16], 1.0 / D, EPS, ALU.mult, ALU.add),
        ])
        prog.wait("scalar", cV, vn)
        sn = prog.op("scalar", lambda e: e.activation(
            cols[:, 0:16], cols[:, 0:16], AF.Sqrt), cS)
        prog.wait("vector", cS, sn)
        vn = prog.vchain(cV, [
            lambda e: e.reciprocal(cols[:, 0:16], cols[:, 0:16]),
            lambda e: e.tensor_scalar(
                cols[:, 16:32], cols[:, 0:16], 256.0, None, ALU.mult),
        ])
        C_R256 = 16

        prog.wait("sync", cV, cV.n)
        prog.wait("sync", dSp, dSp.n)
        prog.wait("scalar", dSp, dSp.n)

        for bc in range(NBC):
            t0 = bc * TB
            prog.wait("sync", cT, cT.n)
            oact_n = 0
            for k in range(NKD):
                oact_n = prog.dma("sync", ab(B_OACT + k * 512, 512),
                                  oacts[128 * k:128 * (k + 1), t0:t0 + TB], dLdS)
            mm4_ticks = []
            mm4_evacs = []
            PSETS = [P[0:4], [P[4], P[5], P[6], PM]]
            MM4_DUAL = False
            for osl in range(NOSL):
                PS = PSETS[osl % 2] if MM4_DUAL else PSETS[0]
                prog.wait("sync", cV, cV.n)
                xn_n = 0
                for ts in range(NTS):
                    xn_n = prog.dma("sync", af(FB_XN + ts * 512, 512),
                                    xnat[t0 + ts * 128:t0 + ts * 128 + 128,
                                         osl * 512:(osl + 1) * 512], dLdS)
                prog.wait("tensor", dLdS, oact_n)
                if MM4_DUAL:
                    if osl < 2:
                        prog.wait("tensor", cV, cV.n)
                    else:
                        prog.wait("tensor", cV, mm4_evacs[osl - 2])
                else:
                    prog.wait("tensor", cV, cV.n)
                MM4_RING = False
                if MM4_RING:
                    for kd in range(NKD):
                        g = osl * NKD + kd
                        slot = B_WOT + (g % 4) * 512
                        if g >= 4:
                            prog.wait("scalar", cT, mm4_ticks[g - 4])
                        else:
                            prog.wait("scalar", cT, cT.n)
                        wn = prog.dma("scalar", ab(slot, 512),
                                      woTb[128 * kd:128 * (kd + 1),
                                           osl * 512:(osl + 1) * 512], dLdC)
                        prog.wait("tensor", dLdC, wn)
                        for ts in range(NTS):
                            def fn(e, kd=kd, ts=ts, slot=slot, PS=PS):
                                return e.matmul(
                                    PS[ts][:, 0:512],
                                    ab(B_OACT + kd * 512 + ts * 128, 128),
                                    ab(slot, 512),
                                    start=(kd == 0), stop=(kd == NKD - 1))
                            if ts == NTS - 1:
                                tn = prog.op("tensor", fn, cT)
                            else:
                                prog.op("tensor", fn)
                        mm4_ticks.append(tn)
                else:
                    prog.wait("scalar", cT, cT.n)
                    wn = 0
                    for kd in range(NKD):
                        wn = prog.dma("scalar", ab(B_WOT + kd * 512, 512),
                                      woTb[128 * kd:128 * (kd + 1),
                                           osl * 512:(osl + 1) * 512], dLdC)
                    prog.wait("tensor", dLdC, wn)
                    for kd in range(NKD):
                        for ts in range(NTS):
                            def fn(e, kd=kd, ts=ts, PS=PS):
                                return e.matmul(
                                    PS[ts][:, 0:512],
                                    ab(B_OACT + kd * 512 + ts * 128, 128),
                                    ab(B_WOT + kd * 512, 512),
                                    start=(kd == 0), stop=(kd == NKD - 1))
                            if kd == NKD - 1 and ts == NTS - 1:
                                tn = prog.op("tensor", fn, cT)
                            else:
                                prog.op("tensor", fn)
                    mm4_ticks.append(tn)
                prog.wait("vector", cT, tn)
                prog.wait("vector", dLdS, xn_n)
                for ts in range(NTS):
                    rc = cols[:, C_R256 + bc * 4 + ts:C_R256 + bc * 4 + ts + 1]
                    prog.op("vector", lambda e, ts=ts, rc=rc, PS=PS: e.tensor_scalar(
                        ai(ts * 512, 512), PS[ts][:, 0:512], rc, None, ALU.mult))
                    vn = prog.op("vector", lambda e, ts=ts, osl=osl: e.scalar_tensor_tensor(
                        af(FB_Y0F + ts * 2048 + osl * 512, 512), ai(ts * 512, 512),
                        1.0 / 256.0, af(FB_XN + ts * 512, 512),
                        ALU.mult, ALU.add), cV)
                mm4_evacs.append(vn)

            xq_sp_base = dSp.n
            for ts in range(NTS):
                y0t = af(FB_Y0F + ts * 2048, 2048)
                vn = prog.vchain(cV, [
                    lambda e, y0t=y0t: e.reduce_sum(cols[:, 40:41], y0t, axis=AX.X),
                    lambda e: e.tensor_scalar(
                        cols[:, 40:41], cols[:, 40:41], 1.0 / D, None, ALU.mult),
                ])
                prog.wait("scalar", cV, vn)
                sn = prog.op("scalar", lambda e, y0t=y0t: e.activation(
                    af(FB_SCR, 2048), y0t, AF.Square), cS)
                prog.wait("vector", cS, sn)
                vn = prog.vchain(cV, [
                    lambda e: e.reduce_sum(cols[:, 41:42], af(FB_SCR, 2048),
                                           axis=AX.X),
                    lambda e: e.tensor_tensor(
                        cols[:, 42:43], cols[:, 40:41], cols[:, 40:41], ALU.mult),
                    lambda e: e.tensor_scalar(
                        cols[:, 41:42], cols[:, 41:42], 1.0 / D, None, ALU.mult),
                    lambda e: e.tensor_tensor(
                        cols[:, 41:42], cols[:, 41:42], cols[:, 42:43], ALU.subtract),
                    lambda e: e.tensor_scalar(
                        cols[:, 41:42], cols[:, 41:42], 1.0, EPS, ALU.mult, ALU.add),
                ])
                prog.wait("scalar", cV, vn)
                sn = prog.op("scalar", lambda e: e.activation(
                    cols[:, 41:42], cols[:, 41:42], AF.Sqrt), cS)
                prog.wait("vector", cS, sn)
                vn = prog.vchain(cV, [
                    lambda e: e.reciprocal(cols[:, 41:42], cols[:, 41:42]),
                    lambda e, ts=ts, y0t=y0t: e.tensor_scalar(
                        ab(B_Y1 + ts * 2048, 2048), y0t, cols[:, 40:41],
                        cols[:, 41:42], ALU.subtract, ALU.mult),
                ])

                prog.wait("vector", dSp, dSp.n)
                vn = prog.vchain(cV, [
                    lambda e, y0t=y0t: e.scalar_tensor_tensor(
                        af(FB_SCR, 2048), y0t, cols[:, 40:41], nwbb[:, 0:2048],
                        ALU.subtract, ALU.mult),
                    lambda e: e.reduce_max(
                        cols[:, 44:45], af(FB_SCR, 2048), axis=AX.X,
                        apply_absolute_value=True),
                    lambda e: e.tensor_tensor(
                        cols[:, 43:44], cols[:, 41:42], cols[:, 41:42], ALU.mult),
                    lambda e: e.tensor_scalar(
                        cols[:, 43:44], cols[:, 43:44], -EPS, 1.0 + EPS,
                        ALU.mult, ALU.add),
                ])
                prog.wait("scalar", cV, vn)
                sn = prog.op("scalar", lambda e: e.activation(
                    cols[:, 43:44], cols[:, 43:44], AF.Sqrt), cS)
                prog.wait("vector", cS, sn)
                vn = prog.vchain(cV, [
                    lambda e: e.reciprocal(cols[:, 43:44], cols[:, 43:44]),
                    lambda e: e.tensor_tensor(
                        cols[:, 42:43], cols[:, 41:42], cols[:, 43:44], ALU.mult),
                    lambda e: e.tensor_tensor(
                        cols[:, 45:46], cols[:, 42:43], cols[:, 44:45], ALU.mult),
                    lambda e: e.tensor_scalar(
                        cols[:, 45:46], cols[:, 45:46], 1e-5, None, ALU.max),
                    lambda e: e.reciprocal(cols[:, 46:47], cols[:, 45:46]),
                    lambda e: e.tensor_scalar(
                        cols[:, 46:47], cols[:, 46:47], 127.0, None, ALU.mult),
                    lambda e: e.tensor_tensor(
                        cols[:, 46:47], cols[:, 46:47], cols[:, 42:43], ALU.mult),
                    lambda e, ts=ts: e.tensor_scalar(
                        cols[:, 48 + ts:49 + ts], cols[:, 45:46], 1.0 / 127.0,
                        None, ALU.mult),
                    lambda e, ts=ts: e.tensor_tensor(
                        cols[:, 48 + ts:49 + ts], cols[:, 48 + ts:49 + ts],
                        cols[:, C_MW1:C_MW1 + 1], ALU.mult),
                    lambda e: e.tensor_scalar(
                        ai(0, 2048), af(FB_SCR, 2048), cols[:, 46:47], None,
                        ALU.mult),
                    lambda e: e.tensor_copy(ab(B_XQ, 2048), ai(0, 2048)),
                ])
                prog.wait("sync", cV, vn)
                prog.dma("sync", xqs[ts * 128:(ts + 1) * 128, :], ab(B_XQ, 2048),
                         dSp)
            xq_sp = dSp.n
            if DEBUG and bc == 0:
                prog.wait("gpsimd", cV, cV.n)
                prog.dma("gpsimd", dbg_y0, ab(B_Y1, 8192), dOut)
                prog.dma("gpsimd", dbg_y1, ab(B_Y1, 8192), dOut)

            prog.wait("sync", cT, cT.n)
            prog.wait("sync", dSp, xq_sp)
            xqt_n = 0
            for kd in range(NKD):
                xqt_n = prog.dma("sync", ab(B_XQT + kd * 512, 512),
                                 xqs[0:TB, kd * 128:(kd + 1) * 128], dLdS,
                                 transpose=True)

            PU = [P[4], P[5], P[6], PM]
            GR_SLOT = [FB_GR, FB_T1, 9216, 9728]
            relu_tick = [0, 0, 0, 0]
            t1_tick = [0, 0, 0, 0]
            hm_tick = [0, 0, 0, 0]
            spill_cnt = [0, 0, 0, 0]
            gu_ticks = []
            prog.wait("tensor", dLdS, xqt_n)
            prog.wait("tensor", cV, cV.n)
            prog.wait("tensor", cS, cS.n)
            prog.wait("scalar", cT, cT.n)
            prog.wait("sync", cT, cT.n)
            for jp in range(NJP):
                if jp > 0:
                    prog.wait("tensor", cS, relu_tick[NTS - 1])
                    prog.wait("tensor", cV, t1_tick[NTS - 1])
                for kd in range(NKD):
                    g = jp * NKD + kd
                    gs = B_G + (g % 4) * 512
                    us = B_U + (g % 4) * 512
                    if g >= 4:
                        prog.wait("scalar", cT, gu_ticks[g - 4])
                        prog.wait("sync", cT, gu_ticks[g - 4])
                    gn_ = prog.dma("scalar", ab(gs, 512),
                                   wqgu[128 * kd:128 * (kd + 1),
                                        jp * 512:(jp + 1) * 512], dLdC)
                    un_ = prog.dma("sync", ab(us, 512),
                                   wqgu[128 * kd:128 * (kd + 1),
                                        F + jp * 512:F + (jp + 1) * 512], dLdS)
                    prog.wait("tensor", dLdC, gn_)
                    prog.wait("tensor", dLdS, un_)
                    for ts in range(NTS):
                        prog.op("tensor", lambda e, kd=kd, ts=ts, gs=gs: e.matmul(
                            P[ts][:, 0:512],
                            ab(B_XQT + kd * 512 + ts * 128, 128),
                            ab(gs, 512),
                            start=(kd == 0), stop=(kd == NKD - 1)))
                        def fnu(e, kd=kd, ts=ts, us=us):
                            return e.matmul(
                                PU[ts][:, 0:512],
                                ab(B_XQT + kd * 512 + ts * 128, 128),
                                ab(us, 512),
                                start=(kd == 0), stop=(kd == NKD - 1))
                        if ts == NTS - 1:
                            tn = prog.op("tensor", fnu, cT)
                        else:
                            prog.op("tensor", fnu)
                    gu_ticks.append(tn)
                prog.wait("scalar", cT, tn)
                prog.wait("vector", cT, tn)
                for ts in range(NTS):
                    hb = FB_HID + ts * 512
                    gr = GR_SLOT[ts]
                    if jp > 0:
                        prog.wait("scalar", cV, hm_tick[ts])
                    sn = prog.op("scalar", lambda e, ts=ts, gr=gr: e.activation(
                        af(gr, 512), P[ts][:, 0:512], AF.Relu), cS)
                    relu_tick[ts] = sn
                    prog.wait("vector", cS, sn)
                    if jp > 0:
                        prog.wait("vector", dSp, spill_cnt[ts])
                    t1n = prog.op("vector", lambda e, hb=hb, gr=gr, ts=ts: e.tensor_tensor(
                        af(hb, 512), af(gr, 512), PU[ts][:, 0:512], ALU.mult), cV)
                    t1_tick[ts] = t1n
                    hm = prog.op("vector", lambda e, hb=hb, gr=gr: e.tensor_tensor(
                        af(hb, 512), af(hb, 512), af(gr, 512), ALU.mult), cV)
                    hm_tick[ts] = hm
                    prog.wait("vector", cV, hm)
                    vn2 = prog.op("vector", lambda e, hb=hb: e.reduce_max(
                        cols[:, 56:57], af(hb, 512), axis=AX.X,
                        apply_absolute_value=True), cV)
                    prog.wait("vector", cV, vn2)
                    vn2 = prog.op("vector", lambda e, ts=ts: e.tensor_tensor(
                        cols[:, 52 + ts:53 + ts], cols[:, 52 + ts:53 + ts],
                        cols[:, 56:57], ALU.max), cV)
                    prog.wait("vector", cV, vn2)
                    prog.wait("gpsimd", cV, hm)
                    spill_cnt[ts] = prog.dma(
                        "gpsimd", hids[ts * 128:(ts + 1) * 128,
                                       jp * 512:(jp + 1) * 512],
                        af(hb, 512), dSp)
            hid_sp = dSp.n

            vn = prog.vchain(cV, [
                lambda e: e.tensor_tensor(
                    cols[:, 66:70], cols[:, 48:52], cols[:, 48:52], ALU.mult),
                lambda e: e.tensor_tensor(
                    cols[:, 66:70], cols[:, 66:70], cols[:, 48:52], ALU.mult),
                lambda e: e.tensor_tensor(
                    cols[:, 52:56], cols[:, 52:56], cols[:, 66:70], ALU.mult),
                lambda e: e.tensor_scalar(
                    cols[:, 52:56], cols[:, 52:56], 1e-5, None, ALU.max),
                lambda e: e.reciprocal(cols[:, 57:61], cols[:, 52:56]),
                lambda e: e.tensor_scalar(
                    cols[:, 57:61], cols[:, 57:61], 127.0, None, ALU.mult),
                lambda e: e.tensor_tensor(
                    cols[:, 57:61], cols[:, 57:61], cols[:, 66:70], ALU.mult),
                lambda e: e.tensor_scalar(
                    cols[:, 62:66], cols[:, 52:56], 1.0 / 127.0, None, ALU.mult),
                lambda e: e.tensor_scalar(
                    cols[:, 62:66], cols[:, 62:66], cols[:, C_MW2:C_MW2 + 1],
                    None, ALU.mult),
            ])

            prog.wait("sync", dSp, hid_sp)
            prog.wait("sync", cV, vn)
            hq_pieces = [(0, 2048), (2048, 2048), (4096, 1536)]
            for ts in range(NTS):
                for (c0, wd) in hq_pieces:
                    prog.wait("sync", dSp, dSp.n)
                    n = prog.dma("sync", af(FB_SCR, wd),
                                 hids[ts * 128:(ts + 1) * 128, c0:c0 + wd], dLdS)
                    prog.wait("vector", dLdS, n)
                    prog.op("vector", lambda e, ts=ts, wd=wd: e.tensor_scalar(
                        ai(0, wd), af(FB_SCR, wd),
                        cols[:, 57 + ts:58 + ts], None, ALU.mult))
                    vn = prog.op("vector", lambda e, wd=wd: e.tensor_copy(
                        ab(B_HQST, wd), ai(0, wd)), cV)
                    prog.wait("sync", cV, vn)
                    prog.dma("sync", hqs[ts * 128:(ts + 1) * 128, c0:c0 + wd],
                             ab(B_HQST, wd), dSp)
            hq_sp = dSp.n

            prog.wait("sync", cT, cT.n)
            prog.wait("sync", dSp, hq_sp)
            hqt_n = 0
            for kf in range(NKF):
                hqt_n = prog.dma("sync", ab(B_HQT + kf * 512, 512),
                                 hqs[0:TB, kf * 128:(kf + 1) * 128], dLdS,
                                 transpose=True)

            dn_ticks = []
            prog.wait("tensor", dLdS, hqt_n)
            prog.wait("tensor", cV, cV.n)
            prog.wait("tensor", cS, cS.n)
            prog.wait("scalar", cT, cT.n)
            prog.wait("sync", cT, cT.n)
            dn_evacs = []
            DPSETS = [P[0:4], [P[4], P[5], P[6], PM]]
            DN_DUAL = False
            for osl in range(NOSL):
                PSd = DPSETS[osl % 2] if DN_DUAL else DPSETS[0]
                if DN_DUAL and osl >= 2:
                    prog.wait("tensor", cV, dn_evacs[osl - 2])
                elif osl >= 1:
                    prog.wait("tensor", cV, dn_evacs[osl - 1])
                for kf in range(NKF):
                    g = osl * NKF + kf
                    ring = B_DNRING + (g % 8) * 512
                    q = "scalar" if (g % 2 == 0) else "sync"
                    qs = dLdC if q == "scalar" else dLdS
                    if g >= 8:
                        prog.wait(q, cT, dn_ticks[g - 8])
                    n_ = prog.dma(q, ab(ring, 512),
                                  wqdn[128 * kf:128 * (kf + 1),
                                       osl * 512:(osl + 1) * 512], qs)
                    prog.wait("tensor", qs, n_)
                    for ts in range(NTS):
                        def fnd(e, kf=kf, ts=ts, ring=ring, PSd=PSd):
                            return e.matmul(
                                PSd[ts][:, 0:512],
                                ab(B_HQT + kf * 512 + ts * 128, 128),
                                ab(ring, 512),
                                start=(kf == 0), stop=(kf == NKF - 1))
                        if ts == NTS - 1:
                            tn = prog.op("tensor", fnd, cT)
                        else:
                            prog.op("tensor", fnd)
                    dn_ticks.append(tn)
                prog.wait("vector", cT, tn)
                for ts in range(NTS):
                    vn = prog.op("vector", lambda e, ts=ts, osl=osl, PSd=PSd: e.scalar_tensor_tensor(
                        af(FB_Y0F + ts * 2048 + osl * 512, 512), PSd[ts][:, 0:512],
                        cols[:, 62 + ts:63 + ts],
                        ab(B_Y1 + ts * 2048 + osl * 512, 512),
                        ALU.mult, ALU.add), cV)
                dn_evacs.append(vn)

            for ts in range(NTS):
                y2t = af(FB_Y0F + ts * 2048, 2048)
                vn = prog.vchain(cV, [
                    lambda e, y2t=y2t: e.reduce_sum(cols[:, 40:41], y2t, axis=AX.X),
                    lambda e: e.tensor_scalar(
                        cols[:, 40:41], cols[:, 40:41], 1.0 / D, None, ALU.mult),
                ])
                prog.wait("scalar", cV, vn)
                sn = prog.op("scalar", lambda e, y2t=y2t: e.activation(
                    af(FB_SCR, 2048), y2t, AF.Square), cS)
                prog.wait("vector", cS, sn)
                vn = prog.vchain(cV, [
                    lambda e: e.reduce_sum(cols[:, 41:42], af(FB_SCR, 2048),
                                           axis=AX.X),
                    lambda e: e.tensor_tensor(
                        cols[:, 42:43], cols[:, 40:41], cols[:, 40:41], ALU.mult),
                    lambda e: e.tensor_scalar(
                        cols[:, 41:42], cols[:, 41:42], 1.0 / D, None, ALU.mult),
                    lambda e: e.tensor_tensor(
                        cols[:, 41:42], cols[:, 41:42], cols[:, 42:43], ALU.subtract),
                    lambda e: e.tensor_scalar(
                        cols[:, 41:42], cols[:, 41:42], 1.0, EPS, ALU.mult, ALU.add),
                ])
                prog.wait("scalar", cV, vn)
                sn = prog.op("scalar", lambda e: e.activation(
                    cols[:, 41:42], cols[:, 41:42], AF.Sqrt), cS)
                prog.wait("vector", cS, sn)
                vn = prog.vchain(cV, [
                    lambda e: e.reciprocal(cols[:, 41:42], cols[:, 41:42]),
                ])
                prog.wait("vector", dOut, dOut.n)
                vn = prog.op("vector", lambda e, y2t=y2t: e.tensor_scalar(
                    af(FB_OUT, 2048), y2t, cols[:, 40:41], cols[:, 41:42],
                    ALU.subtract, ALU.mult), cV)
                prog.wait("sync", cV, vn)
                prog.dma("sync", out[t0 + ts * 128:t0 + ts * 128 + 128, :],
                         af(FB_OUT, 2048), dOut)
            vn = prog.op("vector", lambda e: e.memset(cols[:, 52:56], 0.0), cV)

        if DEBUG:
            prog.wait("gpsimd", cV, cV.n)
            prog.dma("gpsimd", dbg_y2p, af(FB_Y0F, 8192), dOut)
            prog.dma("gpsimd", dbg_y1l, ab(B_Y1, 8192), dOut)
            prog.dma("gpsimd", dbg_hqt, ab(B_HQT, 22528), dOut)
            prog.dma("gpsimd", dbg_h2, h2row[:], dOut)
            prog.dma("gpsimd", dbg_cols, cols[:], dOut)
            prog.dma("gpsimd", dbg_vals, vals[:], dOut)
            prog.dma("gpsimd", dbg_carry, carry[:], dOut)
            prog.dma("gpsimd", dbg_svc, svc[:], dOut)
            prog.dma("gpsimd", dbg_cols2, cols2[:], dOut)
        prog.wait("gpsimd", dOut, dOut.n)

        with nc.Block() as block:
            @block.sync
            def _(e):
                for fn in prog.q["sync"]:
                    fn(e)

            @block.scalar
            def _(e):
                for fn in prog.q["scalar"]:
                    fn(e)

            @block.vector
            def _(e):
                for fn in prog.q["vector"]:
                    fn(e)

            @block.tensor
            def _(e):
                for fn in prog.q["tensor"]:
                    fn(e)

            @block.gpsimd
            def _(e):
                for fn in prog.q["gpsimd"]:
                    fn(e)

    return nc


_NC_CACHE = None


def kernel(**inputs):
    global _NC_CACHE
    x = np.asarray(inputs["L0_attn_hidden_states"], dtype=np.float32)
    w_i = np.ascontiguousarray(np.asarray(inputs["L0_attn_w_i"], dtype=np.float32))
    w_f = np.ascontiguousarray(np.asarray(inputs["L0_attn_w_f"], dtype=np.float32))
    w_g = np.ascontiguousarray(np.asarray(inputs["L0_attn_w_g"], dtype=np.float32))
    w_oT = np.ascontiguousarray(np.asarray(inputs["L0_attn_w_o"], dtype=np.float32).T)
    s_i = np.asarray(inputs["L0_attn_s_i"], dtype=np.float32)
    s_f = np.asarray(inputs["L0_attn_s_f"], dtype=np.float32)
    s_g = np.asarray(inputs["L0_attn_s_g"], dtype=np.float32)
    s_o = np.asarray(inputs["L0_attn_s_o"], dtype=np.float32)
    g_norm = np.asarray(inputs["L0_attn_g_norm_w"], dtype=np.float32)
    w_guT = np.ascontiguousarray(np.asarray(inputs["L0_ffn_w_gate_up"], dtype=np.float32).T)
    w_dnT = np.ascontiguousarray(np.asarray(inputs["L0_ffn_w_down"], dtype=np.float32).T)
    norm_w = np.asarray(inputs["L0_ffn_norm_weight"], dtype=np.float32)

    if _NC_CACHE is None:
        _NC_CACHE = build_nc()
    nc = _NC_CACHE

    eye = np.eye(128, dtype=np.float32)
    nwb = np.ascontiguousarray(np.broadcast_to(norm_w[None, :], (128, D)))
    svs = {
        "sv_i": np.ascontiguousarray(s_i.reshape(16, 128)),
        "sv_f": np.ascontiguousarray(s_f.reshape(16, 128)),
        "sv_g": np.ascontiguousarray(s_g.reshape(16, 128)),
        "sv_gn": np.ascontiguousarray(g_norm.reshape(16, 128)),
        "sv_o": np.ascontiguousarray(s_o.reshape(16, 128)),
    }

    in_maps = []
    for c in range(8):
        b, s = c // 2, c % 2
        if s == 0:
            xTc = np.ascontiguousarray(
                np.concatenate([x[b, 0:2048].T, x[b, 0:2048].T], axis=1))
            mk = np.zeros((16, 128), np.float32)
        else:
            xTc = np.ascontiguousarray(x[b].T)
            mk = np.ones((16, 128), np.float32)
        in_maps.append(dict(
            xT=xTc,
            xnat=np.ascontiguousarray(x[b, s * 2048:(s + 1) * 2048]),
            w_i=w_i, w_f=w_f, w_g=w_g, w_oT=w_oT, w_guT=w_guT, w_dnT=w_dnT,
            nwb=nwb, eye=eye, sv_mk=np.ascontiguousarray(mk), **svs,
        ))

    res = run_bass_kernel_spmd(nc, in_maps, core_ids=list(range(8)))
    outp = np.empty((B, T, D), dtype=np.float32)
    for c in range(8):
        b, s = c // 2, c % 2
        outp[b, s * 2048:(s + 1) * 2048] = res.results[c]["out"]
    if DEBUG:
        return outp, res.results
    return outp


# version 19 (final)
# speedup vs baseline: 1.3904x; 1.0066x over previous
from contextlib import ExitStack

import numpy as np

import concourse.bass as bass
import concourse.mybir as mybir
from concourse.bass_utils import run_bass_kernel_spmd

F32 = mybir.dt.float32
BF16 = mybir.dt.bfloat16
I32 = mybir.dt.int32
ALU = mybir.AluOpType
AF = mybir.ActivationFunctionType
AX = mybir.AxisListType

B, T, D = 4, 4096, 2048
F = 5632
F2 = 2 * F
TPC = 2048
NKD = D // 128
ESL = 512
NES = D // ESL
TC = 512
NTCA = T // TC
NPRE = 4
TB = 512
NBC = TPC // TB
NTS = TB // 128
NOSL = D // 512
NJP = F // 512
NKF = F // 128
EPS = 1e-5
NG = F2 * D
ND = F * D

ENGINES = ("sync", "scalar", "vector", "tensor", "gpsimd")
DEBUG = False


class Sem:
    def __init__(self, handle):
        self.h = handle
        self.n = 0

    def inc(self, k):
        self.n += k
        return self.n


class Prog:
    def __init__(self):
        self.q = {e: [] for e in ENGINES}

    def on(self, eng, fn):
        self.q[eng].append(fn)

    def dma(self, eng, out, in_, sem, transpose=False):
        def fn(e, out=out, in_=in_, sem=sem, tr=transpose):
            if tr:
                e.dma_start_transpose(out, in_).then_inc(sem.h, 16)
            else:
                e.dma_start(out=out, in_=in_).then_inc(sem.h, 16)
        self.on(eng, fn)
        return sem.inc(16)

    def wait(self, eng, sem, val=None):
        v = sem.n if val is None else val
        if v <= 0:
            return
        self.on(eng, lambda e, h=sem.h, v=v: e.wait_ge(h, v))

    def vchain(self, sem, ops):
        self.wait("vector", sem, sem.n)
        vn = None
        for f in ops:
            vn = self.op("vector", f, sem)
            self.wait("vector", sem, vn)
        return vn

    def op(self, eng, fn, sem=None):
        if sem is None:
            self.on(eng, fn)
            return None
        def wrapped(e, fn=fn, sem=sem):
            fn(e).then_inc(sem.h, 1)
        self.on(eng, wrapped)
        return sem.inc(1)


def build_nc():
    nc = bass.Bass("TRN2", target_bir_lowering=False, debug=False,
                   detect_race_conditions=False)

    xT = nc.dram_tensor("xT", [D, T], F32, kind="ExternalInput").ap()
    xnat = nc.dram_tensor("xnat", [TPC, D], F32, kind="ExternalInput").ap()
    w_i = nc.dram_tensor("w_i", [D, D], F32, kind="ExternalInput").ap()
    w_f = nc.dram_tensor("w_f", [D, D], F32, kind="ExternalInput").ap()
    w_g = nc.dram_tensor("w_g", [D, D], F32, kind="ExternalInput").ap()
    w_oT = nc.dram_tensor("w_oT", [D, D], F32, kind="ExternalInput").ap()
    w_guT = nc.dram_tensor("w_guT", [D, F2], F32, kind="ExternalInput").ap()
    w_dnT = nc.dram_tensor("w_dnT", [F, D], F32, kind="ExternalInput").ap()
    sv_in = {}
    for nm in ("sv_i", "sv_f", "sv_g", "sv_gn", "sv_o", "sv_mk"):
        sv_in[nm] = nc.dram_tensor(nm, [16, 128], F32, kind="ExternalInput").ap()
    nwb_in = nc.dram_tensor("nwb", [128, D], F32, kind="ExternalInput").ap()
    eye_in = nc.dram_tensor("eye", [128, 128], F32, kind="ExternalInput").ap()
    out = nc.dram_tensor("out", [TPC, D], F32, kind="ExternalOutput").ap()

    KD = "ExternalOutput" if DEBUG else "Internal"
    x256s = nc.dram_tensor("x256s", [D, T], BF16, kind=KD).ap()
    oacts = nc.dram_tensor("oacts", [D, TPC], BF16, kind=KD).ap()
    woTb = nc.dram_tensor("woTb", [D, D], BF16, kind="Internal").ap()
    wqgu = nc.dram_tensor("wqgu", [D, F2], BF16, kind=KD).ap()
    wqdn = nc.dram_tensor("wqdn", [F, D], BF16, kind=KD).ap()
    xqs = nc.dram_tensor("xqs", [TB, D], BF16, kind=KD).ap()
    hids = nc.dram_tensor("hids", [TB, F], F32, kind=KD).ap()
    hqs = nc.dram_tensor("hqs", [TB, F], BF16, kind=KD).ap()
    if DEBUG:
        dbg_h2 = nc.dram_tensor("dbg_h2", [1, TPC], F32, kind="ExternalOutput").ap()
        dbg_cols = nc.dram_tensor("dbg_cols", [128, 96], F32, kind="ExternalOutput").ap()
        dbg_vals = nc.dram_tensor("dbg_vals", [1, 16], F32, kind="ExternalOutput").ap()
        dbg_carry = nc.dram_tensor("dbg_carry", [128, 16], F32, kind="ExternalOutput").ap()
        dbg_svc = nc.dram_tensor("dbg_svc", [128, 128], F32, kind="ExternalOutput").ap()
        dbg_cols2 = nc.dram_tensor("dbg_cols2", [128, 160], F32, kind="ExternalOutput").ap()
        dbg_y0 = nc.dram_tensor("dbg_y0", [128, 8192], BF16, kind="ExternalOutput").ap()
        dbg_y1 = nc.dram_tensor("dbg_y1", [128, 8192], BF16, kind="ExternalOutput").ap()
        dbg_y2p = nc.dram_tensor("dbg_y2p", [128, 8192], F32, kind="ExternalOutput").ap()
        dbg_y1l = nc.dram_tensor("dbg_y1l", [128, 8192], BF16, kind="ExternalOutput").ap()
        dbg_hqt = nc.dram_tensor("dbg_hqt", [128, 22528], BF16, kind="ExternalOutput").ap()

    prog = Prog()

    with ExitStack() as es_:
        ctx = es_.enter_context
        AB = ctx(nc.sbuf_tensor("AB", [128, 47104], BF16))
        AFr = ctx(nc.sbuf_tensor("AFr", [128, 18432], F32))
        AI = ctx(nc.sbuf_tensor("AI", [128, 4096], I32))
        svc = ctx(nc.sbuf_tensor("svc", [128, 128], F32))
        carry = ctx(nc.sbuf_tensor("carry", [128, 16], F32))
        nwbb = ctx(nc.sbuf_tensor("nwbb", [128, D], F32))
        eyef = ctx(nc.sbuf_tensor("eyef", [128, 128], F32))
        ones_col = ctx(nc.sbuf_tensor("ones_col", [128, 1], F32))
        ones_row = ctx(nc.sbuf_tensor("ones_row", [1, 128], F32))
        h2row = ctx(nc.sbuf_tensor("h2row", [1, TPC], F32))
        cols = ctx(nc.sbuf_tensor("cols", [128, 96], F32))
        vals = ctx(nc.sbuf_tensor("vals", [1, 16], F32))
        cols2 = ctx(nc.sbuf_tensor("cols2", [128, 160], F32))
        p0bo = ctx(nc.sbuf_tensor("p0bo", [128, 4096], BF16))
        svraw = ctx(nc.sbuf_tensor("svraw", [16, 128], F32))

        P = [ctx(nc.psum_tensor(f"p{i}", [128, 512], F32)) for i in range(7)]
        PM = ctx(nc.psum_tensor("pm", [128, 512], F32))

        dLdS = Sem(ctx(nc.semaphore("dLdS")))
        dLdC = Sem(ctx(nc.semaphore("dLdC")))
        dLdG = Sem(ctx(nc.semaphore("dLdG")))
        dSp = Sem(ctx(nc.semaphore("dSp")))
        dSpG = Sem(ctx(nc.semaphore("dSpG")))
        dOut = Sem(ctx(nc.semaphore("dOut")))
        cV = Sem(ctx(nc.semaphore("cV")))
        cS = Sem(ctx(nc.semaphore("cS")))
        cT = Sem(ctx(nc.semaphore("cT")))

        A_WSL = {"i": 0, "f": 8192, "g": 16384}
        A_XC = 24576
        A_SGB = 40960
        A_OAST = 43008
        B_Y1 = 0
        B_Y0 = 8192
        B_WOT = 16384
        B_XQT = 16384
        B_OACT = 24576
        B_G = 24576
        B_HQT = 24576
        B_U = 34816
        B_XQ = 8192
        B_HQST = 10240
        B_DNRING = 12288
        FA_IB = 0
        FA_HB = 2048
        FA_H2 = 4096
        FA_TMP = 4608
        FA_FB = 5120
        FB_XN = 0
        FB_GR = 2048
        FB_T1 = 2560
        FB_SCR = 3072
        FB_OUT = 5120
        FB_HID = 7168
        FB_Y0F = 10240

        def ab(off, n):
            return AB[:, off:off + n]

        def af(off, n):
            return AFr[:, off:off + n]

        def ai(off, n):
            return AI[:, off:off + n]

        prog.op("vector", lambda e: e.memset(ones_col[:], 1.0))
        prog.op("vector", lambda e: e.memset(ones_row[:], 1.0))
        prog.op("vector", lambda e: e.memset(h2row[:], 0.0))
        prog.op("vector", lambda e: e.memset(carry[:], 0.0))
        vset = prog.op("vector", lambda e: e.memset(cols[:], 0.0), cV)

        n = prog.dma("sync", eyef[:], eye_in, dLdS)
        prog.wait("tensor", dLdS, n)
        prog.dma("sync", nwbb[:], nwb_in, dLdS)

        sv_order = [("sv_i", 0), ("sv_f", 1), ("sv_g", 2), ("sv_gn", 3),
                    ("sv_o", 4), ("sv_mk", 5)]
        for nm, idx in sv_order:
            n = prog.dma("sync", svraw[:], sv_in[nm], dLdS)
            prog.wait("tensor", dLdS, n)
            tn = prog.op("tensor", lambda e: e.transpose(
                PM[:, 0:16], svraw[0:16, :], eyef[0:16, 0:16]), cT)
            prog.wait("vector", cT, tn)
            vn = prog.op("vector", lambda e, idx=idx: e.tensor_copy(
                svc[:, idx * 16:(idx + 1) * 16], PM[:, 0:16]), cV)
            prog.wait("tensor", cV, vn)
            prog.wait("sync", cT, tn)
        SV_SI256, SV_SF256, SV_SG256, SV_GNSO, SV_MASK = 96 - 16, 96, 32, 48, 80
        vn = prog.vchain(cV, [
            lambda e: e.tensor_scalar(
                svc[:, 0:16], svc[:, 0:16], 1.0 / 256.0, None, ALU.mult),
            lambda e: e.tensor_scalar(
                svc[:, 16:32], svc[:, 16:32], 1.0 / 256.0, None, ALU.mult),
            lambda e: e.tensor_scalar(
                svc[:, 32:48], svc[:, 32:48], 1.0 / 256.0, None, ALU.mult),
            lambda e: e.tensor_tensor(
                svc[:, 48:64], svc[:, 48:64], svc[:, 64:80], ALU.mult),
        ])
        SV_SI256, SV_SF256, SV_SG256, SV_GNSO, SV_MASK = 0, 16, 32, 48, 80

        for k in range(NKD):
            buf = A_XC + (k % 2) * 2048
            if k >= 2:
                prog.wait("gpsimd", dSpG, 16 * (k - 1))
            prog.dma("gpsimd", ab(buf, 2048), w_oT[128 * k:128 * (k + 1), :], dLdG)
            prog.wait("gpsimd", dLdG)
            prog.dma("gpsimd", woTb[128 * k:128 * (k + 1), :], ab(buf, 2048), dSpG)
        woT_done = dSp.n

        m = 0
        for k in range(NKD):
            for h in range(2):
                rs = slice(128 * k, 128 * (k + 1))
                cs = slice(2048 * h, 2048 * (h + 1))
                fo = FA_IB + (m % 2) * 2048
                io_ = (m % 2) * 2048
                bo = A_WSL["i"] + (m % 2) * 2048
                if m >= 2:
                    prog.wait("sync", dSp, woT_done + 16 * (m - 1))
                n = prog.dma("sync", af(fo, 2048), xT[rs, cs], dLdS)
                prog.wait("vector", dLdS, n)
                prog.op("vector", lambda e, fo=fo, io_=io_: e.tensor_scalar(
                    ai(io_, 2048), af(fo, 2048), 256.0, None, ALU.mult))
                vn = prog.op("vector", lambda e, io_=io_, bo=bo: e.tensor_copy(
                    ab(bo, 2048), ai(io_, 2048)), cV)
                prog.wait("sync", cV, vn)
                prog.dma("sync", x256s[rs, cs], ab(bo, 2048), dSp)
                m += 1
        x256_done = dSp.n

        def wtiles():
            for k in range(NKD):
                for j in range(5):
                    yield (w_guT, wqgu, 128 * k, 2048 * j, 2048, 0)
                yield (w_guT, wqgu, 128 * k, 10240, 1024, 0)
            for k in range(F // 128):
                yield (w_dnT, wqdn, 128 * k, 0, 2048, 1)

        prog.wait("scalar", cV, vset)
        tiles_list = list(wtiles())
        NT_GU = 96
        AF_P0 = 8192

        def p0b_gen():
            sn_hist = []
            ld_hist = []

            def consume(t):
                fo = AF_P0 + (t % 4) * 2048
                wd = tiles_list[t][4]
                sn = prog.op("vector", lambda e, fo=fo, wd=wd, t=t: e.tensor_reduce(
                    cols2[:, t:t + 1], af(fo, wd), AX.X, ALU.add,
                    apply_absolute_value=True), cV)
                sn_hist.append(sn)

            for t, (wt, _, r0, c0, wd, which) in enumerate(tiles_list):
                fo = AF_P0 + (t % 4) * 2048
                if t >= 3:
                    prog.wait("sync", cV, sn_hist[t - 3])
                ld_hist.append(prog.dma("sync", af(fo, wd),
                                        wt[r0:r0 + 128, c0:c0 + wd], dLdS))
                if t >= 1:
                    prog.wait("vector", dLdS, ld_hist[t])
                    consume(t - 1)
                yield
            prog.wait("vector", dLdS, ld_hist[-1])
            consume(len(tiles_list) - 1)
        def p0_stats_finalize():
            vn = prog.vchain(cV, [
                lambda e: e.reduce_sum(cols[:, 32:33], cols2[:, 0:NT_GU], axis=AX.X),
                lambda e: e.reduce_sum(cols[:, 33:34], cols2[:, NT_GU:len(tiles_list)], axis=AX.X),
            ])
            prog.wait("tensor", cV, cV.n)
            tn = prog.op("tensor", lambda e: e.matmul(
                PM[0:1, 0:1], cols[:, 32:33], ones_col[:], start=True, stop=True), cT)
            tn = prog.op("tensor", lambda e: e.matmul(
                PM[0:1, 1:2], cols[:, 33:34], ones_col[:], start=False, stop=True,
                skip_group_check=True), cT)
            prog.wait("vector", cT, tn)
            vn = prog.vchain(cV, [
                lambda e: e.tensor_copy(vals[0:1, 0:2], PM[0:1, 0:2]),
            ])
            prog.wait("tensor", cV, vn)
            tn = prog.op("tensor", lambda e: e.matmul(
                PM[:, 16:18], ones_row[:], vals[0:1, 0:2], start=True, stop=True,
                skip_group_check=True), cT)
            prog.wait("vector", cT, tn)
            vn = prog.vchain(cV, [
                lambda e: e.tensor_copy(cols[:, 70:72], PM[:, 16:18]),
                lambda e: e.tensor_scalar(
                    cols[:, 36:37], cols[:, 70:71], 1.0 / NG, 1e-5, ALU.mult, ALU.max),
                lambda e: e.tensor_scalar(
                    cols[:, 37:38], cols[:, 71:72], 1.0 / ND, 1e-5, ALU.mult, ALU.max),
                lambda e: e.reciprocal(cols[:, 38:40], cols[:, 36:38]),
            ])
            prog.wait("tensor", cV, vn)
        C_MW1, C_MW2, C_SW1, C_SW2 = 36, 37, 38, 39

        def p0c_gen():
            vn_hist = []
            ld2_hist = []
            st_hist = []

            def consume(t):
                fo = AF_P0 + (t % 4) * 2048
                io_ = (t % 2) * 2048
                bo = (t % 2) * 2048
                (wt, dst, r0, c0, wd, which) = tiles_list[t]
                if t >= 2:
                    prog.wait("vector", dSp, st_hist[t - 2])
                swcol = cols[:, C_SW1 + which:C_SW1 + which + 1]
                prog.op("vector", lambda e, fo=fo, io_=io_, wd=wd, sw=swcol: e.tensor_scalar(
                    ai(io_, wd), af(fo, wd), sw, None, ALU.mult))
                vn = prog.op("vector", lambda e, io_=io_, bo=bo, wd=wd: e.tensor_scalar(
                    p0bo[:, bo:bo + wd], ai(io_, wd), 1.0, -1.0,
                    ALU.min, ALU.max), cV)
                vn_hist.append(vn)
                prog.wait("scalar", cV, vn)
                st_hist.append(prog.dma(
                    "scalar", dst[r0:r0 + 128, c0:c0 + wd],
                    p0bo[:, bo:bo + wd], dSp))

            for t, (wt, dst, r0, c0, wd, which) in enumerate(tiles_list):
                fo = AF_P0 + (t % 4) * 2048
                if t >= 3:
                    prog.wait("sync", cV, vn_hist[t - 3])
                ld2_hist.append(prog.dma("sync", af(fo, wd),
                                         wt[r0:r0 + 128, c0:c0 + wd], dLdS))
                if t >= 1:
                    prog.wait("vector", dLdS, ld2_hist[t])
                    consume(t - 1)
                yield
            prog.wait("vector", dLdS, ld2_hist[-1])
            consume(len(tiles_list) - 1)

        prog.wait("gpsimd", dSp, dSp.n)
        prog.wait("gpsimd", dSpG, dSpG.n)
        prog.wait("gpsimd", cV, cV.n)
        p0_state = {"gen": p0b_gen(), "phase": "b"}

        def p0_drive(nsteps):
            for _ in range(nsteps):
                if p0_state["gen"] is None:
                    return
                try:
                    next(p0_state["gen"])
                except StopIteration:
                    if p0_state["phase"] == "b":
                        if not P0_INTERLEAVE_C:
                            p0_state["gen"] = None
                            p0_state["phase"] = "b_done"
                            return
                        p0_stats_finalize()
                        p0_state["gen"] = p0c_gen()
                        p0_state["phase"] = "c"
                    else:
                        p0_state["gen"] = None
                        return

        def p0_drain():
            while p0_state["gen"] is not None:
                p0_drive(1000)

        P0_INTERLEAVE_C = True

        seq = [(es, tc) for es in range(NES) for tc in range(NTCA)]
        chunk_tick = {}
        xc_n = {}
        PI = [P[4], P[5], P[6], PM]

        def emit_xc_load(idx):
            es, tc = seq[idx]
            tcol = tc * TC
            base = A_XC + (idx % 2) * 8192
            n = 0
            for k in range(NKD):
                n = prog.dma("sync", ab(base + k * 512, 512),
                             x256s[128 * k:128 * (k + 1), tcol:tcol + TC], dLdS)
            xc_n[idx] = n

        emit_xc_load(0)
        for es in range(NES):
            prog.wait("gpsimd", cT, cT.n)
            wsl_ready = {}
            for wname, wap in (("f", w_f), ("i", w_i), ("g", w_g)):
                for k in range(NKD):
                    prog.dma("gpsimd", ab(A_WSL[wname] + k * 512, 512),
                             wap[128 * k:128 * (k + 1), es * ESL:(es + 1) * ESL],
                             dLdG)
                wsl_ready[wname] = dLdG.n

            for tc in range(NTCA):
                idx = es * NTCA + tc
                stored = tc >= NPRE
                tcol = tc * TC
                xcb = A_XC + (idx % 2) * 8192
                if idx + 1 < len(seq):
                    if idx >= 1:
                        prog.wait("sync", cT, chunk_tick[idx - 1])
                    emit_xc_load(idx + 1)
                prog.wait("tensor", dLdS, xc_n[idx])
                prog.wait("tensor", cV, cV.n)
                prog.wait("tensor", cS, cS.n)

                def mm_four(woff, banks):
                    last = None
                    for et in range(4):
                        for kd in range(NKD):
                            def fn(e, et=et, kd=kd, woff=woff, banks=banks, xcb=xcb):
                                return e.matmul(
                                    banks[et][:, 0:TC],
                                    ab(woff + kd * 512 + et * 128, 128),
                                    ab(xcb + kd * 512, 512),
                                    start=(kd == 0), stop=(kd == NKD - 1))
                            if kd == NKD - 1:
                                last = prog.op("tensor", fn, cT)
                            else:
                                prog.op("tensor", fn)
                    return last

                prog.wait("tensor", dLdG, wsl_ready["f"])
                tn = mm_four(A_WSL["f"], P)
                prog.wait("scalar", cT, tn)
                sig_last = 0
                for et in range(4):
                    k = es * 4 + et
                    sig_last = prog.op("scalar", lambda e, et=et, k=k: e.activation(
                        af(FA_FB + et * 512, 512), P[et][:, 0:TC], AF.Sigmoid,
                        scale=svc[:, SV_SF256 + k:SV_SF256 + k + 1]), cS)
                prog.wait("tensor", dLdG, wsl_ready["i"])
                tn = mm_four(A_WSL["i"], PI)
                i_tick = tn
                prog.wait("vector", cT, tn)
                prog.wait("vector", cS, sig_last)
                prog.wait("vector", cV, cV.n)
                for et in range(4):
                    k = es * 4 + et
                    prog.op("vector", lambda e, et=et, k=k: e.tensor_scalar(
                        af(FA_IB + et * 512, 512), PI[et][:, 0:TC],
                        svc[:, SV_SI256 + k:SV_SI256 + k + 1], None, ALU.mult))
                    prog.op("vector", lambda e, et=et: e.tensor_tensor(
                        af(FA_TMP, 512), af(FA_FB + et * 512, 512),
                        af(FA_IB + et * 512, 512), ALU.mult))
                    prog.op("vector", lambda e, et=et: e.tensor_tensor(
                        af(FA_IB + et * 512, 512), af(FA_IB + et * 512, 512),
                        af(FA_TMP, 512), ALU.subtract))
                    vn = prog.op("vector", lambda e, et=et, k=k: e.tensor_tensor_scan(
                        af(FA_HB + et * 512, 512), af(FA_FB + et * 512, 512),
                        af(FA_IB + et * 512, 512), carry[:, k:k + 1],
                        ALU.mult, ALU.add), cV)
                    prog.wait("vector", cV, vn)
                    vn = prog.op("vector", lambda e, et=et, k=k: e.tensor_copy(
                        carry[:, k:k + 1], af(FA_HB + et * 512 + 511, 1)), cV)
                    prog.wait("vector", cV, vn)
                if tc == NPRE - 1:
                    vn = prog.op("vector", lambda e, es=es: e.tensor_tensor(
                        carry[:, es * 4:es * 4 + 4], carry[:, es * 4:es * 4 + 4],
                        svc[:, SV_MASK + es * 4:SV_MASK + es * 4 + 4],
                        ALU.mult), cV)
                    prog.wait("vector", cV, vn)
                if stored:
                    prog.wait("tensor", cS, sig_last)
                    prog.wait("tensor", dLdG, wsl_ready["g"])
                    tn = mm_four(A_WSL["g"], P)
                    prog.wait("scalar", cT, tn)
                    for et in range(4):
                        k = es * 4 + et
                        prog.op("scalar", lambda e, et=et, k=k: e.activation(
                            ab(A_SGB + et * 512, 512), P[et][:, 0:TC], AF.Silu,
                            scale=svc[:, SV_SG256 + k:SV_SG256 + k + 1]), cS)
                    prog.wait("vector", cS, cS.n)
                    prog.wait("vector", dSp, dSp.n)
                    prog.wait("vector", dSpG, dSpG.n)
                    stcol = tcol - NPRE * TC
                    for et in range(4):
                        k = es * 4 + et
                        oa_vn = prog.op("vector", lambda e, et=et, k=k: e.scalar_tensor_tensor(
                            ab(A_OAST + et * 512, 512), af(FA_HB + et * 512, 512),
                            svc[:, SV_GNSO + k:SV_GNSO + k + 1],
                            ab(A_SGB + et * 512, 512), ALU.mult, ALU.mult), cV)
                        prog.wait("scalar", cV, oa_vn)
                        sn = prog.op("scalar", lambda e, et=et: e.activation(
                            af(FA_H2, 512), af(FA_HB + et * 512, 512),
                            AF.Square), cS)
                        prog.wait("tensor", cS, sn)
                        if et == 0:
                            prog.wait("tensor", cV, cV.n)
                        tn = prog.op("tensor", lambda e, et=et: e.matmul(
                            PM[0:1, 0:TC], ones_col[:], af(FA_H2, 512),
                            start=(et == 0), stop=(et == 3),
                            skip_group_check=True), cT)
                        if et < 3:
                            prog.wait("scalar", cT, tn)
                    prog.wait("vector", cT, tn)
                    vn = prog.op("vector", lambda e, stcol=stcol: e.tensor_tensor(
                        h2row[0:1, stcol:stcol + TC], h2row[0:1, stcol:stcol + TC],
                        PM[0:1, 0:TC], ALU.add), cV)
                    prog.wait("gpsimd", cV, cV.n)
                    for et in range(4):
                        prog.dma("gpsimd",
                                 oacts[es * ESL + et * 128:es * ESL + et * 128 + 128,
                                       stcol:stcol + TC],
                                 ab(A_OAST + et * 512, 512), dSpG)
                chunk_tick[idx] = cT.n
                p0_drive(0)

        p0_drain()
        if p0_state["phase"] == "b_done":
            p0_stats_finalize()
            p0_state["gen"] = p0c_gen()
            p0_state["phase"] = "c"
            p0_drain()

        prog.wait("tensor", cV, cV.n)
        for j in range(TPC // 128):
            tn = prog.op("tensor", lambda e, j=j: e.transpose(
                PM[:, 0:1], h2row[0:1, 128 * j:128 * (j + 1)],
                eyef[0:1, 0:1]), cT)
            prog.wait("vector", cT, tn)
            vn = prog.op("vector", lambda e, j=j: e.tensor_copy(
                cols[:, j:j + 1], PM[:, 0:1]), cV)
            prog.wait("tensor", cV, vn)
        vn = prog.vchain(cV, [
            lambda e: e.tensor_scalar(
                cols[:, 0:16], cols[:, 0:16], 1.0 / D, EPS, ALU.mult, ALU.add),
        ])
        prog.wait("scalar", cV, vn)
        sn = prog.op("scalar", lambda e: e.activation(
            cols[:, 0:16], cols[:, 0:16], AF.Sqrt), cS)
        prog.wait("vector", cS, sn)
        vn = prog.vchain(cV, [
            lambda e: e.reciprocal(cols[:, 0:16], cols[:, 0:16]),
            lambda e: e.tensor_scalar(
                cols[:, 16:32], cols[:, 0:16], 256.0, None, ALU.mult),
        ])
        C_R256 = 16

        prog.wait("sync", cV, cV.n)
        prog.wait("sync", dSp, dSp.n)
        prog.wait("scalar", dSp, dSp.n)
        prog.wait("sync", dSpG, dSpG.n)
        prog.wait("scalar", dSpG, dSpG.n)

        for bc in range(NBC):
            t0 = bc * TB
            prog.wait("sync", cT, cT.n)
            oact_n = 0
            for k in range(NKD):
                oact_n = prog.dma("sync", ab(B_OACT + k * 512, 512),
                                  oacts[128 * k:128 * (k + 1), t0:t0 + TB], dLdS)
            mm4_ticks = []
            mm4_evacs = []
            PSETS = [P[0:4], [P[4], P[5], P[6], PM]]
            MM4_DUAL = False
            for osl in range(NOSL):
                PS = PSETS[osl % 2] if MM4_DUAL else PSETS[0]
                prog.wait("sync", cV, cV.n)
                xn_n = 0
                for ts in range(NTS):
                    xn_n = prog.dma("sync", af(FB_XN + ts * 512, 512),
                                    xnat[t0 + ts * 128:t0 + ts * 128 + 128,
                                         osl * 512:(osl + 1) * 512], dLdS)
                prog.wait("tensor", dLdS, oact_n)
                if MM4_DUAL:
                    if osl < 2:
                        prog.wait("tensor", cV, cV.n)
                    else:
                        prog.wait("tensor", cV, mm4_evacs[osl - 2])
                else:
                    prog.wait("tensor", cV, cV.n)
                MM4_RING = False
                if MM4_RING:
                    for kd in range(NKD):
                        g = osl * NKD + kd
                        slot = B_WOT + (g % 4) * 512
                        if g >= 4:
                            prog.wait("scalar", cT, mm4_ticks[g - 4])
                        else:
                            prog.wait("scalar", cT, cT.n)
                        wn = prog.dma("scalar", ab(slot, 512),
                                      woTb[128 * kd:128 * (kd + 1),
                                           osl * 512:(osl + 1) * 512], dLdC)
                        prog.wait("tensor", dLdC, wn)
                        for ts in range(NTS):
                            def fn(e, kd=kd, ts=ts, slot=slot, PS=PS):
                                return e.matmul(
                                    PS[ts][:, 0:512],
                                    ab(B_OACT + kd * 512 + ts * 128, 128),
                                    ab(slot, 512),
                                    start=(kd == 0), stop=(kd == NKD - 1))
                            if ts == NTS - 1:
                                tn = prog.op("tensor", fn, cT)
                            else:
                                prog.op("tensor", fn)
                        mm4_ticks.append(tn)
                else:
                    prog.wait("scalar", cT, cT.n)
                    wn = 0
                    for kd in range(NKD):
                        wn = prog.dma("scalar", ab(B_WOT + kd * 512, 512),
                                      woTb[128 * kd:128 * (kd + 1),
                                           osl * 512:(osl + 1) * 512], dLdC)
                    prog.wait("tensor", dLdC, wn)
                    for kd in range(NKD):
                        for ts in range(NTS):
                            def fn(e, kd=kd, ts=ts, PS=PS):
                                return e.matmul(
                                    PS[ts][:, 0:512],
                                    ab(B_OACT + kd * 512 + ts * 128, 128),
                                    ab(B_WOT + kd * 512, 512),
                                    start=(kd == 0), stop=(kd == NKD - 1))
                            if kd == NKD - 1 and ts == NTS - 1:
                                tn = prog.op("tensor", fn, cT)
                            else:
                                prog.op("tensor", fn)
                    mm4_ticks.append(tn)
                prog.wait("vector", cT, tn)
                prog.wait("vector", dLdS, xn_n)
                for ts in range(NTS):
                    rc = cols[:, C_R256 + bc * 4 + ts:C_R256 + bc * 4 + ts + 1]
                    prog.op("vector", lambda e, ts=ts, rc=rc, PS=PS: e.tensor_scalar(
                        ai(ts * 512, 512), PS[ts][:, 0:512], rc, None, ALU.mult))
                    vn = prog.op("vector", lambda e, ts=ts, osl=osl: e.scalar_tensor_tensor(
                        af(FB_Y0F + ts * 2048 + osl * 512, 512), ai(ts * 512, 512),
                        1.0 / 256.0, af(FB_XN + ts * 512, 512),
                        ALU.mult, ALU.add), cV)
                mm4_evacs.append(vn)

            xq_sp_base = dSp.n
            CM, CIN, CMM, CMX = 100, 104, 108, 112
            CR2, CIR, CM1, CS1 = 116, 120, 124, 128
            for ts in range(NTS):
                y0t = af(FB_Y0F + ts * 2048, 2048)
                fn = lambda e, y0t=y0t, ts=ts: e.reduce_sum(
                    cols2[:, CM + ts:CM + ts + 1], y0t, axis=AX.X)
                if ts == NTS - 1:
                    prog.op("vector", fn, cV)
                else:
                    prog.op("vector", fn)
            prog.wait("scalar", cV, cV.n)
            red_t = [0] * NTS
            for ts in range(NTS):
                y0t = af(FB_Y0F + ts * 2048, 2048)
                buf = FB_SCR if ts % 2 == 0 else FB_XN
                if ts >= 2:
                    prog.wait("scalar", cV, red_t[ts - 2])
                sn = prog.op("scalar", lambda e, y0t=y0t, buf=buf: e.activation(
                    af(buf, 2048), y0t, AF.Square), cS)
                prog.wait("vector", cS, sn)
                red_t[ts] = prog.op("vector", lambda e, buf=buf, ts=ts: e.reduce_sum(
                    cols2[:, CIN + ts:CIN + ts + 1], af(buf, 2048), axis=AX.X), cV)
            vn = prog.vchain(cV, [
                lambda e: e.tensor_scalar(
                    cols2[:, CM:CM + 4], cols2[:, CM:CM + 4], 1.0 / D, None,
                    ALU.mult),
                lambda e: e.tensor_scalar(
                    cols2[:, CIN:CIN + 4], cols2[:, CIN:CIN + 4], 1.0 / D, None,
                    ALU.mult),
                lambda e: e.tensor_tensor(
                    cols2[:, CMM:CMM + 4], cols2[:, CM:CM + 4], cols2[:, CM:CM + 4],
                    ALU.mult),
                lambda e: e.tensor_tensor(
                    cols2[:, CIN:CIN + 4], cols2[:, CIN:CIN + 4],
                    cols2[:, CMM:CMM + 4], ALU.subtract),
                lambda e: e.tensor_scalar(
                    cols2[:, CIN:CIN + 4], cols2[:, CIN:CIN + 4], 1.0, EPS,
                    ALU.mult, ALU.add),
            ])
            prog.wait("scalar", cV, vn)
            sn = prog.op("scalar", lambda e: e.activation(
                cols2[:, CIN:CIN + 4], cols2[:, CIN:CIN + 4], AF.Sqrt), cS)
            prog.wait("vector", cS, sn)
            vn = prog.vchain(cV, [
                lambda e: e.reciprocal(cols2[:, CIN:CIN + 4], cols2[:, CIN:CIN + 4]),
            ])
            for ts in range(NTS):
                y0t = af(FB_Y0F + ts * 2048, 2048)
                buf = FB_SCR if ts % 2 == 0 else FB_XN
                prog.op("vector", lambda e, y0t=y0t, ts=ts: e.tensor_scalar(
                    ab(B_Y1 + ts * 2048, 2048), y0t, cols2[:, CM + ts:CM + ts + 1],
                    cols2[:, CIN + ts:CIN + ts + 1], ALU.subtract, ALU.mult))
                prog.op("vector", lambda e, y0t=y0t, buf=buf, ts=ts:
                        e.scalar_tensor_tensor(
                    af(buf, 2048), y0t, cols2[:, CM + ts:CM + ts + 1],
                    nwbb[:, 0:2048], ALU.subtract, ALU.mult))
                prog.op("vector", lambda e, buf=buf, ts=ts: e.reduce_max(
                    cols2[:, CMX + ts:CMX + ts + 1], af(buf, 2048), axis=AX.X,
                    apply_absolute_value=True))
            vn = prog.vchain(cV, [
                lambda e: e.tensor_tensor(
                    cols2[:, CR2:CR2 + 4], cols2[:, CIN:CIN + 4],
                    cols2[:, CIN:CIN + 4], ALU.mult),
                lambda e: e.tensor_scalar(
                    cols2[:, CR2:CR2 + 4], cols2[:, CR2:CR2 + 4], -EPS, 1.0 + EPS,
                    ALU.mult, ALU.add),
            ])
            prog.wait("scalar", cV, vn)
            sn = prog.op("scalar", lambda e: e.activation(
                cols2[:, CR2:CR2 + 4], cols2[:, CR2:CR2 + 4], AF.Sqrt), cS)
            prog.wait("vector", cS, sn)
            vn = prog.vchain(cV, [
                lambda e: e.reciprocal(cols2[:, CR2:CR2 + 4], cols2[:, CR2:CR2 + 4]),
                lambda e: e.tensor_tensor(
                    cols2[:, CIR:CIR + 4], cols2[:, CIN:CIN + 4],
                    cols2[:, CR2:CR2 + 4], ALU.mult),
                lambda e: e.tensor_tensor(
                    cols2[:, CM1:CM1 + 4], cols2[:, CIR:CIR + 4],
                    cols2[:, CMX:CMX + 4], ALU.mult),
                lambda e: e.tensor_scalar(
                    cols2[:, CM1:CM1 + 4], cols2[:, CM1:CM1 + 4], 1e-5, None,
                    ALU.max),
                lambda e: e.reciprocal(cols2[:, CS1:CS1 + 4], cols2[:, CM1:CM1 + 4]),
                lambda e: e.tensor_scalar(
                    cols2[:, CS1:CS1 + 4], cols2[:, CS1:CS1 + 4], 127.0, None,
                    ALU.mult),
                lambda e: e.tensor_tensor(
                    cols2[:, CS1:CS1 + 4], cols2[:, CS1:CS1 + 4],
                    cols2[:, CIR:CIR + 4], ALU.mult),
                lambda e: e.tensor_scalar(
                    cols[:, 48:52], cols2[:, CM1:CM1 + 4], 1.0 / 127.0, None,
                    ALU.mult),
                lambda e: e.tensor_scalar(
                    cols[:, 48:52], cols[:, 48:52], cols[:, C_MW1:C_MW1 + 1], None,
                    ALU.mult),
            ])
            for ts in range(NTS):
                y0t = af(FB_Y0F + ts * 2048, 2048)
                buf = FB_SCR if ts % 2 == 0 else FB_XN
                prog.op("vector", lambda e, y0t=y0t, buf=buf, ts=ts:
                        e.scalar_tensor_tensor(
                    af(buf, 2048), y0t, cols2[:, CM + ts:CM + ts + 1],
                    nwbb[:, 0:2048], ALU.subtract, ALU.mult))
                prog.op("vector", lambda e, buf=buf, ts=ts: e.tensor_scalar(
                    ai(0, 2048), af(buf, 2048), cols2[:, CS1 + ts:CS1 + ts + 1],
                    None, ALU.mult))
                prog.wait("vector", dSp, dSp.n)
                vn = prog.op("vector", lambda e: e.tensor_copy(
                    ab(B_XQ, 2048), ai(0, 2048)), cV)
                prog.wait("sync", cV, vn)
                prog.dma("sync", xqs[ts * 128:(ts + 1) * 128, :], ab(B_XQ, 2048),
                         dSp)
            xq_sp = dSp.n
            if DEBUG and bc == 0:
                prog.wait("gpsimd", cV, cV.n)
                prog.dma("gpsimd", dbg_y0, ab(B_Y1, 8192), dOut)
                prog.dma("gpsimd", dbg_y1, ab(B_Y1, 8192), dOut)

            prog.wait("sync", cT, cT.n)
            prog.wait("sync", dSp, xq_sp)
            xqt_n = 0
            for kd in range(NKD):
                xqt_n = prog.dma("sync", ab(B_XQT + kd * 512, 512),
                                 xqs[0:TB, kd * 128:(kd + 1) * 128], dLdS,
                                 transpose=True)

            PU = [P[4], P[5], P[6], PM]
            GR_SLOT = [FB_GR, FB_T1, 9216, 9728]
            relu_tick = [0, 0, 0, 0]
            t1_tick = [0, 0, 0, 0]
            hm_tick = [0, 0, 0, 0]
            spill_cnt = [0, 0, 0, 0]
            gu_ticks = []
            prog.wait("tensor", dLdS, xqt_n)
            prog.wait("tensor", cV, cV.n)
            prog.wait("tensor", cS, cS.n)
            prog.wait("scalar", cT, cT.n)
            prog.wait("sync", cT, cT.n)
            for jp in range(NJP):
                if jp > 0:
                    prog.wait("tensor", cS, relu_tick[NTS - 1])
                    prog.wait("tensor", cV, t1_tick[NTS - 1])
                for kd in range(NKD):
                    g = jp * NKD + kd
                    gs = B_G + (g % 4) * 512
                    us = B_U + (g % 4) * 512
                    if g >= 4:
                        prog.wait("scalar", cT, gu_ticks[g - 4])
                        prog.wait("sync", cT, gu_ticks[g - 4])
                    gn_ = prog.dma("scalar", ab(gs, 512),
                                   wqgu[128 * kd:128 * (kd + 1),
                                        jp * 512:(jp + 1) * 512], dLdC)
                    un_ = prog.dma("sync", ab(us, 512),
                                   wqgu[128 * kd:128 * (kd + 1),
                                        F + jp * 512:F + (jp + 1) * 512], dLdS)
                    prog.wait("tensor", dLdC, gn_)
                    prog.wait("tensor", dLdS, un_)
                    for ts in range(NTS):
                        prog.op("tensor", lambda e, kd=kd, ts=ts, gs=gs: e.matmul(
                            P[ts][:, 0:512],
                            ab(B_XQT + kd * 512 + ts * 128, 128),
                            ab(gs, 512),
                            start=(kd == 0), stop=(kd == NKD - 1)))
                        def fnu(e, kd=kd, ts=ts, us=us):
                            return e.matmul(
                                PU[ts][:, 0:512],
                                ab(B_XQT + kd * 512 + ts * 128, 128),
                                ab(us, 512),
                                start=(kd == 0), stop=(kd == NKD - 1))
                        if ts == NTS - 1:
                            tn = prog.op("tensor", fnu, cT)
                        else:
                            prog.op("tensor", fnu)
                    gu_ticks.append(tn)
                prog.wait("scalar", cT, tn)
                prog.wait("vector", cT, tn)
                for ts in range(NTS):
                    hb = FB_HID + ts * 512
                    gr = GR_SLOT[ts]
                    if jp > 0:
                        prog.wait("scalar", cV, hm_tick[ts])
                    sn = prog.op("scalar", lambda e, ts=ts, gr=gr: e.activation(
                        af(gr, 512), P[ts][:, 0:512], AF.Relu), cS)
                    relu_tick[ts] = sn
                    prog.wait("vector", cS, sn)
                    if jp > 0:
                        prog.wait("vector", dSpG, spill_cnt[ts])
                    t1n = prog.op("vector", lambda e, hb=hb, gr=gr, ts=ts: e.tensor_tensor(
                        af(hb, 512), af(gr, 512), PU[ts][:, 0:512], ALU.mult), cV)
                    t1_tick[ts] = t1n
                    hm = prog.op("vector", lambda e, hb=hb, gr=gr: e.tensor_tensor(
                        af(hb, 512), af(hb, 512), af(gr, 512), ALU.mult), cV)
                    hm_tick[ts] = hm
                    prog.wait("vector", cV, hm)
                    vn2 = prog.op("vector", lambda e, hb=hb: e.reduce_max(
                        cols[:, 56:57], af(hb, 512), axis=AX.X,
                        apply_absolute_value=True), cV)
                    prog.wait("vector", cV, vn2)
                    vn2 = prog.op("vector", lambda e, ts=ts: e.tensor_tensor(
                        cols[:, 52 + ts:53 + ts], cols[:, 52 + ts:53 + ts],
                        cols[:, 56:57], ALU.max), cV)
                    prog.wait("vector", cV, vn2)
                    prog.wait("gpsimd", cV, hm)
                    spill_cnt[ts] = prog.dma(
                        "gpsimd", hids[ts * 128:(ts + 1) * 128,
                                       jp * 512:(jp + 1) * 512],
                        af(hb, 512), dSpG)
            hid_sp = dSpG.n

            vn = prog.vchain(cV, [
                lambda e: e.tensor_tensor(
                    cols[:, 66:70], cols[:, 48:52], cols[:, 48:52], ALU.mult),
                lambda e: e.tensor_tensor(
                    cols[:, 66:70], cols[:, 66:70], cols[:, 48:52], ALU.mult),
                lambda e: e.tensor_tensor(
                    cols[:, 52:56], cols[:, 52:56], cols[:, 66:70], ALU.mult),
                lambda e: e.tensor_scalar(
                    cols[:, 52:56], cols[:, 52:56], 1e-5, None, ALU.max),
                lambda e: e.reciprocal(cols[:, 57:61], cols[:, 52:56]),
                lambda e: e.tensor_scalar(
                    cols[:, 57:61], cols[:, 57:61], 127.0, None, ALU.mult),
                lambda e: e.tensor_tensor(
                    cols[:, 57:61], cols[:, 57:61], cols[:, 66:70], ALU.mult),
                lambda e: e.tensor_scalar(
                    cols[:, 62:66], cols[:, 52:56], 1.0 / 127.0, None, ALU.mult),
                lambda e: e.tensor_scalar(
                    cols[:, 62:66], cols[:, 62:66], cols[:, C_MW2:C_MW2 + 1],
                    None, ALU.mult),
            ])

            prog.wait("sync", dSpG, hid_sp)
            prog.wait("sync", cV, vn)
            hq_pieces = [(0, 2048), (2048, 2048), (4096, 1536)]
            for ts in range(NTS):
                for (c0, wd) in hq_pieces:
                    prog.wait("sync", dSp, dSp.n)
                    n = prog.dma("sync", af(FB_SCR, wd),
                                 hids[ts * 128:(ts + 1) * 128, c0:c0 + wd], dLdS)
                    prog.wait("vector", dLdS, n)
                    prog.op("vector", lambda e, ts=ts, wd=wd: e.tensor_scalar(
                        ai(0, wd), af(FB_SCR, wd),
                        cols[:, 57 + ts:58 + ts], None, ALU.mult))
                    vn = prog.op("vector", lambda e, wd=wd: e.tensor_copy(
                        ab(B_HQST, wd), ai(0, wd)), cV)
                    prog.wait("sync", cV, vn)
                    prog.dma("sync", hqs[ts * 128:(ts + 1) * 128, c0:c0 + wd],
                             ab(B_HQST, wd), dSp)
            hq_sp = dSp.n

            prog.wait("sync", cT, cT.n)
            prog.wait("sync", dSp, hq_sp)
            hqt_n = 0
            for kf in range(NKF):
                hqt_n = prog.dma("sync", ab(B_HQT + kf * 512, 512),
                                 hqs[0:TB, kf * 128:(kf + 1) * 128], dLdS,
                                 transpose=True)

            dn_ticks = []
            prog.wait("tensor", dLdS, hqt_n)
            prog.wait("tensor", cV, cV.n)
            prog.wait("tensor", cS, cS.n)
            prog.wait("scalar", cT, cT.n)
            prog.wait("sync", cT, cT.n)
            dn_evacs = []
            DPSETS = [P[0:4], [P[4], P[5], P[6], PM]]
            DN_DUAL = False
            for osl in range(NOSL):
                PSd = DPSETS[osl % 2] if DN_DUAL else DPSETS[0]
                if DN_DUAL and osl >= 2:
                    prog.wait("tensor", cV, dn_evacs[osl - 2])
                elif osl >= 1:
                    prog.wait("tensor", cV, dn_evacs[osl - 1])
                for kf in range(NKF):
                    g = osl * NKF + kf
                    ring = B_DNRING + (g % 8) * 512
                    q = "scalar" if (g % 2 == 0) else "sync"
                    qs = dLdC if q == "scalar" else dLdS
                    if g >= 8:
                        prog.wait(q, cT, dn_ticks[g - 8])
                    n_ = prog.dma(q, ab(ring, 512),
                                  wqdn[128 * kf:128 * (kf + 1),
                                       osl * 512:(osl + 1) * 512], qs)
                    prog.wait("tensor", qs, n_)
                    for ts in range(NTS):
                        def fnd(e, kf=kf, ts=ts, ring=ring, PSd=PSd):
                            return e.matmul(
                                PSd[ts][:, 0:512],
                                ab(B_HQT + kf * 512 + ts * 128, 128),
                                ab(ring, 512),
                                start=(kf == 0), stop=(kf == NKF - 1))
                        if ts == NTS - 1:
                            tn = prog.op("tensor", fnd, cT)
                        else:
                            prog.op("tensor", fnd)
                    dn_ticks.append(tn)
                prog.wait("vector", cT, tn)
                for ts in range(NTS):
                    vn = prog.op("vector", lambda e, ts=ts, osl=osl, PSd=PSd: e.scalar_tensor_tensor(
                        af(FB_Y0F + ts * 2048 + osl * 512, 512), PSd[ts][:, 0:512],
                        cols[:, 62 + ts:63 + ts],
                        ab(B_Y1 + ts * 2048 + osl * 512, 512),
                        ALU.mult, ALU.add), cV)
                dn_evacs.append(vn)

            for ts in range(NTS):
                y2t = af(FB_Y0F + ts * 2048, 2048)
                fn = lambda e, y2t=y2t, ts=ts: e.reduce_sum(
                    cols2[:, CM + ts:CM + ts + 1], y2t, axis=AX.X)
                if ts == NTS - 1:
                    prog.op("vector", fn, cV)
                else:
                    prog.op("vector", fn)
            prog.wait("scalar", cV, cV.n)
            red_t = [0] * NTS
            for ts in range(NTS):
                y2t = af(FB_Y0F + ts * 2048, 2048)
                buf = FB_SCR if ts % 2 == 0 else FB_XN
                if ts >= 2:
                    prog.wait("scalar", cV, red_t[ts - 2])
                sn = prog.op("scalar", lambda e, y2t=y2t, buf=buf: e.activation(
                    af(buf, 2048), y2t, AF.Square), cS)
                prog.wait("vector", cS, sn)
                red_t[ts] = prog.op("vector", lambda e, buf=buf, ts=ts: e.reduce_sum(
                    cols2[:, CIN + ts:CIN + ts + 1], af(buf, 2048), axis=AX.X), cV)
            vn = prog.vchain(cV, [
                lambda e: e.tensor_scalar(
                    cols2[:, CM:CM + 4], cols2[:, CM:CM + 4], 1.0 / D, None,
                    ALU.mult),
                lambda e: e.tensor_scalar(
                    cols2[:, CIN:CIN + 4], cols2[:, CIN:CIN + 4], 1.0 / D, None,
                    ALU.mult),
                lambda e: e.tensor_tensor(
                    cols2[:, CMM:CMM + 4], cols2[:, CM:CM + 4], cols2[:, CM:CM + 4],
                    ALU.mult),
                lambda e: e.tensor_tensor(
                    cols2[:, CIN:CIN + 4], cols2[:, CIN:CIN + 4],
                    cols2[:, CMM:CMM + 4], ALU.subtract),
                lambda e: e.tensor_scalar(
                    cols2[:, CIN:CIN + 4], cols2[:, CIN:CIN + 4], 1.0, EPS,
                    ALU.mult, ALU.add),
            ])
            prog.wait("scalar", cV, vn)
            sn = prog.op("scalar", lambda e: e.activation(
                cols2[:, CIN:CIN + 4], cols2[:, CIN:CIN + 4], AF.Sqrt), cS)
            prog.wait("vector", cS, sn)
            vn = prog.vchain(cV, [
                lambda e: e.reciprocal(cols2[:, CIN:CIN + 4], cols2[:, CIN:CIN + 4]),
            ])
            for ts in range(NTS):
                y2t = af(FB_Y0F + ts * 2048, 2048)
                prog.wait("vector", dOut, dOut.n)
                vn = prog.op("vector", lambda e, y2t=y2t, ts=ts: e.tensor_scalar(
                    af(FB_OUT, 2048), y2t, cols2[:, CM + ts:CM + ts + 1],
                    cols2[:, CIN + ts:CIN + ts + 1], ALU.subtract, ALU.mult), cV)
                prog.wait("sync", cV, vn)
                prog.dma("sync", out[t0 + ts * 128:t0 + ts * 128 + 128, :],
                         af(FB_OUT, 2048), dOut)
            vn = prog.op("vector", lambda e: e.memset(cols[:, 52:56], 0.0), cV)

        if DEBUG:
            prog.wait("gpsimd", cV, cV.n)
            prog.dma("gpsimd", dbg_y2p, af(FB_Y0F, 8192), dOut)
            prog.dma("gpsimd", dbg_y1l, ab(B_Y1, 8192), dOut)
            prog.dma("gpsimd", dbg_hqt, ab(B_HQT, 22528), dOut)
            prog.dma("gpsimd", dbg_h2, h2row[:], dOut)
            prog.dma("gpsimd", dbg_cols, cols[:], dOut)
            prog.dma("gpsimd", dbg_vals, vals[:], dOut)
            prog.dma("gpsimd", dbg_carry, carry[:], dOut)
            prog.dma("gpsimd", dbg_svc, svc[:], dOut)
            prog.dma("gpsimd", dbg_cols2, cols2[:], dOut)
        prog.wait("gpsimd", dOut, dOut.n)

        with nc.Block() as block:
            @block.sync
            def _(e):
                for fn in prog.q["sync"]:
                    fn(e)

            @block.scalar
            def _(e):
                for fn in prog.q["scalar"]:
                    fn(e)

            @block.vector
            def _(e):
                for fn in prog.q["vector"]:
                    fn(e)

            @block.tensor
            def _(e):
                for fn in prog.q["tensor"]:
                    fn(e)

            @block.gpsimd
            def _(e):
                for fn in prog.q["gpsimd"]:
                    fn(e)

    return nc


_NC_CACHE = None


def kernel(**inputs):
    global _NC_CACHE
    x = np.asarray(inputs["L0_attn_hidden_states"], dtype=np.float32)
    w_i = np.ascontiguousarray(np.asarray(inputs["L0_attn_w_i"], dtype=np.float32))
    w_f = np.ascontiguousarray(np.asarray(inputs["L0_attn_w_f"], dtype=np.float32))
    w_g = np.ascontiguousarray(np.asarray(inputs["L0_attn_w_g"], dtype=np.float32))
    w_oT = np.ascontiguousarray(np.asarray(inputs["L0_attn_w_o"], dtype=np.float32).T)
    s_i = np.asarray(inputs["L0_attn_s_i"], dtype=np.float32)
    s_f = np.asarray(inputs["L0_attn_s_f"], dtype=np.float32)
    s_g = np.asarray(inputs["L0_attn_s_g"], dtype=np.float32)
    s_o = np.asarray(inputs["L0_attn_s_o"], dtype=np.float32)
    g_norm = np.asarray(inputs["L0_attn_g_norm_w"], dtype=np.float32)
    w_guT = np.ascontiguousarray(np.asarray(inputs["L0_ffn_w_gate_up"], dtype=np.float32).T)
    w_dnT = np.ascontiguousarray(np.asarray(inputs["L0_ffn_w_down"], dtype=np.float32).T)
    norm_w = np.asarray(inputs["L0_ffn_norm_weight"], dtype=np.float32)

    if _NC_CACHE is None:
        _NC_CACHE = build_nc()
    nc = _NC_CACHE

    eye = np.eye(128, dtype=np.float32)
    nwb = np.ascontiguousarray(np.broadcast_to(norm_w[None, :], (128, D)))
    svs = {
        "sv_i": np.ascontiguousarray(s_i.reshape(16, 128)),
        "sv_f": np.ascontiguousarray(s_f.reshape(16, 128)),
        "sv_g": np.ascontiguousarray(s_g.reshape(16, 128)),
        "sv_gn": np.ascontiguousarray(g_norm.reshape(16, 128)),
        "sv_o": np.ascontiguousarray(s_o.reshape(16, 128)),
    }

    in_maps = []
    for c in range(8):
        b, s = c // 2, c % 2
        if s == 0:
            xTc = np.ascontiguousarray(
                np.concatenate([x[b, 0:2048].T, x[b, 0:2048].T], axis=1))
            mk = np.zeros((16, 128), np.float32)
        else:
            xTc = np.ascontiguousarray(x[b].T)
            mk = np.ones((16, 128), np.float32)
        in_maps.append(dict(
            xT=xTc,
            xnat=np.ascontiguousarray(x[b, s * 2048:(s + 1) * 2048]),
            w_i=w_i, w_f=w_f, w_g=w_g, w_oT=w_oT, w_guT=w_guT, w_dnT=w_dnT,
            nwb=nwb, eye=eye, sv_mk=np.ascontiguousarray(mk), **svs,
        ))

    res = run_bass_kernel_spmd(nc, in_maps, core_ids=list(range(8)))
    outp = np.empty((B, T, D), dtype=np.float32)
    for c in range(8):
        b, s = c // 2, c % 2
        outp[b, s * 2048:(s + 1) * 2048] = res.results[c]["out"]
    if DEBUG:
        return outp, res.results
    return outp
